# version 06 (final)
# speedup vs baseline: 1.5920x; 1.0109x over previous
import numpy as np
import ml_dtypes
from contextlib import ExitStack

import concourse.bass as bass
import concourse.mybir as mybir
from concourse.bass import IndirectOffsetOnAxis
from concourse.bass_utils import run_bass_kernel_spmd

F32 = mybir.dt.float32
BF16 = mybir.dt.bfloat16
I32 = mybir.dt.int32
AF = mybir.ActivationFunctionType
ALU = mybir.AluOpType
AXX = mybir.AxisListType.X
BFNP = ml_dtypes.bfloat16

T, D, NT, ND = 4096, 1024, 32, 8
ENGS = ("sync", "scalar", "vector", "tensor", "gpsimd")


class Prog:
    def __init__(self, nc, es):
        self.nc = nc
        self.sem = {}
        for e in ENGS:
            self.sem[e] = es.enter_context(nc.semaphore("S" + e))
        self.NQ = 6
        for e in ("qsync", "qscalar", "qgpsimd"):
            for j in range(self.NQ):
                self.sem[f"{e}{j}"] = es.enter_context(nc.semaphore(f"S{e}{j}"))
        self.rot = {"qsync": 0, "qscalar": 0, "qgpsimd": 0}
        self.sem["coll"] = es.enter_context(nc.semaphore("Scoll"))
        self.cnt = {k: 0 for k in self.sem}
        self.q = {e: [] for e in ENGS}
        self.water = {}
        self.reg = {}
        self.pend = {e: [] for e in ENGS}

    def _wait(self, eng, semkey, val):
        if not val:
            return
        if self.water.get((eng, semkey), 0) >= val:
            return
        self.water[(eng, semkey)] = val
        s = self.sem[semkey]
        self.q[eng].append(lambda E, s=s, v=val: E.wait_ge(s, v))

    def op(self, eng, fn, reads=(), writes=(), war=(), dma=False, inc=True):
        if dma:
            qu = "q" + eng
            semkey = f"{qu}{self.rot[qu]}"
            self.rot[qu] = (self.rot[qu] + 1) % self.NQ
            self._wait(eng, semkey, self.cnt[semkey])
        else:
            semkey = eng
        for r in reads:
            rg = self.reg.get(r)
            if rg and rg[0]:
                self._wait(eng, rg[0][0], rg[0][1])
        for w in tuple(writes) + tuple(war):
            rg = self.reg.get(w)
            if rg:
                if rg[0]:
                    self._wait(eng, rg[0][0], rg[0][1])
                for sk, v in rg[1].items():
                    self._wait(eng, sk, v)
        if inc:
            amt = 16 if dma else 1
            self.cnt[semkey] += amt
            val = self.cnt[semkey]
            s = self.sem[semkey]
            self.q[eng].append(lambda E, f=fn, s=s, a=amt: f(E).then_inc(s, a))
            tok = (semkey, val)
            for r in self.pend[eng] + list(reads):
                rg = self.reg.setdefault(r, [None, {}])
                rg[1][semkey] = val
            self.pend[eng] = []
            for w in writes:
                self.reg[w] = [tok, {}]
            return tok
        else:
            self.q[eng].append(lambda E, f=fn: f(E))
            self.pend[eng] += list(reads)
            assert not writes, "inc=False ops cannot own writes"
            return None

    def barrier(self):
        for e in ENGS:
            for k in list(self.cnt):
                self._wait(e, k, self.cnt[k])

    def finalize(self):
        for k in list(self.cnt):
            self._wait("sync", k, self.cnt[k])
        with self.nc.Block() as block:
            for e in ENGS:
                ops = self.q[e]
                if not ops:
                    continue
                def body(E, ops=ops):
                    for f in ops:
                        f(E)
                getattr(block, e)(body)


class Ctx:
    def __init__(self, enable_partition_id=False):
        self.es = ExitStack()
        self.nc = bass.Bass("TRN2", target_bir_lowering=False,
                            enable_partition_id=enable_partition_id)
        self.pr = Prog(self.nc, self.es)
        self.t = {}
        self.pfx = ""
        self.cur_es = self.es

    def scope(self):
        from contextlib import contextmanager
        @contextmanager
        def _s():
            es = ExitStack()
            old = self.cur_es
            self.cur_es = es
            try:
                yield
            finally:
                es.close()
                self.cur_es = old
        return _s()

    def sb(self, name, shape, dtype=F32):
        h = self.cur_es.enter_context(self.nc.sbuf_tensor(self.pfx + name, shape, dtype))
        self.t[name] = h
        return h

    def ps(self, name, shape, dtype=F32):
        h = self.cur_es.enter_context(self.nc.psum_tensor(self.pfx + name, shape, dtype))
        self.t[name] = h
        return h

    def din(self, name, shape, dtype=F32):
        h = self.nc.dram_tensor(self.pfx + name, shape, dtype, kind="ExternalInput")
        self.t[name] = h
        return h

    def dout(self, name, shape, dtype=F32):
        h = self.nc.dram_tensor(self.pfx + name, shape, dtype, kind="ExternalOutput")
        self.t[name] = h
        return h

    def dint(self, name, shape, dtype=F32):
        h = self.nc.dram_tensor(self.pfx + name, shape, dtype, kind="Internal")
        self.t[name] = h
        return h

    def close(self):
        self.pr.finalize()
        self.es.close()
        return self.nc


def cast_weight(cx, src, dst, rows, cols, bufa, bufb, keya, keyb, tag):
    pr = cx.pr
    for k in range((rows + 127) // 128):
        r0, r1 = k * 128, min(rows, (k + 1) * 128)
        pr.op("gpsimd", lambda E, r0=r0, r1=r1: E.dma_start(
            out=dst[r0:r1, :], in_=src[r0:r1, :]),
            writes=[f"{tag}dst{k}"], dma=True)


def cast_weight2(cx, src, dst, rows, cols, tag, bufs=None):
    pr = cx.pr
    for k in range((rows + 127) // 128):
        pr.op("gpsimd", lambda E, k=k: E.dma_start(
            out=dst[k * 128:(k + 1) * 128, :], in_=src[k * 128:(k + 1) * 128, :]),
            writes=[f"{tag}done"], dma=True)


def ln_tile(pr, x_ap, xreads, out_ap, outkey, cols, colskey, sq, sqkey):
    pr.op("vector", lambda E: E.reduce_sum(cols[:, 0:1], x_ap, axis=AXX),
          reads=xreads, writes=[colskey])
    pr.op("scalar", lambda E: E.mul(cols[:, 0:1], cols[:, 0:1], 1.0 / D),
          reads=[colskey], writes=[colskey])
    pr.op("vector", lambda E: E.tensor_scalar(out_ap, x_ap, cols[:, 0:1], None,
                                              ALU.subtract),
          reads=list(xreads) + [colskey], writes=[outkey])
    pr.op("scalar", lambda E: E.activation(sq, out_ap, AF.Square),
          reads=[outkey], writes=[sqkey])
    pr.op("vector", lambda E: E.reduce_sum(cols[:, 1:2], sq, axis=AXX),
          reads=[sqkey], writes=[colskey + "v"])
    pr.op("vector", lambda E: E.tensor_scalar(cols[:, 1:2], cols[:, 1:2], 1.0 / D,
                                              1e-5, ALU.mult, ALU.add),
          reads=[colskey + "v"], writes=[colskey + "v"])
    pr.op("scalar", lambda E: E.activation(cols[:, 1:2], cols[:, 1:2], AF.Sqrt),
          reads=[colskey + "v"], writes=[colskey + "v"])
    pr.op("vector", lambda E: E.reciprocal(cols[:, 1:2], cols[:, 1:2]),
          reads=[colskey + "v"], writes=[colskey + "v"])
    pr.op("vector", lambda E: E.tensor_scalar(out_ap, out_ap, cols[:, 1:2], None,
                                              ALU.mult),
          reads=[outkey, colskey + "v"], writes=[outkey])


def build_l0(cx=None, xin_t=None, xout_t=None, maxphase=9):
    standalone = cx is None
    if standalone:
        cx = Ctx()
    pr = cx.pr

    xin = xin_t if xin_t is not None else cx.din("xin", [T, D])
    wbct = cx.din("wbct", [D, 136])
    vec8 = cx.din("vec8", [8, 3])
    d_bc = cx.din("d_bc", [128, 8])
    wr_in = cx.din("wr_in", [D, D])
    wi_in = cx.din("wi_in", [D, D])
    frc = cx.din("frc", [D, D], BF16)
    fic = cx.din("fic", [D, D], BF16)
    nfic = cx.din("nfic", [D, D], BF16)
    grc = cx.din("grc", [D, D], BF16)
    ngic = cx.din("ngic", [D, D], BF16)
    bias_bcd = cx.din("bias_bcd", [128, D])
    identb = cx.din("identb", [128, 128], BF16)
    identf = cx.din("identf", [128, 128])
    masknegd = cx.din("masknegd", [128, 128])
    ones1fd = cx.din("ones1fd", [1, 128])
    xout = xout_t if xout_t is not None else cx.dout("xout", [T, D])

    wr_bf = cx.dint("wr_bf", [D, D], BF16)
    wi_bf = cx.dint("wi_bf", [D, D], BF16)
    trt_d = cx.dint("trt_d", [D, D], BF16)
    tit_d = cx.dint("tit_d", [D, D], BF16)
    s_d = cx.dint("s_d", [T, D])
    g_d = cx.dint("g_d", [T, D])
    xnb_d = cx.dint("xnb_d", [T, D], BF16)

    xa = cx.sb("xa", [128, D]); xb_ = cx.sb("xb_", [128, D])
    sq = cx.sb("sq", [128, D])
    cols = cx.sb("cols", [128, 4])
    xnf = cx.sb("xnf", [128, D])
    xnb2 = cx.sb("xnb2", [128, 2 * D], BF16)
    xnbt = cx.sb("xnbt", [128, D], BF16)
    xnT = cx.sb("xnT", [128, ND * T], BF16)
    idb = cx.sb("idb", [128, 128], BF16)
    idf = cx.sb("idf", [128, 128])
    mneg = cx.sb("mneg", [128, 128])
    onesf = cx.sb("onesf", [1, 128])
    bias_sb = cx.sb("bias_sb", [128, D])
    wbctb = cx.sb("wbctb", [128, 8 * 136], BF16)
    v8 = cx.sb("v8", [8, 3])
    negA = cx.sb("negA", [8, 1])
    dbc = cx.sb("dbc", [128, 8])
    dtT = cx.sb("dtT", [8, T])
    aT = cx.sb("aT", [8, T])
    caT = cx.sb("caT", [8, T])
    t8a = caT
    t8b = aT
    ones8 = cx.sb("ones8", [8, 128])
    btb = cx.sb("btb", [64, T], BF16)
    ctb = cx.sb("ctb", [64, T], BF16)
    dtcols = cx.sb("dtcols", [128, 8 * NT])
    cacols = cx.sb("cacols", [128, 8 * NT])
    cab = cx.sb("cab", [128, 128])
    lm = cx.sb("lm", [128, 128])
    cbts = cx.sb("cbts", [128, 128])
    intraT = cx.sb("intraT", [128, 128], BF16)
    xdt = cx.sb("xdt", [128, 128], BF16)
    bnorm = cx.sb("bnorm", [128, 64], BF16)
    bhat = cx.sb("bhat", [128, 64], BF16)
    ecol = cx.sb("ecol", [128, 1])
    ecol2 = cx.sb("ecol2", [128, 1])
    lcol = cx.sb("lcol", [64, 1])
    Sst = cx.sb("Sst", [64, 8 * 128])
    Sbf = cx.sb("Sbf", [64, 8 * 128], BF16)
    yoff = cx.sb("yoff", [128, 128])
    ones_t = cx.sb("ones_t", [128, 128])
    bctmp = cx.sb("bctmp", [128, 128])
    yt1 = cx.sb("yt1", [128, 128])
    stile = cx.sb("stile", [128, D])
    wsA = cx.sb("wsA", [128, D], BF16); wsB = cx.sb("wsB", [128, D], BF16)
    fsA = cx.sb("fsA", [128, 512], BF16); fsB = cx.sb("fsB", [128, 512], BF16)
    tevac = cx.sb("tevac", [128, 512], BF16)
    mbf = cx.sb("mbf", [128, ND * D], BF16)
    gtile = cx.sb("gtile", [128, D])
    sgt = cx.sb("sgt", [128, D])

    ps = [cx.ps(f"ps{i}", [128, 512]) for i in range(6)]
    pstb = cx.ps("pstb", [128, 128], BF16)
    pstf = cx.ps("pstf", [128, 16])

    for name, dst, k in [("identb", idb, "idb"), ("identf", idf, "idf"),
                         ("masknegd", mneg, "mneg"), ("ones1fd", onesf, "onesf"),
                         ("vec8", v8, "v8"), ("d_bc", dbc, "dbc"),
                         ("bias_bcd", bias_sb, "biasbc")]:
        pr.op("sync", lambda E, s=cx.t[name], d=dst: E.dma_start(out=d[:, :], in_=s[:, :]),
              writes=[k], dma=True)
    pr.op("gpsimd", lambda E: E.memset(ones8[:, :], 1.0), writes=["ones8"])
    pr.op("gpsimd", lambda E: E.memset(ones_t[:, :], 1.0), writes=["ones_t"])
    pr.op("gpsimd", lambda E: E.memset(Sst[:, :], 0.0),
          writes=["Sst"] + [f"Sst{h}" for h in range(8)])
    pr.op("gpsimd", lambda E: E.memset(Sbf[:, :], 0.0),
          writes=["Sbf"] + [f"Sbf{h}" for h in range(8)])
    pr.op("scalar", lambda E: E.activation(negA[:, :], v8[:, 1:2], AF.Exp),
          reads=["v8"], writes=["negA"])
    pr.op("vector", lambda E: E.tensor_scalar(negA[:, :], negA[:, :], -1.0, None, ALU.mult),
          reads=["negA"], writes=["negA"])
    for k in range(8):
        pr.op("sync", lambda E, k=k: E.dma_start(out=xa[:, 0:136], in_=wbct[k * 128:(k + 1) * 128, :]),
              writes=["xa"], dma=True)
        pr.op("scalar", lambda E, k=k: E.copy(wbctb[:, k * 136:(k + 1) * 136], xa[:, 0:136]),
              reads=["xa"], writes=["wbctb"])
    cast_weight(cx, wr_in, wr_bf, D, D, xa, wsA, "xa", "wsA", "cwr")
    cast_weight(cx, wi_in, wi_bf, D, D, xb_, wsB, "xb_", "wsB", "cwi")

    if maxphase < 1:
        return cx
    for i in range(NT):
        xt, xk = (xa, "xa") if i % 2 == 0 else (xb_, "xb_")
        pr.op("sync", lambda E, i=i, xt=xt: E.dma_start(out=xt[:, :], in_=xin[i * 128:(i + 1) * 128, :]),
              writes=[xk], dma=True)
        ln_tile(pr, xt[:, :], [xk], xnf[:, :], "xnf", cols, "cols", sq[:, :], "sq")
        half = (i % 2) * D
        hk = f"xnb2h{i % 2}"
        pr.op("scalar", lambda E, half=half: E.copy(xnb2[:, half:half + D], xnf[:, :]),
              reads=["xnf"], writes=[hk])
        pr.op("sync", lambda E, i=i, half=half: E.dma_start(
            out=xnb_d[i * 128:(i + 1) * 128, :], in_=xnb2[:, half:half + D]),
            reads=[hk], writes=[f"xnbd{i}"], dma=True)
        for j in range(ND):
            pr.op("tensor", lambda E, half=half, j=j: E.transpose(
                pstb[:, :], xnb2[:, half + j * 128: half + (j + 1) * 128], idb[:, :]),
                reads=[hk, "idb"], writes=["pstb"])
            pr.op("scalar", lambda E, i=i, j=j: E.copy(
                xnT[:, j * T + i * 128: j * T + (i + 1) * 128], pstb[:, :]),
                reads=["pstb"], writes=[f"xnT{j}"])

    if maxphase < 2:
        return cx
    for (off, w, dstt, dkey) in [(0, 8, dtT, "dtT_raw"), (8, 64, btb, "btb"),
                                 (72, 64, ctb, "ctb")]:
        for c in range(8):
            for k in range(8):
                pr.op("tensor", lambda E, off=off, w=w, c=c, k=k: E.matmul(
                    ps[0][0:w, 0:512],
                    wbctb[:, k * 136 + off: k * 136 + off + w],
                    xnT[:, k * T + c * 512: k * T + (c + 1) * 512],
                    start=(k == 0), stop=(k == 7)),
                    reads=["wbctb", f"xnT{k}"],
                    writes=["ps0"] if k == 7 else [], war=["ps0"] if k == 0 else ())
            pr.op("scalar", lambda E, w=w, c=c, dstt=dstt: E.copy(
                dstt[0:w, c * 512:(c + 1) * 512], ps[0][0:w, 0:512]),
                reads=["ps0"], writes=[dkey])
    if maxphase < 1.4:
        return cx
    pr.op("vector", lambda E: E.tensor_scalar(dtT[:, :], dtT[:, :], v8[0:8, 0:1], None, ALU.add),
          reads=["dtT_raw", "v8"], writes=["dtT"])
    pr.op("scalar", lambda E: E.activation(t8a[:, :], dtT[:, :], AF.Abs),
          reads=["dtT"], writes=["t8a"])
    pr.op("scalar", lambda E: E.activation(t8a[:, :], t8a[:, :], AF.Exp, scale=-1.0),
          reads=["t8a"], writes=["t8a"])
    pr.op("scalar", lambda E: E.activation(t8a[:, :], t8a[:, :], AF.Ln, bias=1.0),
          reads=["t8a"], writes=["t8a"])
    pr.op("vector", lambda E: E.tensor_scalar(t8b[:, :], dtT[:, :], 0.0, None, ALU.max),
          reads=["dtT"], writes=["t8b"])
    pr.op("vector", lambda E: E.tensor_add(dtT[:, :], t8a[:, :], t8b[:, :]),
          reads=["t8a", "t8b"], writes=["dtT"])
    if maxphase < 1.6:
        return cx
    pr.op("vector", lambda E: E.tensor_scalar(aT[:, :], dtT[:, :], negA[0:8, 0:1], None, ALU.mult),
          reads=["dtT", "negA"], writes=["aT"])
    for i in range(NT):
        pr.op("vector", lambda E, i=i: E.tensor_tensor_scan(
            caT[:, i * 128:(i + 1) * 128], ones8[:, :], aT[:, i * 128:(i + 1) * 128],
            0.0, ALU.mult, ALU.add),
            reads=["aT", "ones8"], writes=["caT"])
    if maxphase < 1.8:
        return cx
    for i in range(NT):
        pr.op("tensor", lambda E, i=i: E.transpose(
            pstf[:, 0:8], dtT[:, i * 128:(i + 1) * 128], idf[0:8, 0:8]),
            reads=["dtT", "idf"], writes=["pstfA"])
        pr.op("vector", lambda E, i=i: E.tensor_copy(dtcols[:, i * 8:(i + 1) * 8], pstf[:, 0:8]),
              reads=["pstfA"], writes=["dtcols"])
        pr.op("tensor", lambda E, i=i: E.transpose(
            pstf[:, 0:8], caT[:, i * 128:(i + 1) * 128], idf[0:8, 0:8]),
            reads=["caT", "idf"], writes=["pstfA"])
        pr.op("vector", lambda E, i=i: E.tensor_copy(cacols[:, i * 8:(i + 1) * 8], pstf[:, 0:8]),
              reads=["pstfA"], writes=["cacols"])

    if maxphase < 3:
        return cx
    def product_pass(dst_d, dtag, pairs):
        for c in range(2):
            for jh in range(2):
                for k in range(8):
                    rd0 = [pairs[0][2] + f"dst{k}"] if pairs[0][2] else []
                    rd1 = [pairs[1][2] + f"dst{k}"] if pairs[1][2] else []
                    pr.op("sync", lambda E, k=k, w=pairs[0][0]: E.dma_start(
                        out=wsA[:, :], in_=w[k * 128:(k + 1) * 128, :]),
                        reads=rd0, writes=["wsA"], dma=True)
                    pr.op("sync", lambda E, k=k, w=pairs[1][0]: E.dma_start(
                        out=wsB[:, :], in_=w[k * 128:(k + 1) * 128, :]),
                        reads=rd1, writes=["wsB"], dma=True)
                    pr.op("sync", lambda E, k=k, c=c, f=pairs[0][1]: E.dma_start(
                        out=fsA[:, :], in_=f[k * 128:(k + 1) * 128, c * 512:(c + 1) * 512]),
                        writes=["fsA"], dma=True)
                    pr.op("sync", lambda E, k=k, c=c, f=pairs[1][1]: E.dma_start(
                        out=fsB[:, :], in_=f[k * 128:(k + 1) * 128, c * 512:(c + 1) * 512]),
                        writes=["fsB"], dma=True)
                    for jj in range(4):
                        j = jh * 4 + jj
                        last = (k == 7)
                        pr.op("tensor", lambda E, j=j, jj=jj, k=k: E.matmul(
                            ps[jj][:, :], wsA[:, j * 128:(j + 1) * 128], fsA[:, :],
                            start=(k == 0), stop=False), reads=["wsA", "fsA"],
                            war=[f"ps{jj}"] if k == 0 else ())
                        pr.op("tensor", lambda E, j=j, jj=jj, last=last: E.matmul(
                            ps[jj][:, :], wsB[:, j * 128:(j + 1) * 128], fsB[:, :],
                            start=False, stop=last), reads=["wsB", "fsB"],
                            writes=[f"ps{jj}"] if last else [])
                for jj in range(4):
                    j = jh * 4 + jj
                    pr.op("scalar", lambda E, jj=jj: E.copy(tevac[:, :], ps[jj][:, :]),
                          reads=[f"ps{jj}"], writes=["tevac"])
                    pr.op("sync", lambda E, j=j, c=c: E.dma_start(
                        out=dst_d[j * 128:(j + 1) * 128, c * 512:(c + 1) * 512],
                        in_=tevac[:, :]),
                        reads=["tevac"], writes=[f"{dtag}dst{j}_{c}"], dma=True)

    product_pass(trt_d, "trt", [(wr_bf, frc, "cwr"), (wi_bf, nfic, "cwi")])
    product_pass(tit_d, "tit", [(wi_bf, frc, "cwi"), (wr_bf, fic, "cwr")])
    for c in range(2):
        for jh in range(2):
            for k in range(8):
                pr.op("sync", lambda E, k=k: E.dma_start(
                    out=wsA[:, :], in_=trt_d[k * 128:(k + 1) * 128, :]),
                    reads=[f"trtdst{k}_0", f"trtdst{k}_1"], writes=["wsA"], dma=True)
                pr.op("sync", lambda E, k=k: E.dma_start(
                    out=wsB[:, :], in_=tit_d[k * 128:(k + 1) * 128, :]),
                    reads=[f"titdst{k}_0", f"titdst{k}_1"], writes=["wsB"], dma=True)
                pr.op("sync", lambda E, k=k, c=c: E.dma_start(
                    out=fsA[:, :], in_=grc[k * 128:(k + 1) * 128, c * 512:(c + 1) * 512]),
                    writes=["fsA"], dma=True)
                pr.op("sync", lambda E, k=k, c=c: E.dma_start(
                    out=fsB[:, :], in_=ngic[k * 128:(k + 1) * 128, c * 512:(c + 1) * 512]),
                    writes=["fsB"], dma=True)
                for jj in range(4):
                    j = jh * 4 + jj
                    last = (k == 7)
                    pr.op("tensor", lambda E, j=j, jj=jj, k=k: E.matmul(
                        ps[jj][:, :], wsA[:, j * 128:(j + 1) * 128], fsA[:, :],
                        start=(k == 0), stop=False), reads=["wsA", "fsA"],
                        war=[f"ps{jj}"] if k == 0 else ())
                    pr.op("tensor", lambda E, j=j, jj=jj, last=last: E.matmul(
                        ps[jj][:, :], wsB[:, j * 128:(j + 1) * 128], fsB[:, :],
                        start=False, stop=last), reads=["wsB", "fsB"],
                        writes=[f"ps{jj}"] if last else [])
            for jj in range(4):
                j = jh * 4 + jj
                pr.op("scalar", lambda E, jj=jj, j=j, c=c: E.copy(
                    mbf[:, j * D + c * 512: j * D + (c + 1) * 512], ps[jj][:, :]),
                    reads=[f"ps{jj}"], writes=[f"mbf{j}"])

    if maxphase < 4:
        return cx
    for i in range(NT):
        pr.op("sync", lambda E, i=i: E.dma_start(out=xnbt[:, :], in_=xnb_d[i * 128:(i + 1) * 128, :]),
              reads=[f"xnbd{i}"], writes=["xnbt"], dma=True)
        pr.op("tensor", lambda E, i=i: E.transpose(
            pstb[:, 0:64], btb[:, i * 128:(i + 1) * 128], idb[0:64, 0:64]),
            reads=["btb", "idb"], writes=["pstb"])
        pr.op("scalar", lambda E: E.copy(bnorm[:, :], pstb[:, 0:64]),
              reads=["pstb"], writes=["bnorm"])
        pr.op("tensor", lambda E, i=i: E.matmul(
            ps[5][:, 0:128], btb[:, i * 128:(i + 1) * 128], ctb[:, i * 128:(i + 1) * 128],
            start=True, stop=True), reads=["btb", "ctb"], writes=["ps5"])
        pr.op("vector", lambda E: E.tensor_copy(cbts[:, :], ps[5][:, 0:128]),
              reads=["ps5"], writes=["cbts"])
        for h in range(8):
            cac = cacols[:, i * 8 + h: i * 8 + h + 1]
            dtc = dtcols[:, i * 8 + h: i * 8 + h + 1]
            pr.op("vector", lambda E, cac=cac: E.tensor_scalar(
                bctmp[:, :], ones_t[:, :], cac, None, ALU.mult),
                reads=["ones_t", "cacols"], writes=["bctmp"])
            pr.op("tensor", lambda E: E.transpose(ps[0][:, 0:128], bctmp[:, :], idf[:, :]),
                  reads=["bctmp", "idf"], writes=["ps0"])
            pr.op("scalar", lambda E: E.copy(cab[:, :], ps[0][:, 0:128]),
                  reads=["ps0"], writes=["cab"])
            pr.op("vector", lambda E, cac=cac: E.scalar_tensor_tensor(
                lm[:, :], cab[:, :], cac, mneg[:, :], ALU.subtract, ALU.add),
                reads=["cab", "cacols", "mneg"], writes=["lm"])
            pr.op("scalar", lambda E: E.activation(lm[:, :], lm[:, :], AF.Exp),
                  reads=["lm"], writes=["lm"])
            pr.op("vector", lambda E: E.tensor_mul(intraT[:, :], cbts[:, :], lm[:, :]),
                  reads=["cbts", "lm"], writes=["intraT"])
            pr.op("vector", lambda E, h=h, dtc=dtc: E.tensor_scalar(
                xdt[:, :], xnbt[:, h * 128:(h + 1) * 128], dtc, None, ALU.mult),
                reads=["xnbt", "dtcols"], writes=["xdt"])
            pr.op("tensor", lambda E: E.matmul(ps[1][:, 0:128], intraT[:, :], xdt[:, :],
                                               start=True, stop=True),
                  reads=["intraT", "xdt"], writes=["ps1"])
            pr.op("tensor", lambda E, i=i, h=h: E.matmul(
                ps[2][:, 0:128], ctb[:, i * 128:(i + 1) * 128], Sbf[:, h * 128:(h + 1) * 128],
                start=True, stop=True), reads=["ctb", f"Sbf{h}"], writes=["ps2"])
            pr.op("scalar", lambda E, cac=cac: E.activation(ecol[:, :], cac, AF.Exp),
                  reads=["cacols"], writes=["ecol"])
            pr.op("vector", lambda E: E.tensor_scalar(yoff[:, :], ps[2][:, 0:128],
                                                      ecol[:, :], None, ALU.mult),
                  reads=["ps2", "ecol"], writes=["yoff"])
            pr.op("vector", lambda E, h=h: E.scalar_tensor_tensor(
                yt1[:, :], xnbt[:, h * 128:(h + 1) * 128],
                dbc[:, h:h + 1], ps[1][:, 0:128], ALU.mult, ALU.add),
                reads=["xnbt", "dbc", "ps1"], writes=["yt1"])
            pr.op("vector", lambda E, h=h: E.tensor_add(stile[:, h * 128:(h + 1) * 128],
                                                        yt1[:, :], yoff[:, :]),
                  reads=["yt1", "yoff"], writes=["stile"])
            pr.op("vector", lambda E, cac=cac: E.tensor_sub(ecol2[:, :], cab[:, 127:128], cac),
                  reads=["cacols", "cab"], writes=["ecol2"])
            pr.op("scalar", lambda E: E.activation(ecol2[:, :], ecol2[:, :], AF.Exp),
                  reads=["ecol2"], writes=["ecol2"])
            pr.op("vector", lambda E: E.tensor_scalar(bhat[:, :], bnorm[:, :], ecol2[:, :],
                                                      None, ALU.mult),
                  reads=["bnorm", "ecol2"], writes=["bhat"])
            pr.op("tensor", lambda E: E.matmul(ps[3][0:64, 0:128], bhat[:, :], xdt[:, :],
                                               start=True, stop=True),
                  reads=["bhat", "xdt"], writes=["ps3"])
            pr.op("scalar", lambda E: E.activation(lcol[:, :], cab[0:64, 127:128], AF.Exp),
                  reads=["cab"], writes=["lcol"])
            pr.op("vector", lambda E, h=h: E.scalar_tensor_tensor(
                Sst[:, h * 128:(h + 1) * 128], Sst[:, h * 128:(h + 1) * 128],
                lcol[:, :], ps[3][0:64, 0:128], ALU.mult, ALU.add),
                reads=[f"Sst{h}", "lcol", "ps3"], writes=[f"Sst{h}"])
            pr.op("scalar", lambda E, h=h: E.copy(Sbf[:, h * 128:(h + 1) * 128],
                                                  Sst[:, h * 128:(h + 1) * 128]),
                  reads=[f"Sst{h}"], writes=[f"Sbf{h}"])
        pr.op("sync", lambda E, i=i: E.dma_start(out=s_d[i * 128:(i + 1) * 128, :],
                                                 in_=stile[:, :]),
              reads=["stile"], writes=[f"s_d{i}"], dma=True)

    if maxphase < 5:
        return cx
    for i in range(NT):
        for c in range(2):
            for j in range(8):
                pr.op("tensor", lambda E, i=i, j=j, c=c: E.matmul(
                    ps[c][:, :], xnT[:, j * T + i * 128: j * T + (i + 1) * 128],
                    mbf[:, j * D + c * 512: j * D + (c + 1) * 512],
                    start=(j == 0), stop=(j == 7)),
                    reads=[f"xnT{j}", f"mbf{j}"],
                    writes=[f"ps{c}"] if j == 7 else [], war=[f"ps{c}"] if j == 0 else ())
        for c in range(2):
            pr.op("vector", lambda E, c=c: E.tensor_add(
                gtile[:, c * 512:(c + 1) * 512], ps[c][:, :], bias_sb[:, c * 512:(c + 1) * 512]),
                reads=[f"ps{c}", "biasbc"], writes=["gtile"])
        pr.op("scalar", lambda E: E.activation(gtile[:, :], gtile[:, :], AF.Silu),
              reads=["gtile"], writes=["gtile"])
        pr.op("sync", lambda E, i=i: E.dma_start(out=g_d[i * 128:(i + 1) * 128, :],
                                                 in_=gtile[:, :]),
              reads=["gtile"], writes=[f"g_d{i}"], dma=True)

    if maxphase < 6:
        return cx
    for i in range(NT):
        xt, xk = (xa, "xa") if i % 2 == 0 else (xb_, "xb_")
        pr.op("sync", lambda E, i=i, xt=xt: E.dma_start(out=xt[:, :], in_=xin[i * 128:(i + 1) * 128, :]),
              writes=[xk], dma=True)
        pr.op("scalar", lambda E, i=i: E.dma_start(out=sgt[:, :], in_=s_d[i * 128:(i + 1) * 128, :]),
              reads=[f"s_d{i}"], writes=["sgt"], dma=True)
        pr.op("gpsimd", lambda E, i=i: E.dma_start(out=gtile[:, :], in_=g_d[i * 128:(i + 1) * 128, :]),
              reads=[f"g_d{i}"], writes=["gtile"], dma=True)
        pr.op("vector", lambda E: E.tensor_mul(sgt[:, :], sgt[:, :], gtile[:, :]),
              reads=["sgt", "gtile"], writes=["sgt"])
        pr.op("vector", lambda E, xt=xt: E.tensor_add(sgt[:, :], sgt[:, :], xt[:, :]),
              reads=["sgt", xk], writes=["sgt"])
        pr.op("sync", lambda E, i=i: E.dma_start(out=xout[i * 128:(i + 1) * 128, :], in_=sgt[:, :]),
              reads=["sgt"], writes=[f"xout{i}"], dma=True)

    return cx


def _host_consts_l0(inputs):
    idx = np.arange(D)
    ang = (2 * np.pi / D) * np.outer(idx, idx)
    Fr = np.cos(ang).astype(np.float32)
    Fi = (-np.sin(ang)).astype(np.float32)
    Gr = (np.cos(ang) / D).astype(np.float32)
    Gi = (np.sin(ang) / D).astype(np.float32)
    s_i = np.arange(128)[:, None]
    t_i = np.arange(128)[None, :]
    v8 = np.stack([np.asarray(inputs["L0_attn_dt_bias"]),
                   np.asarray(inputs["L0_attn_A_log"]),
                   np.zeros(8)], axis=1).astype(np.float32)
    return {
        "wbct": np.ascontiguousarray(np.asarray(inputs["L0_attn_W_xproj"]).T).astype(np.float32),
        "vec8": v8,
        "d_bc": np.ascontiguousarray(np.broadcast_to(
            np.asarray(inputs["L0_attn_D"])[None, :], (128, 8))).astype(np.float32),
        "wr_in": np.asarray(inputs["L0_ffn_w_freq_real"], np.float32),
        "wi_in": np.asarray(inputs["L0_ffn_w_freq_imag"], np.float32),
        "frc": Fr.astype(BFNP), "fic": Fi.astype(BFNP), "nfic": (-Fi).astype(BFNP),
        "grc": Gr.astype(BFNP), "ngic": (-Gi).astype(BFNP),
        "bias_bcd": np.ascontiguousarray(np.broadcast_to(
            np.asarray(inputs["L0_ffn_bias"])[None, :], (128, D))).astype(np.float32),
        "identb": np.eye(128, dtype=BFNP),
        "identf": np.eye(128, dtype=np.float32),
        "masknegd": np.where(s_i <= t_i, 0.0, -1e30).astype(np.float32),
        "ones1fd": np.ones((1, 128), np.float32),
    }


def build_all():
    cx = Ctx(enable_partition_id=True)
    xin = cx.nc.dram_tensor("xin", [T, D], F32, kind="ExternalInput")
    x1s = cx.nc.dram_tensor("x1s", [T, D], F32, kind="Internal")
    x2s = cx.nc.dram_tensor("x2s", [T, D], F32, kind="Internal")
    x3s = cx.nc.dram_tensor("x3s", [T, D], F32, kind="Internal")
    xout = cx.nc.dram_tensor("xout", [T, D], F32, kind="ExternalOutput")
    chain = [(build_l0, xin, x1s), (build_l1, x1s, x2s),
             (build_l2, x2s, x3s), (build_l3, x3s, xout)]
    for i, (bld, xi, xo) in enumerate(chain):
        cx.pfx = f"y{i}_"
        with cx.scope():
            bld(cx, xin_t=xi, xout_t=xo)
        cx.pr.barrier()
    cx.pfx = ""
    return cx


def _all_consts(inputs):
    com = {}
    h1 = _host_consts_l1(inputs, np.zeros((128, D), np.float32))
    del h1["x16"]
    for i, hc in enumerate([_host_consts_l0(inputs), h1,
                            _host_consts_l2(inputs), _host_consts_l3(inputs)]):
        for k, v in hc.items():
            com[f"y{i}_{k}"] = v
    return com


def kernel(**inputs):
    inputs = {k: np.asarray(v) for k, v in inputs.items()}
    x = np.asarray(inputs["L0_attn_x"], np.float32)
    cx = build_all()
    nc = cx.close()
    com = _all_consts(inputs)
    maps = []
    for b in range(8):
        m = dict(com)
        m["xin"] = np.ascontiguousarray(x[b])
        maps.append(m)
    res = run_bass_kernel_spmd(nc, maps, core_ids=list(range(8)))
    return np.stack([r["xout"] for r in res.results], axis=0)


def kernel_unfused(**inputs):
    inputs = {k: np.asarray(v) for k, v in inputs.items()}
    x = np.asarray(inputs["L0_attn_x"], np.float32)

    def launch(build, com, xcur, extra=None):
        cx = build()
        nc = cx.close()
        maps = []
        for b in range(8):
            m = dict(com) if not callable(com) else dict(com(b))
            m["xin"] = np.ascontiguousarray(xcur[b])
            maps.append(m)
        res = run_bass_kernel_spmd(nc, maps, core_ids=list(range(8)))
        return np.stack([r["xout"] for r in res.results], axis=0)

    x1 = launch(build_l0, _host_consts_l0(inputs), x)
    x16 = x1[:, :16, :]
    x2 = launch(build_l1,
                lambda b: _host_consts_l1(inputs, np.roll(x16, -b, axis=0).reshape(128, D)),
                x1)
    x3 = launch(build_l2, _host_consts_l2(inputs), x2)
    x4 = launch(build_l3, _host_consts_l3(inputs), x3)
    return x4


def build_l3(cx=None, xin_t=None, xout_t=None):
    standalone = cx is None
    if standalone:
        cx = Ctx()
    pr = cx.pr
    NH, HD, C = 16, 64, 128
    NG, GC, GW = 16, 2, 256

    xin = xin_t if xin_t is not None else cx.din("xin", [T, D])
    wqt = cx.din("wqt", [D, D]); wkt = cx.din("wkt", [D, D])
    wvt = cx.din("wvt", [D, D]); wgt = cx.din("wgt", [D, D])
    wot = cx.din("wot", [D, D])
    w1 = cx.din("w1", [D, 4 * D]); w2 = cx.din("w2", [4 * D, D])
    alogit = cx.din("alogit", [NH, HD])
    bqc_d = cx.din("bqc_d", [128, 8]); bgc_d = cx.din("bgc_d", [128, 8])
    bkc_d = cx.din("bkc_d", [128, 8]); bk_d = cx.din("bk_d", [128, D])
    b1c_d = cx.din("b1c_d", [128, 32])
    bv_d = cx.din("bv_d", [128, D]); bo_d = cx.din("bo_d", [128, D])
    b2_d = cx.din("b2_d", [128, D])
    lng_d = cx.din("lng_d", [128, D]); lnb_d = cx.din("lnb_d", [128, D])
    identb3 = cx.din("identb3", [128, 128], BF16)
    identf3 = cx.din("identf3", [128, 128])
    tril01 = cx.din("tril01", [128, 128])
    relrow = cx.din("relrow", [128, 384])
    relkrow = cx.din("relkrow", [64, 128])
    xout = xout_t if xout_t is not None else cx.dout("xout", [T, D])

    wq_bf = cx.dint("wq_bf", [D, D], BF16); wk_bf = cx.dint("wk_bf", [D, D], BF16)
    wv_bf = cx.dint("wv_bf", [D, D], BF16); wg_bf = cx.dint("wg_bf", [D, D], BF16)
    wo_bf = cx.dint("wo_bf", [D, D], BF16)
    w1_bf = cx.dint("w1_bf", [D, 4 * D], BF16)
    w2_bf = cx.dint("w2_bf", [4 * D, D], BF16)
    s_d = cx.dint("s_d", [T, D]); g_d = cx.dint("g_d", [T, D])
    xn_d = cx.dint("xn_d", [T, D])

    xa = cx.sb("xa", [128, D]); xb_ = cx.sb("xb_", [128, D])
    sq = cx.sb("sq", [128, D]); cols = cx.sb("cols", [128, 4])
    xnf = cx.sb("xnf", [128, D])
    xnb2 = cx.sb("xnb2", [128, 2 * D], BF16)
    xnT = cx.sb("xnT", [128, ND * T], BF16)
    idb = cx.sb("idb", [128, 128], BF16); idf = cx.sb("idf", [128, 128])
    tril = cx.sb("tril", [128, 128])
    bqc = cx.sb("bqc", [128, 8]); bgc = cx.sb("bgc", [128, 8])
    bkc = cx.sb("bkc", [128, 8]); bkb = cx.sb("bkb", [128, D])
    b1c = cx.sb("b1c", [128, 32])
    bvb = cx.sb("bvb", [128, D]); bob = cx.sb("bob", [128, D])
    b2b = cx.sb("b2b", [128, D])
    lngb = cx.sb("lngb", [128, D]); lnbb = cx.sb("lnbb", [128, D])
    alg = cx.sb("alg", [NH, HD])
    lnal2 = cx.sb("lnal2", [128, NH])
    colp = cx.sb("colp", [128, 8])
    dccp = cx.sb("dccp", [128, 8])
    powA = cx.sb("powA", [128, 8 * 64], BF16)
    powB = cx.sb("powB", [128, 8 * 64], BF16)
    powC = cx.sb("powC", [128, 8 * 64], BF16)
    powD = cx.sb("powD", [128, 8 * 64], BF16)
    powq = cx.sb("powq", [128, 8 * 128], BF16)
    powK = cx.sb("powK", [128, NH * 64], BF16)
    ptmp = cx.sb("ptmp", [128, 128])
    qtb = cx.sb("qtb", [128, 8 * GW], BF16)
    ktb = cx.sb("ktb", [128, 8 * GW], BF16)
    gts = cx.sb("gts", [128, 8 * GW], BF16)
    vtb = cx.sb("vtb", [128, GC * D], BF16)
    knb = cx.sb("knb", [128, GC * D], BF16)
    otb = cx.sb("otb", [128, 8 * GW], BF16)
    qsA = cx.sb("qsA", [128, 64], BF16); qsB = cx.sb("qsB", [128, 64], BF16)
    qsC = cx.sb("qsC", [128, 64], BF16)
    ksA = cx.sb("ksA", [128, 64], BF16); ksB = cx.sb("ksB", [128, 64], BF16)
    ksC = cx.sb("ksC", [128, 64], BF16)
    qh = cx.sb("qh", [128, 128], BF16)
    khn = cx.sb("khn", [128, 64], BF16)
    atb = cx.sb("atb", [128, 128], BF16)
    Sst3 = cx.sb("Sst3", [128, 8 * 64])
    Sb3 = cx.sb("Sb3", [128, 8 * 64], BF16)
    w1cb = cx.sb("w1cb", [128, 8 * 128], BF16)
    w1cbB = cx.sb("w1cbB", [128, 8 * 128], BF16)
    w2rb = cx.sb("w2rb", [128, D], BF16)
    wob_s = cx.sb("wob_s", [128, D], BF16)
    htb = cx.sb("htb", [128, 32 * 256], BF16)
    stile = cx.sb("stile", [128, D]); gtile = cx.sb("gtile", [128, D])
    sgt = cx.sb("sgt", [128, D])

    ps = [cx.ps(f"ps{i}", [128, 512]) for i in range(6)]
    pstb = cx.ps("pstb", [128, 128], BF16)
    pstf = cx.ps("pstf", [128, 128])

    for name, dst, k in [("identb3", idb, "idb"), ("identf3", idf, "idf"),
                         ("tril01", tril, "tril"), ("bqc_d", bqc, "bqc"),
                         ("bgc_d", bgc, "bgc"), ("b1c_d", b1c, "b1c"),
                         ("bkc_d", bkc, "bkc"), ("bk_d", bkb, "bkb"),
                         ("bv_d", bvb, "bvb"), ("bo_d", bob, "bob"),
                         ("b2_d", b2b, "b2b"), ("lng_d", lngb, "lngb"),
                         ("lnb_d", lnbb, "lnbb"), ("alogit", alg, "alg")]:
        pr.op("sync", lambda E, s=cx.t[name], d=dst: E.dma_start(out=d[:, :], in_=s[:, :]),
              writes=[k], dma=True)
    for src_, dst_, tag in [(wqt, wq_bf, "cwq"), (wkt, wk_bf, "cwk"),
                            (wvt, wv_bf, "cwv"), (wgt, wg_bf, "cwg"),
                            (wot, wo_bf, "cwo")]:
        cast_weight2(cx, src_, dst_, D, D, tag)
    cast_weight2(cx, w1, w1_bf, D, 4 * D, "cw1")
    cast_weight2(cx, w2, w2_bf, 4 * D, D, "cw2")
    pr.op("gpsimd", lambda E: E.memset(Sst3[:, :], 0.0),
          writes=[f"S3_{p}_{r0}" for p in range(8) for r0 in (0, 64)])
    pr.op("gpsimd", lambda E: E.memset(Sb3[:, :], 0.0),
          writes=[f"Sb3_{p}_{r0}" for p in range(8) for r0 in (0, 64)])
    pr.op("scalar", lambda E: E.activation(alg[:, :], alg[:, :], AF.Sigmoid),
          reads=["alg"], writes=["alg"])
    pr.op("scalar", lambda E: E.activation(alg[:, :], alg[:, :], AF.Ln),
          reads=["alg"], writes=["alg"])
    pr.op("tensor", lambda E: E.transpose(pstf[0:HD, 0:NH], alg[:, :], idf[0:NH, 0:NH]),
          reads=["alg", "idf"], writes=["pstfA"])
    pr.op("vector", lambda E: E.tensor_copy(lnal2[0:64, :], pstf[0:HD, 0:NH]),
          reads=["pstfA"], writes=["lnal2"])
    pr.op("gpsimd", lambda E: E.dma_start(out=lnal2[64:128, :], in_=lnal2[0:64, :]),
          reads=["lnal2"], writes=["lnal2d"], dma=True)
    for p in range(8):
        pr.op("vector", lambda E, p=p: E.tensor_copy(colp[0:64, p:p + 1], lnal2[0:64, 2 * p:2 * p + 1]),
              reads=["lnal2"], writes=["colp"])
        pr.op("vector", lambda E, p=p: E.tensor_copy(colp[64:128, p:p + 1], lnal2[64:128, 2 * p + 1:2 * p + 2]),
              reads=["lnal2d"], writes=["colp"])
    pr.op("vector", lambda E: E.tensor_scalar(dccp[:, :], colp[:, :], float(C), None, ALU.mult),
          reads=["colp"], writes=["dccp"])
    pr.op("scalar", lambda E: E.activation(dccp[:, :], dccp[:, :], AF.Exp),
          reads=["dccp"], writes=["dccp"])
    relrow_sb = cx.sb("relrow_sb", [128, 384])
    relk_sb = cx.sb("relk_sb", [64, 128])
    pr.op("sync", lambda E: E.dma_start(out=relrow_sb[:, :], in_=relrow[:, :]),
          writes=["relrow_sb"], dma=True)
    pr.op("sync", lambda E: E.dma_start(out=relk_sb[:, :], in_=relkrow[:, :]),
          writes=["relk_sb"], dma=True)
    for p in range(8):
        for (dst, off, w, dk) in [(powA, 0, 64, "powA"), (powB, 64, 64, "powB"),
                                  (powC, 128, 64, "powC"), (powD, 192, 64, "powD"),
                                  (powq, 256, 128, "powq")]:
            pr.op("vector", lambda E, p=p, off=off, w=w: E.tensor_scalar(
                ptmp[:, 0:w], relrow_sb[:, off:off + w], colp[:, p:p + 1], None, ALU.mult),
                reads=["relrow_sb", "colp"], writes=["ptmp"])
            pr.op("scalar", lambda E, p=p, dst=dst, w=w, dk=dk: E.activation(
                dst[:, p * w:(p + 1) * w], ptmp[:, 0:w], AF.Exp),
                reads=["ptmp"], writes=[dk])
    for h in range(NH):
        pr.op("vector", lambda E, h=h: E.tensor_scalar(
            ptmp[0:64, 0:128], relk_sb[:, :], lnal2[0:64, h:h + 1], None, ALU.mult),
            reads=["relk_sb", "lnal2"], writes=["ptmp"])
        pr.op("scalar", lambda E: E.activation(atb[0:64, 0:128], ptmp[0:64, 0:128], AF.Exp),
              reads=["ptmp"], writes=["atbK"])
        pr.op("tensor", lambda E: E.transpose(pstb[:, 0:64], atb[0:64, 0:128], idb[0:64, 0:64]),
              reads=["atbK", "idb"], writes=["pstb"])
        pr.op("scalar", lambda E, h=h: E.copy(powK[:, h * 64:(h + 1) * 64], pstb[:, 0:64]),
              reads=["pstb"], writes=["powKx"])
    pr.op("gpsimd", lambda E: E.memset(atb[64:128, 0:64], 0.0), writes=["atbZ"])

    for i in range(NT):
        xt, xk = (xa, "xa") if i % 2 == 0 else (xb_, "xb_")
        pr.op("sync", lambda E, i=i, xt=xt: E.dma_start(out=xt[:, :], in_=xin[i * 128:(i + 1) * 128, :]),
              writes=[xk], dma=True)
        ln_tile(pr, xt[:, :], [xk], xnf[:, :], "xnf", cols, "cols", sq[:, :], "sq")
        pr.op("vector", lambda E: E.tensor_mul(xnf[:, :], xnf[:, :], lngb[:, :]),
              reads=["xnf", "lngb"], writes=["xnf"])
        pr.op("vector", lambda E: E.tensor_add(xnf[:, :], xnf[:, :], lnbb[:, :]),
              reads=["xnf", "lnbb"], writes=["xnf"])
        half = (i % 2) * D
        hk = f"xnb2h{i % 2}"
        pr.op("scalar", lambda E, half=half: E.copy(xnb2[:, half:half + D], xnf[:, :]),
              reads=["xnf"], writes=[hk])
        pr.op("gpsimd", lambda E, i=i: E.dma_start(out=xn_d[i * 128:(i + 1) * 128, :], in_=xnf[:, :]),
              reads=["xnf"], writes=[f"xn_d{i}"], dma=True)
        for j in range(ND):
            pr.op("tensor", lambda E, half=half, j=j: E.transpose(
                pstb[:, :], xnb2[:, half + j * 128: half + (j + 1) * 128], idb[:, :]),
                reads=[hk, "idb"], writes=["pstb"])
            pr.op("scalar", lambda E, i=i, j=j: E.copy(
                xnT[:, j * T + i * 128: j * T + (i + 1) * 128], pstb[:, :]),
                reads=["pstb"], writes=[f"xnT{j}"])

    for g in range(NG):
        t0 = g * GW
        for (wd, dst, dkey, bias_col, act) in [
                (wq_bf, qtb, "qtb", ("bqc", bqc), None),
                (wk_bf, ktb, "ktb", ("bkc", bkc), None),
                (wg_bf, gts, "gts", ("bgc", bgc), AF.Silu)]:
            for j in range(8):
                pp, pk = ("ps0", 0) if j % 2 == 0 else ("ps5", 5)
                for k in range(8):
                    par = (j * 8 + k) % 4
                    wsl = w2rb[:, par * 128:(par + 1) * 128]
                    wkk = f"w2rb{par}"
                    pr.op("sync" if k % 2 == 0 else "scalar",
                          lambda E, wd=wd, k=k, j=j, wsl=wsl: E.dma_start(
                        out=wsl, in_=wd[k * 128:(k + 1) * 128, j * 128:(j + 1) * 128]),
                        reads=[{"qtb": "cwqdone", "ktb": "cwkdone", "gts": "cwgdone"}[dkey]],
                        writes=[wkk], war=["w2rbh0"], dma=True)
                    pr.op("tensor", lambda E, j=j, k=k, t0=t0, wsl=wsl, pk=pk: E.matmul(
                        ps[pk][:, 0:GW], wsl,
                        xnT[:, k * T + t0: k * T + t0 + GW],
                        start=(k == 0), stop=(k == 7)),
                        reads=[wkk, f"xnT{k}"],
                        writes=[pp] if k == 7 else [], war=[pp] if k == 0 else ())
                if act is None:
                    pr.op("vector", lambda E, j=j, dst=dst, bc=bias_col[1], pk=pk: E.tensor_scalar(
                        dst[:, j * GW:(j + 1) * GW], ps[pk][:, 0:GW], bc[:, j:j + 1], None, ALU.add),
                        reads=[pp, bias_col[0]], writes=[dkey])
                else:
                    pr.op("scalar", lambda E, j=j, dst=dst, bc=bias_col[1], act=act, pk=pk: E.activation(
                        dst[:, j * GW:(j + 1) * GW], ps[pk][:, 0:GW], act, bias=bc[:, j:j + 1]),
                        reads=[pp, bias_col[0]], writes=[dkey])
        for (wd, wtag, dst, dkey, bias_bc) in [
                (wv_bf, "cwv", vtb, "vtb", ("bvb", bvb)), (wk_bf, "cwk", knb, "knb", ("bkb", bkb))]:
            for tt in range(GC):
                for c in range(2):
                    for k in range(8):
                        par = k % 2
                        wsl = w2rb[:, par * 512:(par + 1) * 512]
                        wkk = f"w2rbh{par}"
                        pr.op("sync" if k % 2 == 0 else "scalar",
                              lambda E, wd=wd, k=k, c=c, wsl=wsl: E.dma_start(
                            out=wsl, in_=wd[k * 128:(k + 1) * 128, c * 512:(c + 1) * 512]),
                            reads=[wtag + "done"], writes=[wkk],
                            war=(["w2rb0", "w2rb1", "w2rb2", "w2rb3"] if wkk == "w2rbh0" else ()),
                            dma=True)
                        pr.op("tensor", lambda E, k=k, c=c, tt=tt, t0=t0, wsl=wsl: E.matmul(
                            ps[1][:, :], xnT[:, k * T + t0 + tt * 128: k * T + t0 + (tt + 1) * 128],
                            wsl, start=(k == 0), stop=(k == 7)),
                            reads=[wkk, f"xnT{k}"],
                            writes=["ps1"] if k == 7 else [], war=["ps1"] if k == 0 else ())
                    pr.op("vector", lambda E, tt=tt, c=c, dst=dst, bb=bias_bc[1]: E.tensor_add(
                        dst[:, tt * D + c * 512: tt * D + (c + 1) * 512], ps[1][:, :],
                        bb[:, c * 512:(c + 1) * 512]),
                        reads=["ps1", bias_bc[0]], writes=[dkey])
        for cc in range(GC):
            coff = cc * 128
            for p in range(8):
                base = p * GW + coff
                pr.op("vector", lambda E, base=base, p=p: E.tensor_mul(
                    qsA[:, :], qtb[:, base:base + 64], powA[:, p * 64:(p + 1) * 64]),
                    reads=["qtb", "powA"], writes=["qsA"])
                pr.op("vector", lambda E, base=base, p=p: E.tensor_mul(
                    qsB[:, :], qtb[:, base + 64:base + 128], powA[:, p * 64:(p + 1) * 64]),
                    reads=["qtb", "powA"], writes=["qsB"])
                pr.op("vector", lambda E, base=base, p=p: E.tensor_mul(
                    qsC[:, :], qtb[:, base + 64:base + 128], powC[:, p * 64:(p + 1) * 64]),
                    reads=["qtb", "powC"], writes=["qsC"])
                pr.op("vector", lambda E, base=base, p=p: E.tensor_mul(
                    ksA[:, :], ktb[:, base:base + 64], powB[:, p * 64:(p + 1) * 64]),
                    reads=["ktb", "powB"], writes=["ksA"])
                pr.op("vector", lambda E, base=base, p=p: E.tensor_mul(
                    ksB[:, :], ktb[:, base + 64:base + 128], powB[:, p * 64:(p + 1) * 64]),
                    reads=["ktb", "powB"], writes=["ksB"])
                pr.op("vector", lambda E, base=base, p=p: E.tensor_mul(
                    ksC[:, :], ktb[:, base:base + 64], powD[:, p * 64:(p + 1) * 64]),
                    reads=["ktb", "powD"], writes=["ksC"])
                pr.op("vector", lambda E, base=base, p=p: E.tensor_mul(
                    qh[:, :], qtb[:, base:base + 128], powq[:, p * 128:(p + 1) * 128]),
                    reads=["qtb", "powq"], writes=["qh"])
                for par in range(2):
                    h = 2 * p + par
                    r0 = par * 64
                    vsl = vtb[:, cc * D + h * 64: cc * D + (h + 1) * 64]
                    gsl = gts[r0:r0 + 64, p * GW + coff: p * GW + coff + 128]
                    knsl = knb[:, cc * D + h * 64: cc * D + (h + 1) * 64]
                    pr.op("tensor", lambda E, r0=r0: E.matmul(
                        pstf[0:64, 0:64], ksA[r0:r0 + 64, :], qsA[r0:r0 + 64, :],
                        start=True, stop=True, tile_position=(r0, 0)),
                        reads=["ksA", "qsA"], writes=["pstfT"])
                    pr.op("vector", lambda E: E.tensor_mul(atb[0:64, 0:64],
                                                           pstf[0:64, 0:64], tril[0:64, 0:64]),
                          reads=["pstfT", "tril"], writes=["atb00"])
                    pr.op("tensor", lambda E, r0=r0: E.matmul(
                        pstf[0:64, 64:128] if False else pstf[0:64, 0:64],
                        ksC[r0:r0 + 64, :], qsC[r0:r0 + 64, :],
                        start=True, stop=True, tile_position=(r0, 0)),
                        reads=["ksC", "qsC"], writes=["pstfT"], war=["atb00"])
                    pr.op("vector", lambda E: E.tensor_copy(atb[0:64, 64:128], pstf[0:64, 0:64]),
                          reads=["pstfT"], writes=["atb10"])
                    pr.op("tensor", lambda E, r0=r0: E.matmul(
                        pstf[64:128, 0:64], ksB[r0:r0 + 64, :], qsB[r0:r0 + 64, :],
                        start=True, stop=True, tile_position=(r0, 64)),
                        reads=["ksB", "qsB"], writes=["pstfT2"])
                    pr.op("vector", lambda E: E.tensor_mul(atb[64:128, 64:128],
                                                           pstf[64:128, 0:64], tril[64:128, 64:128]),
                          reads=["pstfT2", "tril"], writes=["atb11"])
                    pr.op("tensor", lambda E, vsl=vsl, r0=r0: E.matmul(
                        ps[2][r0:r0 + 64, 0:128], vsl, atb[:, :],
                        start=True, stop=False, tile_position=(0, r0)),
                        reads=["vtb", "atb00", "atb10", "atb11", "atbZ"], war=["ps2"])
                    pr.op("tensor", lambda E, p=p, r0=r0: E.matmul(
                        ps[2][r0:r0 + 64, 0:128], Sb3[r0:r0 + 64, p * 64:(p + 1) * 64],
                        qh[r0:r0 + 64, :], start=False, stop=True, tile_position=(r0, r0)),
                        reads=[f"Sb3_{p}_{r0}", "qh"], writes=["ps2"])
                    pr.op("vector", lambda E, knsl=knsl, h=h: E.tensor_mul(
                        khn[:, :], knsl, powK[:, h * 64:(h + 1) * 64]),
                        reads=["knb", "powKx"], writes=["khn"])
                    pr.op("tensor", lambda E, r0=r0, vsl=vsl: E.matmul(
                        ps[3][r0:r0 + 64, 0:64], khn[:, :], vsl,
                        start=True, stop=True, tile_position=(0, r0)),
                        reads=["khn", "vtb"], writes=["ps3"])
                    pr.op("vector", lambda E, p=p, r0=r0: E.scalar_tensor_tensor(
                        Sst3[r0:r0 + 64, p * 64:(p + 1) * 64],
                        Sst3[r0:r0 + 64, p * 64:(p + 1) * 64],
                        dccp[r0:r0 + 64, p:p + 1], ps[3][r0:r0 + 64, 0:64],
                        ALU.mult, ALU.add),
                        reads=[f"S3_{p}_{r0}", "dccp", "ps3"], writes=[f"S3_{p}_{r0}"])
                    pr.op("scalar", lambda E, p=p, r0=r0: E.copy(
                        Sb3[r0:r0 + 64, p * 64:(p + 1) * 64],
                        Sst3[r0:r0 + 64, p * 64:(p + 1) * 64]),
                        reads=[f"S3_{p}_{r0}"], writes=[f"Sb3_{p}_{r0}"])
                    pr.op("vector", lambda E, gsl=gsl, p=p, r0=r0, coff=coff: E.tensor_mul(
                        otb[r0:r0 + 64, p * GW + coff: p * GW + coff + 128],
                        ps[2][r0:r0 + 64, 0:128], gsl),
                        reads=["ps2", "gts"], writes=["otb"])
        for k in range(8):
            pr.op("sync", lambda E, k=k: E.dma_start(
                out=wob_s[:, :], in_=wo_bf[k * 128:(k + 1) * 128, :]),
                reads=["cwodone"], writes=["wob_s"], dma=True)
            for tt in range(GC):
                for c in range(2):
                    pj = 1 + tt * 2 + c
                    pr.op("tensor", lambda E, k=k, c=c, tt=tt, pj=pj: E.matmul(
                        ps[pj][:, :], otb[:, k * GW + tt * 128: k * GW + (tt + 1) * 128],
                        wob_s[:, c * 512:(c + 1) * 512],
                        start=(k == 0), stop=(k == 7)),
                        reads=["otb", "wob_s"],
                        writes=[f"ps{pj}"] if k == 7 else [], war=[f"ps{pj}"] if k == 0 else ())
        for tt in range(GC):
            i = g * GC + tt
            xt, xk = (xa, "xa") if i % 2 == 0 else (xb_, "xb_")
            pr.op("scalar", lambda E, i=i, xt=xt: E.dma_start(
                out=xt[:, :], in_=xn_d[i * 128:(i + 1) * 128, :]),
                reads=[f"xn_d{i}"], writes=[xk], dma=True)
            for c in range(2):
                pj = 1 + tt * 2 + c
                pr.op("vector", lambda E, c=c, pj=pj: E.tensor_add(
                    stile[:, c * 512:(c + 1) * 512], ps[pj][:, :], bob[:, c * 512:(c + 1) * 512]),
                    reads=[f"ps{pj}", "bob"], writes=["stile"])
            pr.op("vector", lambda E, xt=xt: E.tensor_add(stile[:, :], stile[:, :], xt[:, :]),
                  reads=["stile", xk], writes=["stile"])
            pr.op("sync", lambda E, i=i: E.dma_start(out=s_d[i * 128:(i + 1) * 128, :], in_=stile[:, :]),
                  reads=["stile"], writes=[f"s_d{i}"], dma=True)

    for tc in range(16):
        t0 = tc * 256
        for kk in range(32):
            wb_, wk_ = (w1cb, "w1cb") if kk % 2 == 0 else (w1cbB, "w1cbB")
            pp, pk = ("ps0", 0) if kk % 2 == 0 else ("ps5", 5)
            pr.op("sync", lambda E, kk=kk, wb_=wb_: E.dma_start(
                out=wb_[:, :].rearrange("p (a q) -> p a q", q=128),
                in_=w1_bf[0:D, kk * 128:(kk + 1) * 128].rearrange("(a p) q -> p a q", p=128)),
                reads=["cw1done"], writes=[wk_], dma=True)
            for k in range(8):
                pr.op("tensor", lambda E, kk=kk, k=k, t0=t0, wb_=wb_, pk=pk: E.matmul(
                    ps[pk][:, 0:256], wb_[:, k * 128:(k + 1) * 128],
                    xnT[:, k * T + t0: k * T + t0 + 256],
                    start=(k == 0), stop=(k == 7)),
                    reads=[wk_, f"xnT{k}"],
                    writes=[pp] if k == 7 else [], war=[pp] if k == 0 else ())
            pr.op("scalar", lambda E, kk=kk, pk=pk: E.activation(
                htb[:, kk * 256:(kk + 1) * 256], ps[pk][:, 0:256], AF.Relu,
                bias=b1c[:, kk:kk + 1]),
                reads=[pp, "b1c"], writes=[f"htb{kk}"])
        for tt in range(2):
            for c in range(2):
                for kk in range(32):
                    par = kk % 2
                    wsl = w2rb[:, par * 512:(par + 1) * 512]
                    wkk = f"w2rbh{par}"
                    pr.op("sync" if kk % 2 == 0 else "scalar",
                          lambda E, kk=kk, c=c, wsl=wsl: E.dma_start(
                        out=wsl, in_=w2_bf[kk * 128:(kk + 1) * 128, c * 512:(c + 1) * 512]),
                        reads=["cw2done"], writes=[wkk],
                        war=(["w2rb0", "w2rb1", "w2rb2", "w2rb3"] if wkk == "w2rbh0" else ()),
                        dma=True)
                    pr.op("tensor", lambda E, kk=kk, c=c, tt=tt, wsl=wsl: E.matmul(
                        ps[1][:, :], htb[:, kk * 256 + tt * 128: kk * 256 + (tt + 1) * 128],
                        wsl, start=(kk == 0), stop=(kk == 31)),
                        reads=[wkk, f"htb{kk}"],
                        writes=["ps1"] if kk == 31 else [], war=["ps1"] if kk == 0 else ())
                pr.op("vector", lambda E, c=c: E.tensor_add(
                    gtile[:, c * 512:(c + 1) * 512], ps[1][:, :], b2b[:, c * 512:(c + 1) * 512]),
                    reads=["ps1", "b2b"], writes=["gtile"])
            pr.op("scalar", lambda E: E.activation(gtile[:, :], gtile[:, :], AF.Silu),
                  reads=["gtile"], writes=["gtile"])
            pr.op("sync", lambda E, t0=t0, tt=tt: E.dma_start(
                out=g_d[t0 + tt * 128: t0 + (tt + 1) * 128, :], in_=gtile[:, :]),
                reads=["gtile"], writes=[f"g_d{2 * tc + tt}"], dma=True)

    for i in range(NT):
        xt, xk = (xa, "xa") if i % 2 == 0 else (xb_, "xb_")
        pr.op("sync", lambda E, i=i, xt=xt: E.dma_start(out=xt[:, :], in_=xin[i * 128:(i + 1) * 128, :]),
              writes=[xk], dma=True)
        pr.op("scalar", lambda E, i=i: E.dma_start(out=sgt[:, :], in_=s_d[i * 128:(i + 1) * 128, :]),
              reads=[f"s_d{i}"], writes=["sgt"], dma=True)
        pr.op("gpsimd", lambda E, i=i: E.dma_start(out=gtile[:, :], in_=g_d[i * 128:(i + 1) * 128, :]),
              reads=[f"g_d{i}"], writes=["gtile"], dma=True)
        pr.op("vector", lambda E: E.tensor_mul(sgt[:, :], sgt[:, :], gtile[:, :]),
              reads=["sgt", "gtile"], writes=["sgt"])
        pr.op("vector", lambda E, xt=xt: E.tensor_add(sgt[:, :], sgt[:, :], xt[:, :]),
              reads=["sgt", xk], writes=["sgt"])
        pr.op("sync", lambda E, i=i: E.dma_start(out=xout[i * 128:(i + 1) * 128, :], in_=sgt[:, :]),
              reads=["sgt"], writes=[f"xout{i}"], dma=True)
    return cx


def _host_consts_l3(inputs):
    def bc(v):
        return np.ascontiguousarray(np.broadcast_to(np.asarray(v, np.float32)[None, :], (128, len(np.asarray(v))))).astype(np.float32)
    def colt(v, n):
        return np.ascontiguousarray(np.asarray(v, np.float32).reshape(n, 128).T)
    s_i = np.arange(128)[:, None]; t_i = np.arange(128)[None, :]
    r64 = np.arange(64, dtype=np.float32)
    r128 = np.arange(128, dtype=np.float32)
    relrow = np.concatenate([np.broadcast_to(r64 - 32, (128, 64)),
                             np.broadcast_to(32 - r64, (128, 64)),
                             np.broadcast_to(r64, (128, 64)),
                             np.broadcast_to(64 - r64, (128, 64)),
                             np.broadcast_to(r128 + 1, (128, 128))], axis=1).astype(np.float32)
    return {
        "wqt": np.ascontiguousarray(np.asarray(inputs["L3_attn_WQ"], np.float32).T),
        "wkt": np.ascontiguousarray(np.asarray(inputs["L3_attn_WK"], np.float32).T),
        "wvt": np.ascontiguousarray(np.asarray(inputs["L3_attn_WV"], np.float32).T),
        "wgt": np.ascontiguousarray(np.asarray(inputs["L3_attn_Wg"], np.float32).T),
        "wot": np.ascontiguousarray(np.asarray(inputs["L3_attn_WO"], np.float32).T),
        "w1": np.asarray(inputs["L3_ffn_W1"], np.float32),
        "w2": np.asarray(inputs["L3_ffn_W2"], np.float32),
        "alogit": np.asarray(inputs["L3_attn_alpha_logit"], np.float32),
        "bqc_d": colt(inputs["L3_attn_bQ"], 8),
        "bkc_d": colt(inputs["L3_attn_bK"], 8),
        "bk_d": bc(inputs["L3_attn_bK"]),
        "bgc_d": colt(inputs["L3_attn_bg"], 8),
        "b1c_d": colt(inputs["L3_ffn_b1"], 32),
        "bv_d": bc(inputs["L3_attn_bV"]),
        "bo_d": bc(inputs["L3_attn_bO"]),
        "b2_d": bc(inputs["L3_ffn_b2"]),
        "lng_d": bc(inputs["L3_attn_ln_gamma"]),
        "lnb_d": bc(inputs["L3_attn_ln_beta"]),
        "identb3": np.eye(128, dtype=BFNP),
        "identf3": np.eye(128, dtype=np.float32),
        "tril01": (s_i <= t_i).astype(np.float32),
        "relrow": relrow,
        "relkrow": np.broadcast_to(127.0 - r128, (64, 128)).astype(np.float32).copy(),
    }


def build_l2(cx=None, xin_t=None, xout_t=None):
    standalone = cx is None
    if standalone:
        cx = Ctx()
    pr = cx.pr
    NSt = 16

    xin = xin_t if xin_t is not None else cx.din("xin", [T, D])
    w1w = cx.din("w1w", [D, D]); v1w = cx.din("v1w", [D, D])
    w2w = cx.din("w2w", [D, D]); dtw = cx.din("dtw", [D, D])
    bpw = cx.din("bpw", [D, NSt]); cpw = cx.din("cpw", [D, NSt])
    alog = cx.din("alog", [D, NSt])
    ff1w = cx.din("ff1w", [D, 3 * D]); ff2w = cx.din("ff2w", [3 * D, D])
    qw = cx.din("qw", [D, D])
    convw = cx.din("convw", [128, 24])
    dpc_d = cx.din("dpc_d", [128, 8])
    w1b_d = cx.din("w1b_d", [128, 8]); v1b_d = cx.din("v1b_d", [128, 8])
    dtb_d = cx.din("dtb_d", [128, 8])
    bpb_d = cx.din("bpb_d", [16, 2])
    ff1b_d = cx.din("ff1b_d", [128, 24])
    w2b_d = cx.din("w2b_d", [128, D]); ff2b_d = cx.din("ff2b_d", [128, D])
    qb_d = cx.din("qb_d", [128, D])
    ln1g_d = cx.din("ln1g_d", [128, D]); ln1b_d = cx.din("ln1b_d", [128, D])
    lng_d = cx.din("lng_d", [128, D]); lnb_d = cx.din("lnb_d", [128, D])
    ln2g_d = cx.din("ln2g_d", [128, D]); ln2b_d = cx.din("ln2b_d", [128, D])
    oh_d = cx.din("oh_d", [16, 16 * 128], BF16)
    identb2 = cx.din("identb2", [128, 128], BF16)
    xout = xout_t if xout_t is not None else cx.dout("xout", [T, D])

    w1_bf = cx.dint("w1_bf", [D, D], BF16); v1_bf = cx.dint("v1_bf", [D, D], BF16)
    w2_bf = cx.dint("w2_bf", [D, D], BF16); dt_bf = cx.dint("dt_bf", [D, D], BF16)
    bp_bf = cx.dint("bp_bf", [D, NSt], BF16); cp_bf = cx.dint("cp_bf", [D, NSt], BF16)
    ff1_bf = cx.dint("ff1_bf", [D, 3 * D], BF16)
    ff2_bf = cx.dint("ff2_bf", [3 * D, D], BF16)
    xn_d = cx.dint("xn_d", [T, D])
    xmT_d = cx.dint("xmT_d", [D, T], BF16)
    xcT_d = cx.dint("xcT_d", [D, T], BF16)
    del_d = cx.dint("del_d", [D, T])
    vT_d = cx.dint("vT_d", [D, T], BF16)
    yvT_d = cx.dint("yvT_d", [D, T], BF16)
    x2m_d = cx.dint("x2m_d", [T, D])
    xn2T_d = cx.dint("xn2T_d", [D, T], BF16)
    s_d = cx.dint("s_d", [T, D]); g_d = cx.dint("g_d", [T, D])
    wdT_d = cx.dint("wdT_d", [D, D], BF16)

    xa = cx.sb("xa", [128, D]); xb_ = cx.sb("xb_", [128, D])
    sq = cx.sb("sq", [128, D]); cols = cx.sb("cols", [128, 4])
    xnf = cx.sb("xnf", [128, D]); xmf = cx.sb("xmf", [128, D])
    xnb2 = cx.sb("xnb2", [128, 2 * D], BF16)
    bb0 = cx.sb("bb0", [128, D]); bb1 = cx.sb("bb1", [128, D])
    bb2 = cx.sb("bb2", [128, D]); bb3 = cx.sb("bb3", [128, D])
    ln1g, ln1b, lng, lnb = bb0, bb1, bb2, bb3
    ln2g, ln2b, w2b = bb0, bb1, bb2
    ff2b = bb3
    qbb = bb0
    convs = cx.sb("convs", [128, 24]); dpc = cx.sb("dpc", [128, 8])
    w1bc = cx.sb("w1bc", [128, 8]); v1bc = cx.sb("v1bc", [128, 8])
    dtbc = cx.sb("dtbc", [128, 8]); bpbc = cx.sb("bpbc", [16, 2])
    ff1bc = cx.sb("ff1bc", [128, 24])
    ohb = cx.sb("ohb", [16, 16 * 128], BF16)
    idb = cx.sb("idb", [128, 128], BF16)
    negA = cx.sb("negA", [128, 8 * 16])
    wrb = cx.sb("wrb", [128, D], BF16)
    wcb = cx.sb("wcb", [128, 8 * 128], BF16)
    wcbB = cx.sb("wcbB", [128, 8 * 128], BF16)
    x1t = cx.sb("x1t", [128, T])
    btTb = cx.sb("btTb", [16, T], BF16); ctTb = cx.sb("ctTb", [16, T], BF16)
    delg = cx.sb("delg", [128, T])
    xcg = cx.sb("xcg", [128, T], BF16)
    ubf = cx.sb("ubf", [128, T], BF16)
    dax = cx.sb("dax", [128, T])
    dbx = x1t
    hsc = cx.sb("hsc", [128, T])
    yg = cx.sb("yg", [128, T])
    vtg = cx.sb("vtg", [128, T], BF16)
    htb = cx.sb("htb", [128, 24 * 256], BF16)
    stile = cx.sb("stile", [128, D]); gtile = cx.sb("gtile", [128, D])
    sgt = cx.sb("sgt", [128, D])
    qx = xnf; qu = xmf
    qsc = cx.sb("qsc", [128, 64]); qm2 = sq; qstep = cx.sb("qstep", [128, D])
    qbf = cx.sb("qbf", [128, D], BF16)
    qxt = cx.sb("qxt", [128, 8 * 128], BF16)

    ps = [cx.ps(f"ps{i}", [128, 512]) for i in range(6)]
    pstb = cx.ps("pstb", [128, 128], BF16)

    for name, dst, k in [("ln1g_d", ln1g, "bb0"), ("ln1b_d", ln1b, "bb1"),
                         ("lng_d", lng, "bb2"), ("lnb_d", lnb, "bb3"),
                         ("convw", convs, "convs"),
                         ("dpc_d", dpc, "dpc"), ("w1b_d", w1bc, "w1bc"),
                         ("v1b_d", v1bc, "v1bc"), ("dtb_d", dtbc, "dtbc"),
                         ("bpb_d", bpbc, "bpbc"), ("ff1b_d", ff1bc, "ff1bc"),
                         ("oh_d", ohb, "ohb"), ("identb2", idb, "idb")]:
        pr.op("sync", lambda E, s=cx.t[name], d=dst: E.dma_start(out=d[:, :], in_=s[:, :]),
              writes=[k], dma=True)
    for src_, dst_, tag in [(w1w, w1_bf, "cw1w"), (v1w, v1_bf, "cv1"),
                            (w2w, w2_bf, "cw2w"), (dtw, dt_bf, "cdt"),
                            (bpw, bp_bf, "cbp"), (cpw, cp_bf, "ccp"),
                            (ff1w, ff1_bf, "cff1"), (ff2w, ff2_bf, "cff2")]:
        cast_weight2(cx, src_, dst_, src_.shape[0], src_.shape[1], tag,
                     bufs=(stile, "stile", qbf, "qbf"))
    for g in range(8):
        pr.op("sync", lambda E, g=g: E.dma_start(out=negA[:, g * 16:(g + 1) * 16],
                                                 in_=alog[g * 128:(g + 1) * 128, :]),
              writes=["negA"], dma=True)
    pr.op("scalar", lambda E: E.activation(negA[:, :], negA[:, :], AF.Exp),
          reads=["negA"], writes=["negA"])
    pr.op("vector", lambda E: E.tensor_scalar(negA[:, :], negA[:, :], -1.0, None, ALU.mult),
          reads=["negA"], writes=["negA"])

    for i in range(NT):
        xt, xk = (xa, "xa") if i % 2 == 0 else (xb_, "xb_")
        pr.op("sync", lambda E, i=i, xt=xt: E.dma_start(out=xt[:, :], in_=xin[i * 128:(i + 1) * 128, :]),
              writes=[xk], dma=True)
        ln_tile(pr, xt[:, :], [xk], xnf[:, :], "xnf", cols, "cols", sq[:, :], "sq")
        pr.op("gpsimd", lambda E, i=i: E.dma_start(out=xn_d[i * 128:(i + 1) * 128, :], in_=xnf[:, :]),
              reads=["xnf"], writes=[f"xn_d{i}"], dma=True)
        pr.op("vector", lambda E: E.tensor_mul(xmf[:, :], xnf[:, :], ln1g[:, :]),
              reads=["xnf", "bb0"], writes=["xmf"])
        pr.op("vector", lambda E: E.tensor_add(xmf[:, :], xmf[:, :], ln1b[:, :]),
              reads=["xmf", "bb1"], writes=["xmf"])
        ln_tile(pr, xmf[:, :], ["xmf"], xmf[:, :], "xmf", cols, "cols", sq[:, :], "sq")
        pr.op("vector", lambda E: E.tensor_mul(xmf[:, :], xmf[:, :], lng[:, :]),
              reads=["xmf", "bb2"], writes=["xmf"])
        pr.op("vector", lambda E: E.tensor_add(xmf[:, :], xmf[:, :], lnb[:, :]),
              reads=["xmf", "bb3"], writes=["xmf"])
        half = (i % 2) * D
        hk = f"xm2h{i % 2}"
        pr.op("scalar", lambda E, half=half: E.copy(xnb2[:, half:half + D], xmf[:, :]),
              reads=["xmf"], writes=[hk])
        for j in range(ND):
            pr.op("tensor", lambda E, half=half, j=j: E.transpose(
                pstb[:, :], xnb2[:, half + j * 128: half + (j + 1) * 128], idb[:, :]),
                reads=[hk, "idb"], writes=["pstb"])
            pr.op("vector", lambda E: E.tensor_copy(qbf[:, 0:128], pstb[:, :]),
                  reads=["pstb"], writes=["qbf"])
            pr.op("gpsimd", lambda E, i=i, j=j: E.dma_start(
                out=xmT_d[j * 128:(j + 1) * 128, i * 128:(i + 1) * 128], in_=qbf[:, 0:128]),
                reads=["qbf"], writes=[f"xmT{j}"], dma=True)

    rgA = cx.sb("rgA", [128, 512], BF16); rgB = cx.sb("rgB", [128, 512], BF16)

    def dproj(w_bf, wtag, in_dram, in_tag, out_w, bias_sbcol, fin_cb, wname):
        njt = (out_w + 127) // 128
        for j in range(njt):
            po = min(128, out_w - j * 128)
            pr.op("sync", lambda E, j=j, po=po, w_bf=w_bf: E.dma_start(
                out=wcb[:, 0:8 * po].rearrange("p (a q) -> p a q", q=po),
                in_=w_bf[0:D, j * 128:j * 128 + po].rearrange("(a p) q -> p a q", p=128)),
                reads=[wtag + "done"], writes=["wcb"], dma=True)
            for tc in range(8):
                for k in range(8):
                    rb, rk = (rgA, "rgA") if k % 2 == 0 else (rgB, "rgB")
                    pr.op("sync" if k % 2 == 0 else "scalar",
                          lambda E, k=k, tc=tc, rb=rb, in_dram=in_dram: E.dma_start(
                        out=rb[:, :], in_=in_dram[k * 128:(k + 1) * 128, tc * 512:(tc + 1) * 512]),
                        reads=[in_tag + str(k)], writes=[rk], dma=True)
                    pr.op("tensor", lambda E, k=k, po=po, rb=rb: E.matmul(
                        ps[0][0:po, :], wcb[:, k * po:(k + 1) * po], rb[:, :],
                        start=(k == 0), stop=(k == 7)),
                        reads=["wcb", rk], writes=["ps0"] if k == 7 else [],
                        war=["ps0"] if k == 0 else ())
                if bias_sbcol is None:
                    pr.op("vector", lambda E, tc=tc, po=po: E.tensor_copy(
                        x1t[0:po, tc * 512:(tc + 1) * 512], ps[0][0:po, :]),
                        reads=["ps0"], writes=["x1t"])
                else:
                    bsb, bkey, = bias_sbcol
                    pr.op("vector", lambda E, tc=tc, po=po, bsb=bsb, j=j: E.tensor_scalar(
                        x1t[0:po, tc * 512:(tc + 1) * 512], ps[0][0:po, :],
                        bsb[0:po, j:j + 1], None, ALU.add),
                        reads=["ps0", bkey], writes=["x1t"])
            fin_cb(j)

    def fin_w1(j):
        pr.op("vector", lambda E, j=j: E.tensor_scalar(
            dax[:, :], x1t[:, :], convs[:, 3 * j + 1:3 * j + 2], None, ALU.mult),
            reads=["x1t", "convs"], writes=["dax"])
        pr.op("vector", lambda E, j=j: E.scalar_tensor_tensor(
            dax[:, 1:T], x1t[:, 0:T - 1], convs[:, 3 * j:3 * j + 1], dax[:, 1:T],
            ALU.mult, ALU.add),
            reads=["x1t", "convs", "dax"], writes=["dax"])
        pr.op("vector", lambda E, j=j: E.scalar_tensor_tensor(
            dax[:, 0:T - 1], x1t[:, 1:T], convs[:, 3 * j + 2:3 * j + 3], dax[:, 0:T - 1],
            ALU.mult, ALU.add),
            reads=["x1t", "convs", "dax"], writes=["dax"])
        pr.op("scalar", lambda E: E.activation(dax[:, :], dax[:, :], AF.Silu),
              reads=["dax"], writes=["dax"])
        pr.op("vector", lambda E: E.tensor_copy(xcg[:, :], dax[:, :]),
              reads=["dax"], writes=["xcg"])
        pr.op("gpsimd", lambda E, j=j: E.dma_start(out=xcT_d[j * 128:(j + 1) * 128, :], in_=xcg[:, :]),
              reads=["xcg"], writes=[f"xcT{j}"], dma=True)

    dproj(w1_bf, "cw1w", xmT_d, "xmT", D, (w1bc, "w1bc"), fin_w1, "w1")

    def fin_v(j):
        pr.op("scalar", lambda E: E.activation(x1t[:, :], x1t[:, :], AF.Silu),
              reads=["x1t"], writes=["x1t"])
        pr.op("vector", lambda E: E.tensor_copy(xcg[:, :], x1t[:, :]),
              reads=["x1t"], writes=["xcg"])
        pr.op("gpsimd", lambda E, j=j: E.dma_start(out=vT_d[j * 128:(j + 1) * 128, :], in_=xcg[:, :]),
              reads=["xcg"], writes=[f"vT{j}"], dma=True)

    dproj(v1_bf, "cv1", xmT_d, "xmT", D, (v1bc, "v1bc"), fin_v, "v1")

    def fin_dt(j):
        pr.op("scalar", lambda E: E.activation(sq[:, 0:T] if False else dax[:, :], x1t[:, :], AF.Abs),
              reads=["x1t"], writes=["dax"])
        pr.op("scalar", lambda E: E.activation(dax[:, :], dax[:, :], AF.Exp, scale=-1.0),
              reads=["dax"], writes=["dax"])
        pr.op("scalar", lambda E: E.activation(dax[:, :], dax[:, :], AF.Ln, bias=1.0),
              reads=["dax"], writes=["dax"])
        pr.op("vector", lambda E: E.tensor_scalar(hsc[:, :], x1t[:, :], 0.0, None, ALU.max),
              reads=["x1t"], writes=["hsc"])
        pr.op("vector", lambda E: E.tensor_add(dax[:, :], dax[:, :], hsc[:, :]),
              reads=["dax", "hsc"], writes=["dax"])
        pr.op("gpsimd", lambda E, j=j: E.dma_start(out=del_d[j * 128:(j + 1) * 128, :], in_=dax[:, :]),
              reads=["dax"], writes=[f"del{j}"], dma=True)

    dproj(dt_bf, "cdt", xcT_d, "xcT", D, (dtbc, "dtbc"), fin_dt, "dt")

    for (wb_, wt_, dstt, dkey, bcol) in [(bp_bf, "cbp", btTb, "btTb", 0), (cp_bf, "ccp", ctTb, "ctTb", 1)]:
        pr.op("sync", lambda E, wb_=wb_: E.dma_start(
            out=wcb[:, 0:8 * 16].rearrange("p (a q) -> p a q", q=16),
            in_=wb_[0:D, 0:16].rearrange("(a p) q -> p a q", p=128)),
            reads=[wt_ + "done"], writes=["wcb"], dma=True)
        for tc in range(8):
            for k in range(8):
                rb, rk = (rgA, "rgA") if k % 2 == 0 else (rgB, "rgB")
                pr.op("sync", lambda E, k=k, tc=tc, rb=rb: E.dma_start(
                    out=rb[:, :], in_=xcT_d[k * 128:(k + 1) * 128, tc * 512:(tc + 1) * 512]),
                    reads=[f"xcT{k}"], writes=[rk], dma=True)
                pr.op("tensor", lambda E, k=k, rb=rb: E.matmul(
                    ps[0][0:16, :], wcb[:, k * 16:(k + 1) * 16], rb[:, :],
                    start=(k == 0), stop=(k == 7)),
                    reads=["wcb", rk], writes=["ps0"] if k == 7 else [],
                    war=["ps0"] if k == 0 else ())
            pr.op("vector", lambda E, tc=tc, dstt=dstt, bcol=bcol: E.tensor_scalar(
                dstt[:, tc * 512:(tc + 1) * 512], ps[0][0:16, :],
                bpbc[:, bcol:bcol + 1], None, ALU.add),
                reads=["ps0", "bpbc"], writes=[dkey])

    for g in range(8):
        pr.op("sync", lambda E, g=g: E.dma_start(out=delg[:, :], in_=del_d[g * 128:(g + 1) * 128, :]),
              reads=[f"del{g}"], writes=["delg"], dma=True)
        pr.op("sync", lambda E, g=g: E.dma_start(out=xcg[:, :], in_=xcT_d[g * 128:(g + 1) * 128, :]),
              reads=[f"xcT{g}"], writes=["xcg"], dma=True)
        pr.op("vector", lambda E: E.tensor_mul(ubf[:, :], delg[:, :], xcg[:, :]),
              reads=["delg", "xcg"], writes=["ubf"])
        pr.op("gpsimd", lambda E: E.memset(yg[:, :], 0.0), writes=["yg"])
        for n in range(NSt):
            pr.op("scalar", lambda E, g=g, n=n: E.activation(
                dax[:, :], delg[:, :], AF.Exp, scale=negA[:, g * 16 + n: g * 16 + n + 1]),
                reads=["delg", "negA"], writes=["dax"])
            for c in range(8):
                pb, pbi = ("ps1", 1) if c % 2 == 0 else ("ps4", 4)
                pr.op("tensor", lambda E, n=n, c=c, pbi=pbi: E.matmul(
                    ps[pbi][:, :], ohb[:, n * 128:(n + 1) * 128],
                    btTb[:, c * 512:(c + 1) * 512], start=True, stop=True),
                    reads=["ohb", "btTb"], writes=[pb])
                pr.op("vector", lambda E, c=c, pbi=pbi: E.tensor_mul(
                    dbx[:, c * 512:(c + 1) * 512], ps[pbi][:, :], ubf[:, c * 512:(c + 1) * 512]),
                    reads=[pb, "ubf"], writes=["dbx"])
            pr.op("vector", lambda E: E.tensor_tensor_scan(
                hsc[:, :], dax[:, :], dbx[:, :], 0.0, ALU.mult, ALU.add),
                reads=["dax", "dbx"], writes=["hsc"])
            for c in range(8):
                pc_, pci = ("ps2", 2) if c % 2 == 0 else ("ps5", 5)
                qsl2 = qstep[:, 0:512] if c % 2 == 0 else qstep[:, 512:1024]
                qk2 = "qstep" if c % 2 == 0 else "qstepB"
                pr.op("tensor", lambda E, n=n, c=c, pci=pci: E.matmul(
                    ps[pci][:, :], ohb[:, n * 128:(n + 1) * 128],
                    ctTb[:, c * 512:(c + 1) * 512], start=True, stop=True),
                    reads=["ohb", "ctTb"], writes=[pc_])
                pr.op("vector", lambda E, c=c, pci=pci, qsl2=qsl2: E.tensor_mul(
                    qsl2, ps[pci][:, :], hsc[:, c * 512:(c + 1) * 512]),
                    reads=[pc_, "hsc"], writes=[qk2])
                pr.op("vector", lambda E, c=c, qsl2=qsl2: E.tensor_add(
                    yg[:, c * 512:(c + 1) * 512], yg[:, c * 512:(c + 1) * 512], qsl2),
                    reads=["yg", qk2], writes=["yg"])
        pr.op("vector", lambda E, g=g: E.scalar_tensor_tensor(
            yg[:, :], xcg[:, :], dpc[:, g:g + 1], yg[:, :], ALU.mult, ALU.add),
            reads=["xcg", "dpc", "yg"], writes=["yg"])
        pr.op("sync", lambda E, g=g: E.dma_start(out=vtg[:, :], in_=vT_d[g * 128:(g + 1) * 128, :]),
              reads=[f"vT{g}"], writes=["vtg"], dma=True)
        pr.op("vector", lambda E: E.tensor_mul(ubf[:, :], yg[:, :], vtg[:, :]),
              reads=["yg", "vtg"], writes=["ubf"])
        pr.op("gpsimd", lambda E, g=g: E.dma_start(out=yvT_d[g * 128:(g + 1) * 128, :], in_=ubf[:, :]),
              reads=["ubf"], writes=[f"yvT{g}"], dma=True)

    for name, dst, k in [("ln2g_d", ln2g, "bb0"), ("ln2b_d", ln2b, "bb1"),
                         ("w2b_d", w2b, "bb2")]:
        pr.op("sync", lambda E, s=cx.t[name], d=dst: E.dma_start(out=d[:, :], in_=s[:, :]),
              writes=[k], dma=True)
    for i in range(NT):
        for c in range(2):
            for k in range(8):
                rb, rk = (rgA, "rgA") if k % 2 == 0 else (rgB, "rgB")
                pr.op("sync", lambda E, i=i, k=k, rb=rb: E.dma_start(
                    out=rb[:, 0:128], in_=yvT_d[k * 128:(k + 1) * 128, i * 128:(i + 1) * 128]),
                    reads=[f"yvT{k}"], writes=[rk], dma=True)
                par = k % 2
                wsl = wrb[:, par * 512:(par + 1) * 512]
                wkk = f"wrbh{par}"
                pr.op("scalar", lambda E, k=k, c=c, wsl=wsl: E.dma_start(
                    out=wsl, in_=w2_bf[k * 128:(k + 1) * 128, c * 512:(c + 1) * 512]),
                    reads=["cw2wdone"], writes=[wkk], dma=True)
                pr.op("tensor", lambda E, k=k, c=c, rb=rb, wsl=wsl: E.matmul(
                    ps[3][:, :], rb[:, 0:128], wsl,
                    start=(k == 0), stop=(k == 7)),
                    reads=[rk, wkk], writes=["ps3"] if k == 7 else [],
                    war=["ps3"] if k == 0 else ())
            pr.op("vector", lambda E, c=c: E.tensor_add(
                stile[:, c * 512:(c + 1) * 512], ps[3][:, :], w2b[:, c * 512:(c + 1) * 512]),
                reads=["ps3", "bb2"], writes=["stile"])
        xt, xk = (xa, "xa") if i % 2 == 0 else (xb_, "xb_")
        pr.op("sync", lambda E, i=i, xt=xt: E.dma_start(out=xt[:, :], in_=xn_d[i * 128:(i + 1) * 128, :]),
              reads=[f"xn_d{i}"], writes=[xk], dma=True)
        pr.op("vector", lambda E, xt=xt: E.tensor_add(stile[:, :], stile[:, :], xt[:, :]),
              reads=["stile", xk], writes=["stile"])
        pr.op("gpsimd", lambda E, i=i: E.dma_start(out=x2m_d[i * 128:(i + 1) * 128, :], in_=stile[:, :]),
              reads=["stile"], writes=[f"x2m{i}"], dma=True)
        ln_tile(pr, stile[:, :], ["stile"], xnf[:, :], "xnf", cols, "cols", sq[:, :], "sq")
        pr.op("vector", lambda E: E.tensor_mul(xnf[:, :], xnf[:, :], ln2g[:, :]),
              reads=["xnf", "bb0"], writes=["xnf"])
        pr.op("vector", lambda E: E.tensor_add(xnf[:, :], xnf[:, :], ln2b[:, :]),
              reads=["xnf", "bb1"], writes=["xnf"])
        half = (i % 2) * D
        hk = f"x2n2h{i % 2}"
        pr.op("scalar", lambda E, half=half: E.copy(xnb2[:, half:half + D], xnf[:, :]),
              reads=["xnf"], writes=[hk])
        for j in range(ND):
            pr.op("tensor", lambda E, half=half, j=j: E.transpose(
                pstb[:, :], xnb2[:, half + j * 128: half + (j + 1) * 128], idb[:, :]),
                reads=[hk, "idb"], writes=["pstb"])
            pr.op("vector", lambda E: E.tensor_copy(qbf[:, 0:128], pstb[:, :]),
                  reads=["pstb"], writes=["qbf"])
            pr.op("gpsimd", lambda E, i=i, j=j: E.dma_start(
                out=xn2T_d[j * 128:(j + 1) * 128, i * 128:(i + 1) * 128], in_=qbf[:, 0:128]),
                reads=["qbf"], writes=[f"xn2T{j}"], dma=True)

    pr.op("sync", lambda E: E.dma_start(out=ff2b[:, :], in_=ff2b_d[:, :]), writes=["bb3"], dma=True)
    for tc in range(16):
        t0 = tc * 256
        for kk in range(24):
            wb_, wk_ = (wcb, "wcb") if kk % 2 == 0 else (wcbB, "wcbB")
            pp, pk = ("ps0", 0) if kk % 2 == 0 else ("ps4", 4)
            pr.op("sync", lambda E, kk=kk, wb_=wb_: E.dma_start(
                out=wb_[:, 0:8 * 128].rearrange("p (a q) -> p a q", q=128),
                in_=ff1_bf[0:D, kk * 128:(kk + 1) * 128].rearrange("(a p) q -> p a q", p=128)),
                reads=["cff1done"], writes=[wk_], dma=True)
            for k in range(8):
                rb, rk = (rgA, "rgA") if k % 2 == 0 else (rgB, "rgB")
                pr.op("sync", lambda E, k=k, t0=t0, rb=rb: E.dma_start(
                    out=rb[:, 0:256], in_=xn2T_d[k * 128:(k + 1) * 128, t0:t0 + 256]),
                    reads=[f"xn2T{k}"], writes=[rk], dma=True)
                pr.op("tensor", lambda E, kk=kk, k=k, rb=rb, wb_=wb_, pk=pk: E.matmul(
                    ps[pk][:, 0:256], wb_[:, k * 128:(k + 1) * 128], rb[:, 0:256],
                    start=(k == 0), stop=(k == 7)),
                    reads=[wk_, rk], writes=[pp] if k == 7 else [],
                    war=[pp] if k == 0 else ())
            pr.op("scalar", lambda E, kk=kk, pk=pk: E.activation(
                htb[:, kk * 256:(kk + 1) * 256], ps[pk][:, 0:256], AF.Gelu,
                bias=ff1bc[:, kk:kk + 1]),
                reads=[pp, "ff1bc"], writes=[f"htb{kk}"])
        for tt in range(2):
            for c in range(2):
                for kk in range(24):
                    par = kk % 2
                    wsl = wrb[:, par * 512:(par + 1) * 512]
                    wkk = f"wrbh{par}"
                    pr.op("sync" if kk % 2 == 0 else "scalar",
                          lambda E, kk=kk, c=c, wsl=wsl: E.dma_start(
                        out=wsl, in_=ff2_bf[kk * 128:(kk + 1) * 128, c * 512:(c + 1) * 512]),
                        reads=["cff2done"], writes=[wkk], dma=True)
                    pr.op("tensor", lambda E, kk=kk, c=c, tt=tt, wsl=wsl: E.matmul(
                        ps[1][:, :], htb[:, kk * 256 + tt * 128: kk * 256 + (tt + 1) * 128],
                        wsl, start=(kk == 0), stop=(kk == 23)),
                        reads=[wkk, f"htb{kk}"], writes=["ps1"] if kk == 23 else [],
                        war=["ps1"] if kk == 0 else ())
                pr.op("vector", lambda E, c=c: E.tensor_add(
                    stile[:, c * 512:(c + 1) * 512], ps[1][:, :], ff2b[:, c * 512:(c + 1) * 512]),
                    reads=["ps1", "bb3"], writes=["stile"])
            i = 2 * tc + tt
            xt, xk = (xa, "xa") if i % 2 == 0 else (xb_, "xb_")
            pr.op("scalar", lambda E, i=i, xt=xt: E.dma_start(
                out=xt[:, :], in_=x2m_d[i * 128:(i + 1) * 128, :]),
                reads=[f"x2m{i}"], writes=[xk], dma=True)
            pr.op("vector", lambda E, xt=xt: E.tensor_add(stile[:, :], stile[:, :], xt[:, :]),
                  reads=["stile", xk], writes=["stile"])
            pr.op("sync", lambda E, i=i: E.dma_start(out=s_d[i * 128:(i + 1) * 128, :], in_=stile[:, :]),
                  reads=["stile"], writes=[f"s_d{i}"], dma=True)

    FP6MAX = 7.5
    def emit_quant(in_ap, in_keys, rows, out_bf_ap, outkey):
        pr.op("scalar", lambda E: E.activation(qu[0:rows, :], in_ap, AF.Abs),
              reads=in_keys, writes=["xmf"])
        pr.op("vector", lambda E: E.reduce_max(qsc[0:rows, 0:32],
                                               qu[0:rows, :].rearrange("p (a b) -> p a b", b=32), axis=AXX),
              reads=["xmf"], writes=["qsc"])
        pr.op("vector", lambda E: E.tensor_scalar(qsc[0:rows, 0:32], qsc[0:rows, 0:32],
                                                  1.0 / FP6MAX, 1e-30, ALU.mult, ALU.max),
              reads=["qsc"], writes=["qsc"])
        pr.op("vector", lambda E: E.reciprocal(qsc[0:rows, 32:64], qsc[0:rows, 0:32]),
              reads=["qsc"], writes=["qsc"])
        qse = cx.t["dax"]
        rse = cx.t["hsc"]
        for b in range(32):
            pr.op("vector", lambda E, b=b: E.tensor_copy(
                qse[0:rows, 0:D].rearrange("p (a b2) -> p a b2", b2=32)[:, :, b:b + 1],
                qsc[0:rows, 0:32].rearrange("p (a o) -> p a o", o=1)),
                reads=["qsc"], writes=["dax"])
            pr.op("vector", lambda E, b=b: E.tensor_copy(
                rse[0:rows, 0:D].rearrange("p (a b2) -> p a b2", b2=32)[:, :, b:b + 1],
                qsc[0:rows, 32:64].rearrange("p (a o) -> p a o", o=1)),
                reads=["qsc"], writes=["hsc"])
        pr.op("vector", lambda E: E.tensor_tensor(qx[0:rows, :], in_ap, rse[0:rows, 0:D],
                                                  ALU.mult),
              reads=in_keys + ["hsc"], writes=["xnf"])
        pr.op("scalar", lambda E: E.activation(qu[0:rows, :], qx[0:rows, :], AF.Abs),
              reads=["xnf"], writes=["xmf"])
        pr.op("vector", lambda E: E.tensor_scalar(qm2[0:rows, :], qu[0:rows, :],
                                                  1.9375, None, ALU.is_gt),
              reads=["xmf"], writes=["sq"])
        pr.op("vector", lambda E: E.tensor_scalar(qstep[0:rows, :], qm2[0:rows, :],
                                                  0.125, 0.125, ALU.mult, ALU.add),
              reads=["sq"], writes=["qstep"])
        pr.op("vector", lambda E: E.tensor_scalar(qu[0:rows, :], qm2[0:rows, :],
                                                  -4.0, 8.0, ALU.mult, ALU.add),
              reads=["sq"], writes=["xmf"])
        pr.op("vector", lambda E: E.tensor_scalar(qm2[0:rows, :], qm2[0:rows, :], 0.0,
                                                  None, ALU.mult),
              reads=["sq"], writes=["sq"]) if False else None
        pr.op("scalar", lambda E: E.activation(qm2[0:rows, :], qx[0:rows, :], AF.Abs),
              reads=["xnf"], writes=["sq"])
        pr.op("vector", lambda E: E.tensor_scalar(qm2[0:rows, :], qm2[0:rows, :],
                                                  3.875, None, ALU.is_gt),
              reads=["sq"], writes=["sq"])
        pr.op("vector", lambda E: E.scalar_tensor_tensor(qstep[0:rows, :], qm2[0:rows, :],
                                                         0.25, qstep[0:rows, :], ALU.mult, ALU.add),
              reads=["sq", "qstep"], writes=["qstep"])
        pr.op("vector", lambda E: E.scalar_tensor_tensor(qu[0:rows, :], qm2[0:rows, :],
                                                         -2.0, qu[0:rows, :], ALU.mult, ALU.add),
              reads=["sq", "xmf"], writes=["xmf"])
        pr.op("vector", lambda E: E.tensor_mul(qx[0:rows, :], qx[0:rows, :], qu[0:rows, :]),
              reads=["xnf", "xmf"], writes=["xnf"])
        MAGIC = 12582912.0
        pr.op("vector", lambda E: E.tensor_scalar(qx[0:rows, :], qx[0:rows, :],
                                                  MAGIC, None, ALU.add),
              reads=["xnf"], writes=["xnf"])
        pr.op("vector", lambda E: E.tensor_scalar(qx[0:rows, :], qx[0:rows, :],
                                                  MAGIC, None, ALU.subtract),
              reads=["xnf"], writes=["xnf"])
        pr.op("vector", lambda E: E.tensor_mul(qx[0:rows, :], qx[0:rows, :], qstep[0:rows, :]),
              reads=["xnf", "qstep"], writes=["xnf"])
        pr.op("vector", lambda E: E.tensor_scalar(qx[0:rows, :], qx[0:rows, :],
                                                  FP6MAX, -FP6MAX, ALU.min, ALU.max),
              reads=["xnf"], writes=["xnf"])
        pr.op("vector", lambda E: E.tensor_tensor(out_bf_ap, qx[0:rows, :],
                                                  qse[0:rows, 0:D], ALU.mult),
              reads=["xnf", "dax"], writes=[outkey])

    pr.op("sync", lambda E: E.dma_start(out=qbb[:, :], in_=qb_d[:, :]), writes=["bb0"], dma=True)
    for k in range(8):
        pr.op("sync", lambda E, k=k: E.dma_start(out=xa[:, :], in_=qw[k * 128:(k + 1) * 128, :]),
              writes=["xa"], dma=True)
        emit_quant(xa[:, :], ["xa"], 128, qbf[:, :], "qbf")
        for j in range(ND):
            pr.op("tensor", lambda E, j=j: E.transpose(
                pstb[:, :], qbf[:, j * 128:(j + 1) * 128], idb[:, :]),
                reads=["qbf", "idb"], writes=["pstb"])
            pr.op("vector", lambda E: E.tensor_copy(qxt[:, 0:128], pstb[:, :]),
                  reads=["pstb"], writes=["qxt"])
            pr.op("gpsimd", lambda E, j=j, k=k: E.dma_start(
                out=wdT_d[j * 128:(j + 1) * 128, k * 128:(k + 1) * 128], in_=qxt[:, 0:128]),
                reads=["qxt"], writes=[f"wdT{j}"], dma=True)
    for i in range(NT):
        xt, xk = (xa, "xa") if i % 2 == 0 else (xb_, "xb_")
        pr.op("sync", lambda E, i=i, xt=xt: E.dma_start(out=xt[:, :], in_=xn_d[i * 128:(i + 1) * 128, :]),
              reads=[f"xn_d{i}"], writes=[xk], dma=True)
        emit_quant(xt[:, :], [xk], 128, qbf[:, :], "qbf")
        for j in range(ND):
            pr.op("tensor", lambda E, j=j: E.transpose(
                pstb[:, :], qbf[:, j * 128:(j + 1) * 128], idb[:, :]),
                reads=["qbf", "idb"], writes=["pstb"])
            pr.op("vector", lambda E, j=j: E.tensor_copy(qxt[:, j * 128:(j + 1) * 128], pstb[:, :]),
                  reads=["pstb"], writes=["qxt"])
        for c in range(2):
            for j in range(ND):
                par = j % 2
                wsl = wrb[:, par * 512:(par + 1) * 512]
                wkk = f"wrbh{par}"
                pr.op("sync", lambda E, j=j, c=c, wsl=wsl: E.dma_start(
                    out=wsl, in_=wdT_d[j * 128:(j + 1) * 128, c * 512:(c + 1) * 512]),
                    reads=[f"wdT{j}"], writes=[wkk], dma=True)
                pr.op("tensor", lambda E, j=j, c=c, wsl=wsl: E.matmul(
                    ps[4][:, :], qxt[:, j * 128:(j + 1) * 128], wsl,
                    start=(j == 0), stop=(j == 7)),
                    reads=["qxt", wkk], writes=["ps4"] if j == 7 else [],
                    war=["ps4"] if j == 0 else ())
            pr.op("vector", lambda E, c=c: E.tensor_add(
                gtile[:, c * 512:(c + 1) * 512], ps[4][:, :], qbb[:, c * 512:(c + 1) * 512]),
                reads=["ps4", "bb0"], writes=["gtile"])
        pr.op("scalar", lambda E: E.activation(gtile[:, :], gtile[:, :], AF.Silu),
              reads=["gtile"], writes=["gtile"])
        pr.op("sync", lambda E, i=i: E.dma_start(out=g_d[i * 128:(i + 1) * 128, :], in_=gtile[:, :]),
              reads=["gtile"], writes=[f"g_d{i}"], dma=True)

    for i in range(NT):
        xt, xk = (xa, "xa") if i % 2 == 0 else (xb_, "xb_")
        pr.op("sync", lambda E, i=i, xt=xt: E.dma_start(out=xt[:, :], in_=xin[i * 128:(i + 1) * 128, :]),
              writes=[xk], dma=True)
        pr.op("scalar", lambda E, i=i: E.dma_start(out=sgt[:, :], in_=s_d[i * 128:(i + 1) * 128, :]),
              reads=[f"s_d{i}"], writes=["sgt"], dma=True)
        pr.op("gpsimd", lambda E, i=i: E.dma_start(out=gtile[:, :], in_=g_d[i * 128:(i + 1) * 128, :]),
              reads=[f"g_d{i}"], writes=["gtile"], dma=True)
        pr.op("vector", lambda E: E.tensor_mul(sgt[:, :], sgt[:, :], gtile[:, :]),
              reads=["sgt", "gtile"], writes=["sgt"])
        pr.op("vector", lambda E, xt=xt: E.tensor_add(sgt[:, :], sgt[:, :], xt[:, :]),
              reads=["sgt", xk], writes=["sgt"])
        pr.op("sync", lambda E, i=i: E.dma_start(out=xout[i * 128:(i + 1) * 128, :], in_=sgt[:, :]),
              reads=["sgt"], writes=[f"xout{i}"], dma=True)
    return cx


def _host_consts_l2(inputs):
    def bc(v):
        v = np.asarray(v, np.float32)
        return np.ascontiguousarray(np.broadcast_to(v[None, :], (128, v.shape[0])))
    def colt(v, n):
        return np.ascontiguousarray(np.asarray(v, np.float32).reshape(n, 128).T)
    oh = np.zeros((16, 16 * 128), np.float32)
    for n in range(16):
        oh[n, n * 128:(n + 1) * 128] = 1.0
    conv = np.asarray(inputs["L2_attn_conv_w"], np.float32).reshape(D, 3)
    convc = np.zeros((128, 24), np.float32)
    for g in range(8):
        convc[:, 3 * g:3 * (g + 1)] = conv[g * 128:(g + 1) * 128, :]
    return {
        "w1w": np.asarray(inputs["L2_attn_w1_w"], np.float32),
        "v1w": np.asarray(inputs["L2_attn_v1_w"], np.float32),
        "w2w": np.asarray(inputs["L2_attn_w2_w"], np.float32),
        "dtw": np.asarray(inputs["L2_attn_dt_w"], np.float32),
        "bpw": np.asarray(inputs["L2_attn_Bp_w"], np.float32),
        "cpw": np.asarray(inputs["L2_attn_Cp_w"], np.float32),
        "alog": np.asarray(inputs["L2_attn_A_log"], np.float32),
        "ff1w": np.asarray(inputs["L2_attn_ff1_w"], np.float32),
        "ff2w": np.asarray(inputs["L2_attn_ff2_w"], np.float32),
        "qw": np.asarray(inputs["L2_ffn_weight"], np.float32),
        "convw": convc,
        "dpc_d": colt(inputs["L2_attn_D_p"], 8),
        "w1b_d": colt(inputs["L2_attn_w1_b"], 8),
        "v1b_d": colt(inputs["L2_attn_v1_b"], 8),
        "dtb_d": colt(inputs["L2_attn_dt_b"], 8),
        "bpb_d": np.ascontiguousarray(np.stack([np.asarray(inputs["L2_attn_Bp_b"], np.float32),
                                                np.asarray(inputs["L2_attn_Cp_b"], np.float32)], axis=1)),
        "ff1b_d": colt(inputs["L2_attn_ff1_b"], 24),
        "w2b_d": bc(inputs["L2_attn_w2_b"]),
        "ff2b_d": bc(inputs["L2_attn_ff2_b"]),
        "qb_d": bc(inputs["L2_ffn_bias"]),
        "ln1g_d": bc(inputs["L2_attn_ln1_g"]), "ln1b_d": bc(inputs["L2_attn_ln1_b"]),
        "lng_d": bc(inputs["L2_attn_ln_g"]), "lnb_d": bc(inputs["L2_attn_ln_b"]),
        "ln2g_d": bc(inputs["L2_attn_ln2_g"]), "ln2b_d": bc(inputs["L2_attn_ln2_b"]),
        "oh_d": oh.astype(BFNP),
        "identb2": np.eye(128, dtype=BFNP),
    }


def build_l1(cx=None, xin_t=None, xout_t=None):
    standalone = cx is None
    if standalone:
        cx = Ctx()
    pr = cx.pr
    E8, KSEL = 8, 1024

    xin = xin_t if xin_t is not None else cx.din("xin", [T, D])
    x16 = None if xin_t is not None else cx.din("x16", [128, D])
    wqt = cx.din("wqt", [D, D]); wkt = cx.din("wkt", [D, D]); wvt = cx.din("wvt", [D, D])
    wot = cx.din("wot", [D, D])
    wlqt = cx.din("wlqt", [64, 64]); wlkt = cx.din("wlkt", [64, 64]); wlvt = cx.din("wlvt", [64, 64])
    bq_d = cx.din("bq_d", [128, D]); bv_d = cx.din("bv_d", [128, D]); bo_d = cx.din("bo_d", [128, D])
    lna_d = cx.din("lna_d", [128, D])
    lnc_d = cx.din("lnc_d", [128, 64]); lnd_d = cx.din("lnd_d", [128, 64])
    blq_d = cx.din("blq_d", [128, 64]); blk_d = cx.din("blk_d", [128, 64]); blv_d = cx.din("blv_d", [128, 64])
    sca_d = cx.din("sca_d", [1, 4])
    maskbd = cx.din("maskbd", [128, 128])
    identf1 = cx.din("identf1", [128, 128])
    identb1 = cx.din("identb1", [128, 128], BF16)
    gatewt = cx.din("gatewt", [D, E8])
    wg_t = cx.din("wg_t", [E8 * D, 4 * D])
    wu_t = cx.din("wu_t", [E8 * D, 4 * D])
    wd_t = cx.din("wd_t", [E8 * 4 * D, D])
    xout = xout_t if xout_t is not None else cx.dout("xout", [T, D])

    wg_bf = cx.dint("wg_bf", [E8 * D, 4 * D], BF16)
    wu_bf = cx.dint("wu_bf", [E8 * D, 4 * D], BF16)
    wd_bf = cx.dint("wd_bf", [E8 * 4 * D, D], BF16)
    wq_bf = cx.dint("wq_bf", [D, D], BF16); wk_bf = cx.dint("wk_bf", [D, D], BF16)
    wv_bf = cx.dint("wv_bf", [D, D], BF16); wo_bf = cx.dint("wo_bf", [D, D], BF16)
    xnb_d = cx.dint("xnb_d", [T, D], BF16)
    DG = D + 32
    xin_es = [cx.dint(f"xin_e{e}", [KSEL, DG], BF16) for e in range(8)]
    ye_ds = [cx.dint(f"ye_d{e}", [KSEL, D]) for e in range(8)]
    s16_d = cx.dint("s16_d", [128, D])
    g_d = cx.dint("g_d", [T, D])

    xa = cx.sb("xa", [128, D]); xb_ = cx.sb("xb_", [128, D])
    sq = cx.sb("sq", [128, D]); cols = cx.sb("cols", [128, 4])
    xnf = cx.sb("xnf", [128, D])
    xnb2 = cx.sb("xnb2", [128, 2 * D], BF16)
    xnT = cx.sb("xnT", [128, ND * T], BF16)
    idb = cx.sb("idb", [128, 128], BF16)
    bqb = cx.sb("bqb", [128, D]); bvb = cx.sb("bvb", [128, D]); bob = cx.sb("bob", [128, D])
    lnab = cx.sb("lnab", [128, D])
    lncb = cx.sb("lncb", [128, 64]); lndb = cx.sb("lndb", [128, 64])
    blqb = cx.sb("blqb", [128, 64]); blkb = cx.sb("blkb", [128, 64]); blvb = cx.sb("blvb", [128, 64])
    scas = cx.sb("scas", [1, 4])
    mbd = cx.sb("mbd", [128, 128])
    wrb = cx.sb("wrb", [128, D], BF16)
    wcb = cx.sb("wcb", [128, 8 * 128], BF16)
    x16s = cx.sb("x16s", [128, D])
    qn = cx.sb("qn", [128, D], BF16)
    kn = cx.sb("kn", [128, D], BF16)
    vn = cx.sb("vn", [128, D], BF16)
    qcur = cx.sb("qcur", [128, D], BF16)
    kcur = cx.sb("kcur", [128, D], BF16)
    vcur = cx.sb("vcur", [128, D], BF16)
    ai0 = cx.sb("ai0", [128, D], BF16)
    ai1 = cx.sb("ai1", [128, D], BF16)
    ai2 = cx.sb("ai2", [128, D], BF16)
    tq = cx.sb("tq", [64, 128], BF16)
    tq2 = cx.sb("tq2", [64, 128], BF16)
    lf = cx.sb("lf", [128, 64])
    lf2 = cx.sb("lf2", [128, 64])
    lfb = cx.sb("lfb", [128, 64], BF16)
    prb = cx.sb("prb", [128, 128], BF16)
    prf = cx.sb("prf", [128, 128])
    c64 = cx.sb("c64", [128, 4])
    dcol = cx.sb("dcol", [128, 16])
    onesc = cx.sb("onesc", [1, 128])
    scc = cx.sb("scc", [128, 4])
    wl3 = cx.sb("wl3", [64, 3 * 64], BF16)
    aff = cx.sb("aff", [128, E8])
    gcol2 = cx.sb("gcol2", [128, 1])
    affT = cx.sb("affT", [E8, T])
    m8 = cx.sb("m8", [E8, T])
    r8 = cx.sb("r8", [E8, T])
    ones8 = cx.sb("ones8", [E8, 128])
    lo8 = cx.sb("lo8", [E8, 4])
    rankc = cx.sb("rankc", [128, 8 * NT])
    rankI = cx.sb("rankI", [128, 8 * NT], I32)
    rkcol = cx.sb("rkcol", [128, 1], I32)
    gatc = cx.sb("gatc", [128, 8 * E8])
    xinT = cx.sb("xinT", [128, 8 * KSEL], BF16)
    htb = xnT
    yet = cx.sb("yet", [128, 512])
    acc = cx.sb("acc", [128, D])
    gtile = cx.sb("gtile", [128, D])

    ps = [cx.ps(f"ps{i}", [128, 512]) for i in range(6)]
    pstb = cx.ps("pstb", [128, 128], BF16)
    sgt = x16s

    for name, dst, k in [("bq_d", bqb, "bqb"), ("bv_d", bvb, "bvb"), ("bo_d", bob, "bob"),
                         ("lna_d", lnab, "lnab"), ("lnc_d", lncb, "lncb"),
                         ("lnd_d", lndb, "lndb"), ("blq_d", blqb, "blqb"),
                         ("blk_d", blkb, "blkb"), ("blv_d", blvb, "blvb"),
                         ("sca_d", scas, "scas"), ("maskbd", mbd, "mbd"),
                         ("identb1", idb, "idb")]:
        pr.op("sync", lambda E, s=cx.t[name], d=dst: E.dma_start(out=d[:, :], in_=s[:, :]),
              writes=[k], dma=True)
    pr.op("gpsimd", lambda E: E.memset(ones8[:, :], 1.0), writes=["ones8"])
    pr.op("gpsimd", lambda E: E.memset(onesc[:, :], 1.0), writes=["onesc"])
    bndreg = cx.es.enter_context(cx.nc.gpsimd.register("bndreg"))
    pr.q["gpsimd"].append(lambda E: E.reg_mov(bndreg, KSEL - 1))
    for src_, dst_, tag in [(wqt, wq_bf, "cwq"), (wkt, wk_bf, "cwk"),
                            (wvt, wv_bf, "cwv"), (wot, wo_bf, "cwo")]:
        cast_weight2(cx, src_, dst_, D, D, tag, bufs=(x16s, "x16s", qn, "qn"))
    cast_weight2(cx, wg_t, wg_bf, E8 * D, 4 * D, "cwg", bufs=(x16s, "x16s", qn, "qn"))
    cast_weight2(cx, wu_t, wu_bf, E8 * D, 4 * D, "cwu", bufs=(x16s, "x16s", qn, "qn"))
    cast_weight2(cx, wd_t, wd_bf, E8 * 4 * D, D, "cwd", bufs=(x16s, "x16s", qn, "qn"))
    for w_, off in [(wlqt, 0), (wlkt, 64), (wlvt, 128)]:
        pr.op("sync", lambda E, w_=w_: E.dma_start(out=lf[0:64, :], in_=w_[:, :]),
              writes=["lf"], dma=True)
        pr.op("scalar", lambda E, off=off: E.copy(wl3[:, off:off + 64], lf[0:64, 0:64]),
              reads=["lf"], writes=["wl3"])

    for i in range(NT):
        xt, xk = (xa, "xa") if i % 2 == 0 else (xb_, "xb_")
        pr.op("sync", lambda E, i=i, xt=xt: E.dma_start(out=xt[:, :], in_=xin[i * 128:(i + 1) * 128, :]),
              writes=[xk], dma=True)
        ln_tile(pr, xt[:, :], [xk], xnf[:, :], "xnf", cols, "cols", sq[:, :], "sq")
        half = (i % 2) * D
        hk = f"xnb2h{i % 2}"
        pr.op("scalar", lambda E, half=half: E.copy(xnb2[:, half:half + D], xnf[:, :]),
              reads=["xnf"], writes=[hk])
        pr.op("gpsimd", lambda E, i=i, half=half: E.dma_start(
            out=xnb_d[i * 128:(i + 1) * 128, :], in_=xnb2[:, half:half + D]),
            reads=[hk], writes=[f"xnbd{i}"], dma=True)
        for j in range(ND):
            pr.op("tensor", lambda E, half=half, j=j: E.transpose(
                pstb[:, :], xnb2[:, half + j * 128: half + (j + 1) * 128], idb[:, :]),
                reads=[hk, "idb"], writes=["pstb"])
            pr.op("scalar", lambda E, i=i, j=j: E.copy(
                xnT[:, j * T + i * 128: j * T + (i + 1) * 128], pstb[:, :]),
                reads=["pstb"], writes=[f"xnT{j}"])

    affN_d = cx.dint("affN_d", [T, E8])
    idf = prf
    pr.op("sync", lambda E: E.dma_start(out=idf[:, :], in_=cx.t["identf1"][:, :]),
          writes=["prf"], dma=True)
    pr.op("gpsimd", lambda E: E.dma_start(
        out=wcb[:, 0:8 * E8].rearrange("p (a q) -> p a q", q=E8),
        in_=gatewt[0:D, :].rearrange("(a p) q -> p a q", p=128)),
        writes=["wcb"], dma=True)
    for i in range(NT):
        for k in range(8):
            pr.op("tensor", lambda E, i=i, k=k: E.matmul(
                ps[0][:, 0:E8], xnT[:, k * T + i * 128: k * T + (i + 1) * 128],
                wcb[:, k * E8:(k + 1) * E8], start=(k == 0), stop=(k == 7)),
                reads=[f"xnT{k}", "wcb"], writes=["ps0"] if k == 7 else [],
                war=["ps0"] if k == 0 else ())
        pr.op("vector", lambda E: E.reduce_max(cols[:, 0:1], ps[0][:, 0:E8], axis=AXX),
              reads=["ps0"], writes=["cols"])
        pr.op("vector", lambda E: E.tensor_scalar(aff[:, :], ps[0][:, 0:E8], cols[:, 0:1],
                                                  None, ALU.subtract),
              reads=["ps0", "cols"], writes=["aff"])
        pr.op("scalar", lambda E: E.activation(aff[:, :], aff[:, :], AF.Exp),
              reads=["aff"], writes=["aff"])
        pr.op("vector", lambda E: E.reduce_sum(cols[:, 1:2], aff[:, :], axis=AXX),
              reads=["aff"], writes=["colsS"])
        pr.op("vector", lambda E: E.reciprocal(cols[:, 1:2], cols[:, 1:2]),
              reads=["colsS"], writes=["colsS"])
        pr.op("vector", lambda E: E.tensor_scalar(aff[:, :], aff[:, :], cols[:, 1:2],
                                                  None, ALU.mult),
              reads=["aff", "colsS"], writes=["aff"])
        pr.op("gpsimd", lambda E, i=i: E.dma_start(out=affN_d[i * 128:(i + 1) * 128, :], in_=aff[:, :]),
              reads=["aff"], writes=[f"affN{i}"], dma=True)
        pr.op("tensor", lambda E, i=i: E.transpose(ps[1][0:E8, 0:128], aff[:, 0:E8],
                                                   idf[:, :]),
              reads=["aff", "prf"], writes=["ps1"])
        pr.op("vector", lambda E, i=i: E.tensor_copy(affT[:, i * 128:(i + 1) * 128],
                                                     ps[1][0:E8, 0:128]),
              reads=["ps1"], writes=["affT"])
    pr.op("gpsimd", lambda E: E.memset(lo8[:, 0:1], 0.0), writes=["lo8"])
    pr.op("gpsimd", lambda E: E.memset(lo8[:, 1:2], 1.0), writes=["lo8"])
    for it in range(36):
        pr.op("vector", lambda E: E.tensor_add(lo8[:, 2:3], lo8[:, 0:1], lo8[:, 1:2]),
              reads=["lo8"], writes=["lo8"])
        pr.op("vector", lambda E: E.tensor_scalar(lo8[:, 2:3], lo8[:, 2:3], 0.5, None, ALU.mult),
              reads=["lo8"], writes=["lo8"])
        pr.op("vector", lambda E: E.tensor_scalar(m8[:, :], affT[:, :], lo8[:, 2:3], None,
                                                  ALU.is_ge),
              reads=["affT", "lo8"], writes=["m8"])
        pr.op("vector", lambda E: E.reduce_sum(lo8[:, 3:4], m8[:, :], axis=AXX),
              reads=["m8"], writes=["lo8"])
        pr.op("vector", lambda E: E.tensor_scalar(c64[0:E8, 0:1], lo8[:, 3:4], float(KSEL),
                                                  None, ALU.is_ge),
              reads=["lo8"], writes=["c64"])
        pr.op("vector", lambda E: E.tensor_sub(c64[0:E8, 1:2], lo8[:, 2:3], lo8[:, 0:1]),
              reads=["lo8"], writes=["c64"])
        pr.op("vector", lambda E: E.tensor_mul(c64[0:E8, 1:2], c64[0:E8, 1:2], c64[0:E8, 0:1]),
              reads=["c64"], writes=["c64"])
        pr.op("vector", lambda E: E.tensor_add(lo8[:, 0:1], lo8[:, 0:1], c64[0:E8, 1:2]),
              reads=["c64", "lo8"], writes=["lo8"])
        pr.op("vector", lambda E: E.tensor_sub(c64[0:E8, 1:2], lo8[:, 2:3], lo8[:, 1:2]),
              reads=["lo8"], writes=["c64"])
        pr.op("vector", lambda E: E.tensor_mul(c64[0:E8, 2:3], c64[0:E8, 1:2], c64[0:E8, 0:1]),
              reads=["c64"], writes=["c64"])
        pr.op("vector", lambda E: E.tensor_add(lo8[:, 1:2], lo8[:, 1:2], c64[0:E8, 1:2]),
              reads=["c64", "lo8"], writes=["lo8"])
        pr.op("vector", lambda E: E.tensor_sub(lo8[:, 1:2], lo8[:, 1:2], c64[0:E8, 2:3]),
              reads=["c64", "lo8"], writes=["lo8"])
    pr.op("vector", lambda E: E.tensor_scalar(m8[:, :], affT[:, :], lo8[:, 0:1], None, ALU.is_ge),
          reads=["affT", "lo8"], writes=["m8"])
    for i in range(NT):
        init = 0.0 if i == 0 else r8[:, i * 128 - 1:i * 128]
        pr.op("vector", lambda E, i=i, init=init: E.tensor_tensor_scan(
            r8[:, i * 128:(i + 1) * 128], ones8[:, :], m8[:, i * 128:(i + 1) * 128],
            init, ALU.mult, ALU.add),
            reads=["m8", "ones8", "r8"], writes=["r8"])
    pr.op("vector", lambda E: E.tensor_mul(r8[:, :], r8[:, :], m8[:, :]),
          reads=["r8", "m8"], writes=["r8"])
    pr.op("vector", lambda E: E.tensor_sub(r8[:, :], r8[:, :], m8[:, :]),
          reads=["r8", "m8"], writes=["r8"])
    pr.op("vector", lambda E: E.tensor_scalar(m8[:, :], m8[:, :], -1000000.0, 1000000.0,
                                              ALU.mult, ALU.add),
          reads=["m8"], writes=["m8"])
    pr.op("vector", lambda E: E.tensor_add(r8[:, :], r8[:, :], m8[:, :]),
          reads=["r8", "m8"], writes=["r8"])
    for i in range(NT):
        pr.op("tensor", lambda E, i=i: E.transpose(ps[1][0:128, 0:E8],
                                                   r8[:, i * 128:(i + 1) * 128], idf[0:E8, 0:E8]),
              reads=["r8", "prf"], writes=["ps1"])
        pr.op("vector", lambda E, i=i: E.tensor_copy(rankc[:, i * 8:(i + 1) * 8], ps[1][0:128, 0:E8]),
              reads=["ps1"], writes=["rankc"])
    pr.op("scalar", lambda E: E.copy(rankI[:, :], rankc[:, :]),
          reads=["rankc"], writes=["rankI"])
    for i in range(NT):
        pr.op("sync", lambda E, i=i: E.dma_start(out=xnb2[:, 0:D],
                                                 in_=xnb_d[i * 128:(i + 1) * 128, :]),
              reads=[f"xnbd{i}"], writes=["xnb2h0"], dma=True)
        pr.op("sync", lambda E, i=i: E.dma_start(out=aff[:, :],
                                                 in_=affN_d[i * 128:(i + 1) * 128, :]),
              reads=[f"affN{i}"], writes=["aff"], dma=True)
        for e in range(E8):
            pr.op("vector", lambda E: E.memset(xnb2[:, D:D + 32], 1.0),
                  reads=[], writes=["xnb2g"])
            pr.op("vector", lambda E, e=e: E.tensor_scalar(
                xnb2[:, D:D + 32], xnb2[:, D:D + 32], aff[:, e:e + 1], None, ALU.mult),
                reads=["aff", "xnb2g"], writes=["xnb2g"])
            pr.op("vector", lambda E, e=e, i=i: E.tensor_copy(
                rkcol[:, :], rankI[:, i * 8 + e:i * 8 + e + 1]),
                reads=["rankI"], writes=["rkcol"])
            pr.op("gpsimd", lambda E, e=e, i=i: E.indirect_dma_start(
                out=xin_es[e][:, :],
                out_offset=IndirectOffsetOnAxis(rkcol[:, 0:1], axis=0),
                in_=xnb2[:, 0:D + 32], in_offset=None,
                bounds_check=bndreg, oob_is_err=False),
                reads=["rkcol", "xnb2h0", "xnb2g"], writes=[f"xin_e{e}"], dma=True)

    SC4 = 0.35355339059327373
    if x16 is None:
        cc_in = cx.dint("cc_in", [16, D])
        cc_out = cx.dint("cc_out", [128, D])
        pr.op("gpsimd", lambda E: E.dma_start(out=cc_in[:, :], in_=xin[0:16, :]),
              writes=["ccin"], dma=True)
        collsem = pr.sem["coll"]
        pr._wait("gpsimd", pr.reg.get("ccin")[0][0], pr.reg["ccin"][0][1])
        pr.q["gpsimd"].append(lambda E: E.collective_compute(
            "AllGather", ALU.bypass, replica_groups=[[0, 1, 2, 3, 4, 5, 6, 7]],
            ins=[cc_in.ap().opt()], outs=[cc_out.ap().opt()]).then_inc(collsem, 1))
        pr.cnt["coll"] += 1
        pr._wait("sync", "coll", 1)
        pr.op("sync", lambda E: E.dma_start(out=x16s[:, :], in_=cc_out[:, :]),
              writes=["x16s"], dma=True)
    else:
        pr.op("sync", lambda E: E.dma_start(out=x16s[:, :], in_=x16[:, :]),
              writes=["x16s"], dma=True)
    ln_tile(pr, x16s[:, :], ["x16s"], xnf[:, :], "xnf", cols, "cols", sq[:, :], "sq")
    ln_tile(pr, xnf[:, :], ["xnf"], x16s[:, :], "x16s", cols, "cols", sq[:, :], "sq")
    pr.op("vector", lambda E: E.tensor_mul(x16s[:, :], x16s[:, :], lnab[:, :]),
          reads=["x16s", "lnab"], writes=["x16s"])
    pr.op("scalar", lambda E: E.copy(xnb2[:, 0:D], x16s[:, :]),
          reads=["x16s"], writes=["xnb2h0"])
    for j in range(ND):
        pr.op("tensor", lambda E, j=j: E.transpose(pstb[:, :], xnb2[:, j * 128:(j + 1) * 128],
                                                   idb[:, :]),
              reads=["xnb2h0", "idb"], writes=["pstb"])
        pr.op("scalar", lambda E, j=j: E.copy(wcb[:, j * 128:(j + 1) * 128], pstb[:, :]),
              reads=["pstb"], writes=["wcb"])
    for (wt, dst, dkey, mode) in [(wq_bf, qn, "qn", "q"), (wk_bf, kn, "kn", "k"),
                                  (wv_bf, vn, "vn", "v")]:
        for c in range(2):
            for k in range(8):
                pr.op("sync", lambda E, wt=wt, k=k, c=c: E.dma_start(
                    out=wrb[:, c * 512:(c + 1) * 512],
                    in_=wt[k * 128:(k + 1) * 128, c * 512:(c + 1) * 512]),
                    reads=["c" + {"q": "wq", "k": "wk", "v": "wv"}[mode] + "done"],
                    writes=["wrb"], dma=True)
                pr.op("tensor", lambda E, k=k, c=c: E.matmul(
                    ps[0][:, :], wcb[:, k * 128:(k + 1) * 128], wrb[:, c * 512:(c + 1) * 512],
                    start=(k == 0), stop=(k == 7)),
                    reads=["wcb", "wrb"], writes=["ps0"] if k == 7 else [],
                    war=["ps0"] if k == 0 else ())
            if mode == "q":
                pr.op("vector", lambda E, c=c: E.tensor_add(xnf[:, c * 512:(c + 1) * 512],
                                                            ps[0][:, :], bqb[:, c * 512:(c + 1) * 512]),
                      reads=["ps0", "bqb"], writes=["xnf"])
                pr.op("vector", lambda E, c=c, dst=dst: E.tensor_scalar(
                    dst[:, c * 512:(c + 1) * 512], xnf[:, c * 512:(c + 1) * 512], SC4, None, ALU.mult),
                    reads=["xnf"], writes=[dkey])
            elif mode == "k":
                pr.op("vector", lambda E, c=c, dst=dst: E.tensor_scalar(
                    dst[:, c * 512:(c + 1) * 512], ps[0][:, :], SC4, None, ALU.mult),
                    reads=["ps0"], writes=[dkey])
            else:
                pr.op("vector", lambda E, c=c, dst=dst: E.tensor_add(
                    dst[:, c * 512:(c + 1) * 512], ps[0][:, :], bvb[:, c * 512:(c + 1) * 512]),
                    reads=["ps0", "bvb"], writes=[dkey])
    pr.op("vector", lambda E: E.tensor_copy(qcur[:, :], qn[:, :]), reads=["qn"], writes=["qcur"])
    pr.op("vector", lambda E: E.tensor_copy(kcur[:, :], kn[:, :]), reads=["kn"], writes=["kcur"])
    pr.op("vector", lambda E: E.tensor_copy(vcur[:, :], vn[:, :]), reads=["vn"], writes=["vcur"])
    kcur2 = kn; vcur2 = vn
    ones128 = cx.sb("ones128", [128, 1])
    tsc4 = lo8
    pr.op("gpsimd", lambda E: E.memset(ones128[:, :], 1.0), writes=["ones128"])
    ai_bufs = [ai0, ai1, ai2]
    for it in range(3):
        kin, kink = (kcur, "kcur") if it % 2 == 0 else (kcur2, "kn")
        kout, koutk = (kcur2, "kn") if it % 2 == 0 else (kcur, "kcur")
        vin, vink = (vcur, "vcur") if it % 2 == 0 else (vcur2, "vn")
        vout, voutk = (vcur2, "vn") if it % 2 == 0 else (vcur, "vcur")
        aib = ai_bufs[it]
        aik = f"ai{it}"
        pr.op("vector", lambda E, it=it: E.tensor_scalar(tsc4[0:1, 0:1], scas[:, 0:1],
                                                         0.005 * it, None, ALU.add),
              reads=["scas"], writes=["lo8"])
        pr.op("vector", lambda E: E.tensor_scalar(tsc4[0:1, 1:2], tsc4[0:1, 0:1], 1.0, None,
                                                  ALU.not_equal),
              reads=["lo8"], writes=["lo8"])
        pr.op("vector", lambda E: E.tensor_scalar(tsc4[0:1, 2:3], tsc4[0:1, 0:1], 0.0, None,
                                                  ALU.is_gt),
              reads=["lo8"], writes=["lo8"])
        pr.op("vector", lambda E: E.tensor_mul(tsc4[0:1, 1:2], tsc4[0:1, 1:2], tsc4[0:1, 2:3]),
              reads=["lo8"], writes=["lo8"])
        pr.op("scalar", lambda E: E.activation(tsc4[0:1, 2:3], tsc4[0:1, 0:1], AF.Sqrt),
              reads=["lo8"], writes=["lo8"])
        pr.op("vector", lambda E: E.reciprocal(tsc4[0:1, 2:3], tsc4[0:1, 2:3]),
              reads=["lo8"], writes=["lo8"])
        pr.op("vector", lambda E: E.tensor_scalar(tsc4[0:1, 2:3], tsc4[0:1, 2:3], -1.0, None,
                                                  ALU.add),
              reads=["lo8"], writes=["lo8"])
        pr.op("vector", lambda E: E.tensor_mul(tsc4[0:1, 1:2], tsc4[0:1, 1:2], tsc4[0:1, 2:3]),
              reads=["lo8"], writes=["lo8"])
        pr.op("vector", lambda E: E.tensor_scalar(tsc4[0:1, 1:2], tsc4[0:1, 1:2], 1.0, None,
                                                  ALU.add),
              reads=["lo8"], writes=["lo8"])
        pr.op("tensor", lambda E: E.matmul(ps[1][:, 0:1], onesc[:, :], tsc4[0:1, 1:2],
                                           start=True, stop=True),
              reads=["onesc", "lo8"], writes=["ps1"])
        pr.op("vector", lambda E: E.tensor_copy(scc[:, 0:1], ps[1][:, 0:1]),
              reads=["ps1"], writes=["scc"])
        for h in range(16):
            hsl = slice(h * 64, (h + 1) * 64)
            pr.op("tensor", lambda E, hsl=hsl, vin=vin: E.transpose(
                pstb[0:64, 0:128], vin[:, hsl], idb[:, :]),
                reads=[vink, "idb"], writes=["pstb"])
            pr.op("vector", lambda E: E.tensor_copy(tq2[0:64, 0:128], pstb[0:64, 0:128]),
                  reads=["pstb"], writes=["tq2"])
            pr.op("tensor", lambda E: E.matmul(ps[2][0:64, 0:128], wl3[:, 128:192], tq2[0:64, 0:128],
                                               start=True, stop=True),
                  reads=["wl3", "tq2"], writes=["ps2"])
            pr.op("vector", lambda E: E.tensor_copy(tq2[0:64, 0:128], ps[2][0:64, 0:128]),
                  reads=["ps2"], writes=["tq2"])
            pr.op("tensor", lambda E: E.transpose(pstb[:, 0:64], tq2[0:64, 0:128], idb[0:64, 0:64]),
                  reads=["tq2", "idb"], writes=["pstb"])
            pr.op("vector", lambda E, hsl=hsl, vout=vout: E.tensor_add(
                vout[:, hsl], pstb[:, 0:64], blvb[:, :]),
                reads=["pstb", "blvb"], writes=[voutk])
            pr.op("tensor", lambda E, hsl=hsl: E.transpose(
                pstb[0:64, 0:128], qcur[:, hsl], idb[:, :]),
                reads=["qcur", "idb"], writes=["pstb"])
            pr.op("vector", lambda E: E.tensor_copy(tq[:, :], pstb[:, 0:64].transpose([0, 1]) if False else pstb[0:64, 0:128]),
                  reads=["pstb"], writes=["tq"]) if False else None
            pr.op("vector", lambda E: E.tensor_copy(tq[0:64, 0:128], pstb[0:64, 0:128]),
                  reads=["pstb"], writes=["tq"])
            pr.op("tensor", lambda E: E.matmul(ps[2][0:64, 0:128], wl3[:, 0:64], tq[0:64, 0:128],
                                               start=True, stop=True),
                  reads=["wl3", "tq"], writes=["ps2"])
            pr.op("vector", lambda E: E.tensor_copy(lfb[0:64, 0:128] if False else tq2[0:64, 0:128], ps[2][0:64, 0:128]),
                  reads=["ps2"], writes=["tq2"])
            pr.op("tensor", lambda E: E.transpose(pstb[:, 0:64], tq2[0:64, 0:128], idb[0:64, 0:64]),
                  reads=["tq2", "idb"], writes=["pstb"])
            pr.op("vector", lambda E: E.tensor_add(lf[:, :], pstb[:, 0:64], blqb[:, :]),
                  reads=["pstb", "blqb"], writes=["lf"])
            pr.op("vector", lambda E: E.reduce_sum(c64[:, 0:1], lf[:, :], axis=AXX),
                  reads=["lf"], writes=["c64b"])
            pr.op("scalar", lambda E: E.mul(c64[:, 0:1], c64[:, 0:1], 1.0 / 64),
                  reads=["c64b"], writes=["c64b"])
            pr.op("vector", lambda E: E.tensor_scalar(lf[:, :], lf[:, :], c64[:, 0:1], None,
                                                      ALU.subtract),
                  reads=["lf", "c64b"], writes=["lf"])
            pr.op("scalar", lambda E: E.activation(lf2[:, :], lf[:, :], AF.Square),
                  reads=["lf"], writes=["lf2"])
            pr.op("vector", lambda E: E.reduce_sum(c64[:, 1:2], lf2[:, :], axis=AXX),
                  reads=["lf2"], writes=["c64b"])
            pr.op("vector", lambda E: E.tensor_scalar(c64[:, 1:2], c64[:, 1:2], 1.0 / 64, 1e-5,
                                                      ALU.mult, ALU.add),
                  reads=["c64b"], writes=["c64b"])
            pr.op("scalar", lambda E: E.activation(c64[:, 1:2], c64[:, 1:2], AF.Sqrt),
                  reads=["c64b"], writes=["c64b"])
            pr.op("vector", lambda E: E.reciprocal(c64[:, 1:2], c64[:, 1:2]),
                  reads=["c64b"], writes=["c64b"])
            pr.op("vector", lambda E: E.tensor_scalar(lf[:, :], lf[:, :], c64[:, 1:2], None,
                                                      ALU.mult),
                  reads=["lf", "c64b"], writes=["lf"])
            pr.op("vector", lambda E: E.tensor_mul(lf[:, :], lf[:, :], lncb[:, :]),
                  reads=["lf", "lncb"], writes=["lf"])
            pr.op("vector", lambda E: E.tensor_scalar(lf[:, :], lf[:, :], scc[:, 0:1], None,
                                                      ALU.mult),
                  reads=["lf", "scc"], writes=["lf"])
            pr.op("vector", lambda E: E.tensor_copy(lfb[:, :], lf[:, :]),
                  reads=["lf"], writes=["lfb"])
            pr.op("tensor", lambda E: E.transpose(pstb[0:64, 0:128], lfb[:, :], idb[:, :]),
                  reads=["lfb", "idb"], writes=["pstb"])
            pr.op("vector", lambda E: E.tensor_copy(tq[0:64, 0:128], pstb[0:64, 0:128]),
                  reads=["pstb"], writes=["tq"])
            pr.op("tensor", lambda E, hsl=hsl, kin=kin: E.transpose(
                pstb[0:64, 0:128], kin[:, hsl], idb[:, :]),
                reads=[kink, "idb"], writes=["pstb"])
            pr.op("vector", lambda E: E.tensor_copy(tq2[0:64, 0:128], pstb[0:64, 0:128]),
                  reads=["pstb"], writes=["tq2"])
            pr.op("tensor", lambda E: E.matmul(ps[2][0:64, 0:128], wl3[:, 64:128], tq2[0:64, 0:128],
                                               start=True, stop=True),
                  reads=["wl3", "tq2"], writes=["ps2"])
            pr.op("vector", lambda E: E.tensor_copy(tq2[0:64, 0:128], ps[2][0:64, 0:128]),
                  reads=["ps2"], writes=["tq2"])
            pr.op("tensor", lambda E: E.transpose(pstb[:, 0:64], tq2[0:64, 0:128], idb[0:64, 0:64]),
                  reads=["tq2", "idb"], writes=["pstb"])
            pr.op("vector", lambda E: E.tensor_add(lf[:, :], pstb[:, 0:64], blkb[:, :]),
                  reads=["pstb", "blkb"], writes=["lf"])
            pr.op("vector", lambda E, hsl=hsl, kout=kout: E.tensor_copy(kout[:, hsl], lf[:, :]),
                  reads=["lf"], writes=[koutk])
            pr.op("vector", lambda E: E.reduce_sum(c64[:, 0:1], lf[:, :], axis=AXX),
                  reads=["lf"], writes=["c64b"])
            pr.op("scalar", lambda E: E.mul(c64[:, 0:1], c64[:, 0:1], 1.0 / 64),
                  reads=["c64b"], writes=["c64b"])
            pr.op("vector", lambda E: E.tensor_scalar(lf[:, :], lf[:, :], c64[:, 0:1], None,
                                                      ALU.subtract),
                  reads=["lf", "c64b"], writes=["lf"])
            pr.op("scalar", lambda E: E.activation(lf2[:, :], lf[:, :], AF.Square),
                  reads=["lf"], writes=["lf2"])
            pr.op("vector", lambda E: E.reduce_sum(c64[:, 1:2], lf2[:, :], axis=AXX),
                  reads=["lf2"], writes=["c64b"])
            pr.op("vector", lambda E: E.tensor_scalar(c64[:, 1:2], c64[:, 1:2], 1.0 / 64, 1e-5,
                                                      ALU.mult, ALU.add),
                  reads=["c64b"], writes=["c64b"])
            pr.op("scalar", lambda E: E.activation(c64[:, 1:2], c64[:, 1:2], AF.Sqrt),
                  reads=["c64b"], writes=["c64b"])
            pr.op("vector", lambda E: E.reciprocal(c64[:, 1:2], c64[:, 1:2]),
                  reads=["c64b"], writes=["c64b"])
            pr.op("vector", lambda E: E.tensor_scalar(lf[:, :], lf[:, :], c64[:, 1:2], None,
                                                      ALU.mult),
                  reads=["lf", "c64b"], writes=["lf"])
            pr.op("vector", lambda E: E.tensor_mul(lf[:, :], lf[:, :], lndb[:, :]),
                  reads=["lf", "lndb"], writes=["lf"])
            pr.op("vector", lambda E: E.tensor_copy(lfb[:, :], lf[:, :]),
                  reads=["lf"], writes=["lfb"])
            pr.op("tensor", lambda E: E.transpose(pstb[0:64, 0:128], lfb[:, :], idb[:, :]),
                  reads=["lfb", "idb"], writes=["pstb"])
            pr.op("vector", lambda E: E.tensor_copy(tq2[0:64, 0:128], pstb[0:64, 0:128]),
                  reads=["pstb"], writes=["tq2"])
            pr.op("tensor", lambda E: E.matmul(ps[3][:, 0:128], tq[0:64, 0:128], tq2[0:64, 0:128],
                                               start=True, stop=True),
                  reads=["tq", "tq2"], writes=["ps3"])
            pr.op("vector", lambda E: E.tensor_scalar(prf[:, :], ps[3][:, 0:128], 0.125, None,
                                                      ALU.mult),
                  reads=["ps3"], writes=["prf"])
            pr.op("vector", lambda E: E.tensor_add(prf[:, :], prf[:, :], mbd[:, :]),
                  reads=["prf", "mbd"], writes=["prf"])
            pr.op("vector", lambda E: E.reduce_max(c64[:, 2:3], prf[:, :], axis=AXX),
                  reads=["prf"], writes=["c64c"])
            pr.op("vector", lambda E: E.tensor_scalar(prf[:, :], prf[:, :], c64[:, 2:3], None,
                                                      ALU.subtract),
                  reads=["prf", "c64c"], writes=["prf"])
            pr.op("scalar", lambda E: E.activation(prf[:, :], prf[:, :], AF.Exp),
                  reads=["prf"], writes=["prf"])
            pr.op("vector", lambda E: E.reduce_sum(c64[:, 3:4], prf[:, :], axis=AXX),
                  reads=["prf"], writes=["c64c"])
            pr.op("vector", lambda E: E.reciprocal(c64[:, 3:4], c64[:, 3:4]),
                  reads=["c64c"], writes=["c64c"])
            pr.op("vector", lambda E: E.tensor_scalar(prb[:, :], prf[:, :], c64[:, 3:4], None,
                                                      ALU.mult),
                  reads=["prf", "c64c"], writes=["prb"])
            pr.op("tensor", lambda E: E.transpose(pstb[:, :], prb[:, :], idb[:, :]),
                  reads=["prb", "idb"], writes=["pstb"])
            pr.op("vector", lambda E: E.tensor_copy(prb[:, :], pstb[:, :]),
                  reads=["pstb"], writes=["prb"])
            pr.op("tensor", lambda E, hsl=hsl, vout=vout: E.matmul(
                ps[4][:, 0:64], prb[:, :], vout[:, hsl], start=True, stop=True),
                reads=["prb", voutk], writes=["ps4"])
            pr.op("vector", lambda E, hsl=hsl, aib=aib: E.tensor_copy(aib[:, hsl], ps[4][:, 0:64]),
                  reads=["ps4"], writes=[aik])
            if it == 1:
                pr.op("vector", lambda E, hsl=hsl: E.tensor_sub(lf[:, :], ai1[:, hsl], ai0[:, hsl]),
                      reads=["ai1", "ai0"], writes=["lf"])
                pr.op("scalar", lambda E: E.activation(lf[:, :], lf[:, :], AF.Abs),
                      reads=["lf"], writes=["lf"])
                pr.op("vector", lambda E, h=h: E.reduce_sum(dcol[:, h:h + 1], lf[:, :], axis=AXX),
                      reads=["lf"], writes=["dcol"])
        pr.op("vector", lambda E, aib=aib: E.tensor_add(qcur[:, :], qcur[:, :], aib[:, :]),
              reads=["qcur", aik], writes=["qcur"])
        if it == 1:
            pr.op("tensor", lambda E: E.matmul(ps[1][0:1, 0:16], ones128[:, :], dcol[:, :],
                                               start=True, stop=True),
                  reads=["ones128", "dcol"], writes=["ps1"])
            pr.op("vector", lambda E: E.tensor_copy(c64[0:1, 0:1] if False else scc[0:1, 2:3],
                                                    ps[1][0:1, 0:1]),
                  reads=["ps1"], writes=["sccd"]) if False else None
            pr.op("vector", lambda E: E.reduce_sum(scc[0:1, 2:3], ps[1][0:1, 0:16], axis=AXX),
                  reads=["ps1"], writes=["sccd"])
            pr.op("vector", lambda E: E.tensor_scalar(scc[0:1, 2:3], scc[0:1, 2:3],
                                                      1.0 / (8 * 16 * 4096 * 64), None, ALU.mult),
                  reads=["sccd"], writes=["sccd"])
            pr.op("vector", lambda E: E.tensor_mul(scc[0:1, 3:4], scc[0:1, 2:3], scas[:, 2:3]),
                  reads=["sccd", "scas"], writes=["sccd"])
            pr.op("vector", lambda E: E.tensor_add(scc[0:1, 3:4], scc[0:1, 3:4], scas[:, 1:2]),
                  reads=["sccd", "scas"], writes=["sccd"])
            pr.op("vector", lambda E: E.tensor_tensor(scc[0:1, 2:3], scc[0:1, 2:3], scc[0:1, 3:4],
                                                      ALU.is_lt),
                  reads=["sccd"], writes=["sccd"])
            pr.op("tensor", lambda E: E.matmul(ps[1][:, 0:1], onesc[:, :], scc[0:1, 2:3],
                                               start=True, stop=True),
                  reads=["onesc", "sccd"], writes=["ps1"])
            pr.op("vector", lambda E: E.tensor_copy(scc[:, 1:2], ps[1][:, 0:1]),
                  reads=["ps1"], writes=["scc"])
    pr.op("vector", lambda E: E.tensor_sub(qn[:, :], ai1[:, :], ai2[:, :]),
          reads=["ai1", "ai2"], writes=["qn"])
    pr.op("vector", lambda E: E.tensor_scalar(qn[:, :], qn[:, :], scc[:, 1:2], None, ALU.mult),
          reads=["qn", "scc"], writes=["qn"])
    pr.op("vector", lambda E: E.tensor_add(ai2[:, :], ai2[:, :], qn[:, :]),
          reads=["ai2", "qn"], writes=["ai2"])
    for j in range(ND):
        pr.op("tensor", lambda E, j=j: E.transpose(pstb[:, :], ai2[:, j * 128:(j + 1) * 128],
                                                   idb[:, :]),
              reads=["ai2", "idb"], writes=["pstb"])
        pr.op("scalar", lambda E, j=j: E.copy(wcb[:, j * 128:(j + 1) * 128], pstb[:, :]),
              reads=["pstb"], writes=["wcb"])
    for c in range(2):
        for k in range(8):
            pr.op("sync", lambda E, k=k, c=c: E.dma_start(
                out=wrb[:, c * 512:(c + 1) * 512],
                in_=wo_bf[k * 128:(k + 1) * 128, c * 512:(c + 1) * 512]),
                reads=["cwodone"], writes=["wrb"], dma=True)
            pr.op("tensor", lambda E, k=k, c=c: E.matmul(
                ps[0][:, :], wcb[:, k * 128:(k + 1) * 128], wrb[:, c * 512:(c + 1) * 512],
                start=(k == 0), stop=(k == 7)),
                reads=["wcb", "wrb"], writes=["ps0"] if k == 7 else [],
                war=["ps0"] if k == 0 else ())
        pr.op("vector", lambda E, c=c: E.tensor_add(x16s[:, c * 512:(c + 1) * 512],
                                                    ps[0][:, :], bob[:, c * 512:(c + 1) * 512]),
              reads=["ps0", "bob"], writes=["x16s"])
    pr.op("sync", lambda E: E.dma_start(out=s16_d[:, :], in_=x16s[:, :]),
          reads=["x16s"], writes=["s16d"], dma=True)

    for e in range(E8):
        for a in range(8):
            pr.op("sync", lambda E, e=e, a=a: E.dma_start(
                out=xnb2[:, 0:DG], in_=xin_es[e][a * 128:(a + 1) * 128, :]),
                reads=[f"xin_e{e}"], writes=["xnb2h0", "xnb2g"], dma=True)
            pr.op("vector", lambda E, e=e, a=a: E.tensor_copy(
                gatc[:, e * 8 + a: e * 8 + a + 1], xnb2[:, D:D + 1]),
                reads=["xnb2g"], writes=["gatc"])
            for j in range(ND):
                pr.op("tensor", lambda E, j=j: E.transpose(pstb[:, :], xnb2[:, j * 128:(j + 1) * 128],
                                                           idb[:, :]),
                      reads=["xnb2h0", "idb"], writes=["pstb"])
                pr.op("scalar", lambda E, j=j, a=a: E.copy(
                    xinT[:, j * KSEL + a * 128: j * KSEL + (a + 1) * 128], pstb[:, :]),
                    reads=["pstb"], writes=[f"xinT{j}"])

        for rh in range(2):
            r0 = rh * 512
            for kk in range(32):
                gb, gk = (wcb, "wcb") if kk % 2 == 0 else (ai0, "ai0")
                ub, uk = (wrb, "wrb") if kk % 2 == 0 else (ai1, "ai1")
                pg = "ps0" if kk % 2 == 0 else "ps2"
                pu = "ps1" if kk % 2 == 0 else "ps3"
                pgi = 0 if kk % 2 == 0 else 2
                pui = 1 if kk % 2 == 0 else 3
                pr.op("sync", lambda E, e=e, kk=kk, gb=gb: E.dma_start(
                    out=gb[:, 0:8 * 128].rearrange("p (a q) -> p a q", q=128),
                    in_=wg_bf[e * D:(e + 1) * D, kk * 128:(kk + 1) * 128].rearrange(
                        "(a p) q -> p a q", p=128)),
                    reads=["cwgdone"], writes=[gk], dma=True)
                pr.op("scalar", lambda E, e=e, kk=kk, ub=ub: E.dma_start(
                    out=ub[:, 0:D].rearrange("p (a q) -> p a q", q=128),
                    in_=wu_bf[e * D:(e + 1) * D, kk * 128:(kk + 1) * 128].rearrange(
                        "(a p) q -> p a q", p=128)),
                    reads=["cwudone"], writes=[uk], dma=True)
                for k in range(8):
                    pr.op("tensor", lambda E, k=k, r0=r0, gb=gb, pgi=pgi: E.matmul(
                        ps[pgi][:, :], gb[:, k * 128:(k + 1) * 128],
                        xinT[:, k * KSEL + r0: k * KSEL + r0 + 512],
                        start=(k == 0), stop=(k == 7)),
                        reads=[gk, f"xinT{k}"], writes=[pg] if k == 7 else [],
                        war=[pg] if k == 0 else ())
                    pr.op("tensor", lambda E, k=k, r0=r0, ub=ub, pui=pui: E.matmul(
                        ps[pui][:, :], ub[:, k * 128:(k + 1) * 128],
                        xinT[:, k * KSEL + r0: k * KSEL + r0 + 512],
                        start=(k == 0), stop=(k == 7)),
                        reads=[uk, f"xinT{k}"], writes=[pu] if k == 7 else [],
                        war=[pu] if k == 0 else ())
                pr.op("scalar", lambda E, pgi=pgi: E.activation(yet[:, :], ps[pgi][:, :], AF.Silu),
                      reads=[pg], writes=["yet"])
                pr.op("vector", lambda E, kk=kk, pui=pui: E.tensor_mul(
                    htb[:, kk * 512:(kk + 1) * 512], yet[:, :], ps[pui][:, :]),
                    reads=["yet", pu], writes=[f"htb{kk}"], war=[f"xnT{kk // 8}"])
            for half in range(2):
                for kk in range(32):
                    db_, dk = (wrb, "wrb") if kk % 2 == 0 else (ai2, "ai2")
                    pr.op("sync" if kk % 2 == 0 else "scalar",
                          lambda E, e=e, kk=kk, db_=db_: E.dma_start(
                        out=db_[:, 0:D],
                        in_=wd_bf[e * 4 * D + kk * 128: e * 4 * D + (kk + 1) * 128, :]),
                        reads=["cwddone"], writes=[dk], dma=True)
                    for rr in range(2):
                        rs = 2 * half + rr
                        for c in range(2):
                            pj = 2 + rr * 2 + c
                            pr.op("tensor", lambda E, kk=kk, rs=rs, c=c, pj=pj, db_=db_: E.matmul(
                                ps[pj][:, :], htb[:, kk * 512 + rs * 128: kk * 512 + (rs + 1) * 128],
                                db_[:, c * 512:(c + 1) * 512], start=(kk == 0), stop=(kk == 31)),
                                reads=[dk, f"htb{kk}"], writes=[f"ps{pj}"] if kk == 31 else [],
                                war=[f"ps{pj}"] if kk == 0 else ())
                for rr in range(2):
                    rs = 2 * half + rr
                    for c in range(2):
                        pj = 2 + rr * 2 + c
                        pr.op("vector", lambda E, e=e, rh=rh, rs=rs, pj=pj: E.tensor_scalar(
                            yet[:, :], ps[pj][:, :], gatc[:, e * 8 + rh * 4 + rs: e * 8 + rh * 4 + rs + 1],
                            None, ALU.mult),
                            reads=[f"ps{pj}", "gatc"], writes=["yet"])
                        pr.op("sync", lambda E, e=e, rh=rh, rs=rs, c=c: E.dma_start(
                            out=ye_ds[e][rh * 512 + rs * 128: rh * 512 + (rs + 1) * 128,
                                         c * 512:(c + 1) * 512], in_=yet[:, :]),
                            reads=["yet"], writes=[f"ye{e}"], dma=True)

    for i in range(NT):
        pr.op("vector", lambda E: E.memset(acc[:, :], 0.0), writes=["acc"])
        for e in range(E8):
            pr.op("vector", lambda E, e=e, i=i: E.tensor_copy(
                rkcol[:, :], rankI[:, i * 8 + e:i * 8 + e + 1]),
                reads=["rankI"], writes=["rkcol"])
            pr.op("gpsimd", lambda E, e=e, i=i: E.indirect_dma_start(
                out=acc[:, :], out_offset=None,
                in_=ye_ds[e][:, :],
                in_offset=IndirectOffsetOnAxis(rkcol[:, 0:1], axis=0),
                bounds_check=bndreg, oob_is_err=False, compute_op=ALU.add),
                reads=["rkcol", f"ye{e}", "acc"], writes=["acc"], dma=True)
        pr.op("scalar", lambda E: E.activation(gtile[:, :], acc[:, :], AF.Silu),
              reads=["acc"], writes=["gtile"])
        pr.op("sync", lambda E, i=i: E.dma_start(out=g_d[i * 128:(i + 1) * 128, :], in_=gtile[:, :]),
              reads=["gtile"], writes=[f"g_d{i}"], dma=True)

    for i in range(NT):
        xt, xk = (xa, "xa") if i % 2 == 0 else (xb_, "xb_")
        pr.op("sync", lambda E, i=i, xt=xt: E.dma_start(out=xt[:, :], in_=xin[i * 128:(i + 1) * 128, :]),
              writes=[xk], dma=True)
        pr.op("gpsimd", lambda E, i=i: E.dma_start(out=gtile[:, :], in_=g_d[i * 128:(i + 1) * 128, :]),
              reads=[f"g_d{i}"], writes=["gtile"], dma=True)
        pr.op("vector", lambda E: E.tensor_copy(sgt[:, :], bob[:, :]),
              reads=["bob"], writes=["x16s"])
        if i == 0:
            if x16 is None:
                pid16 = cx.nc.partition_id() * 16
                pr.op("gpsimd", lambda E: E.dma_start(
                    out=sgt[0:16, :], in_=s16_d[bass.ds(pid16, 16), :]),
                    reads=["s16d"], writes=["x16s"], dma=True)
            else:
                pr.op("scalar", lambda E: E.dma_start(out=sgt[0:16, :], in_=s16_d[0:16, :]),
                      reads=["s16d"], writes=["x16s"], dma=True)
        pr.op("vector", lambda E: E.tensor_mul(sgt[:, :], sgt[:, :], gtile[:, :]),
              reads=["x16s", "gtile"], writes=["x16s"])
        pr.op("vector", lambda E, xt=xt: E.tensor_add(sgt[:, :], sgt[:, :], xt[:, :]),
              reads=["x16s", xk], writes=["x16s"])
        pr.op("sync", lambda E, i=i: E.dma_start(out=xout[i * 128:(i + 1) * 128, :], in_=sgt[:, :]),
              reads=["x16s"], writes=[f"xout{i}"], dma=True)
    return cx


def _host_consts_l1(inputs, x16_rot):
    def bc(v):
        v = np.asarray(v, np.float32)
        return np.ascontiguousarray(np.broadcast_to(v[None, :], (128, v.shape[0])))
    s_i = np.arange(128)[:, None]; t_i = np.arange(128)[None, :]
    blockdiag = (s_i // 16 == t_i // 16)
    return {
        "x16": np.ascontiguousarray(x16_rot.astype(np.float32)),
        "wqt": np.ascontiguousarray(np.asarray(inputs["L1_attn_Wq"], np.float32).T),
        "wkt": np.ascontiguousarray(np.asarray(inputs["L1_attn_Wk"], np.float32).T),
        "wvt": np.ascontiguousarray(np.asarray(inputs["L1_attn_Wv"], np.float32).T),
        "wot": np.ascontiguousarray(np.asarray(inputs["L1_attn_Wo"], np.float32).T),
        "wlqt": np.ascontiguousarray(np.asarray(inputs["L1_attn_Wlq"], np.float32).T),
        "wlkt": np.ascontiguousarray(np.asarray(inputs["L1_attn_Wlk"], np.float32).T),
        "wlvt": np.ascontiguousarray(np.asarray(inputs["L1_attn_Wlv"], np.float32).T),
        "bq_d": bc(inputs["L1_attn_bq"]), "bv_d": bc(inputs["L1_attn_bv"]),
        "bo_d": bc(inputs["L1_attn_bo"]),
        "lna_d": bc(inputs["L1_attn_lna_w"]),
        "lnc_d": bc(inputs["L1_attn_lnc_w"]), "lnd_d": bc(inputs["L1_attn_lnd_w"]),
        "blq_d": bc(inputs["L1_attn_blq"]), "blk_d": bc(inputs["L1_attn_blk"]),
        "blv_d": bc(inputs["L1_attn_blv"]),
        "sca_d": np.array([[float(inputs["L1_attn_temp"]), float(inputs["L1_attn_threshold"]),
                            float(inputs["L1_attn_factor"]), 0.0]], np.float32),
        "maskbd": np.where(blockdiag, 0.0, -1e30).astype(np.float32),
        "identb1": np.eye(128, dtype=BFNP),
        "identf1": np.eye(128, dtype=np.float32),
        "gatewt": np.ascontiguousarray(np.asarray(inputs["L1_ffn_gate_w"], np.float32).T),
        "wg_t": np.ascontiguousarray(np.asarray(inputs["L1_ffn_w_gate"], np.float32).transpose(0, 2, 1).reshape(8 * D, 4 * D)),
        "wu_t": np.ascontiguousarray(np.asarray(inputs["L1_ffn_w_up"], np.float32).transpose(0, 2, 1).reshape(8 * D, 4 * D)),
        "wd_t": np.ascontiguousarray(np.asarray(inputs["L1_ffn_w_down"], np.float32).transpose(0, 2, 1).reshape(8 * 4 * D, D)),
    }


# version 07
# speedup vs baseline: 1.5967x; 1.0030x over previous
import numpy as np
import ml_dtypes
from contextlib import ExitStack

import concourse.bass as bass
import concourse.mybir as mybir
from concourse.bass import IndirectOffsetOnAxis
from concourse.bass_utils import run_bass_kernel_spmd

F32 = mybir.dt.float32
BF16 = mybir.dt.bfloat16
I32 = mybir.dt.int32
AF = mybir.ActivationFunctionType
ALU = mybir.AluOpType
AXX = mybir.AxisListType.X
BFNP = ml_dtypes.bfloat16

T, D, NT, ND = 4096, 1024, 32, 8
ENGS = ("sync", "scalar", "vector", "tensor", "gpsimd")


class Prog:
    def __init__(self, nc, es):
        self.nc = nc
        self.sem = {}
        for e in ENGS:
            self.sem[e] = es.enter_context(nc.semaphore("S" + e))
        self.NQ = 6
        for e in ("qsync", "qscalar", "qgpsimd"):
            for j in range(self.NQ):
                self.sem[f"{e}{j}"] = es.enter_context(nc.semaphore(f"S{e}{j}"))
        self.rot = {"qsync": 0, "qscalar": 0, "qgpsimd": 0}
        self.sem["coll"] = es.enter_context(nc.semaphore("Scoll"))
        self.cnt = {k: 0 for k in self.sem}
        self.q = {e: [] for e in ENGS}
        self.water = {}
        self.reg = {}
        self.pend = {e: [] for e in ENGS}

    def _wait(self, eng, semkey, val):
        if not val:
            return
        if self.water.get((eng, semkey), 0) >= val:
            return
        self.water[(eng, semkey)] = val
        s = self.sem[semkey]
        self.q[eng].append(lambda E, s=s, v=val: E.wait_ge(s, v))

    def op(self, eng, fn, reads=(), writes=(), war=(), dma=False, inc=True):
        if dma:
            qu = "q" + eng
            semkey = f"{qu}{self.rot[qu]}"
            self.rot[qu] = (self.rot[qu] + 1) % self.NQ
            self._wait(eng, semkey, self.cnt[semkey])
        else:
            semkey = eng
        for r in reads:
            rg = self.reg.get(r)
            if rg and rg[0]:
                self._wait(eng, rg[0][0], rg[0][1])
        for w in tuple(writes) + tuple(war):
            rg = self.reg.get(w)
            if rg:
                if rg[0]:
                    self._wait(eng, rg[0][0], rg[0][1])
                for sk, v in rg[1].items():
                    self._wait(eng, sk, v)
        if inc:
            amt = 16 if dma else 1
            self.cnt[semkey] += amt
            val = self.cnt[semkey]
            s = self.sem[semkey]
            self.q[eng].append(lambda E, f=fn, s=s, a=amt: f(E).then_inc(s, a))
            tok = (semkey, val)
            for r in self.pend[eng] + list(reads):
                rg = self.reg.setdefault(r, [None, {}])
                rg[1][semkey] = val
            self.pend[eng] = []
            for w in writes:
                self.reg[w] = [tok, {}]
            return tok
        else:
            self.q[eng].append(lambda E, f=fn: f(E))
            self.pend[eng] += list(reads)
            assert not writes, "inc=False ops cannot own writes"
            return None

    def barrier(self):
        for e in ENGS:
            for k in list(self.cnt):
                self._wait(e, k, self.cnt[k])

    def finalize(self):
        for k in list(self.cnt):
            self._wait("sync", k, self.cnt[k])
        with self.nc.Block() as block:
            for e in ENGS:
                ops = self.q[e]
                if not ops:
                    continue
                def body(E, ops=ops):
                    for f in ops:
                        f(E)
                getattr(block, e)(body)


class Ctx:
    def __init__(self, enable_partition_id=False):
        self.es = ExitStack()
        self.nc = bass.Bass("TRN2", target_bir_lowering=False,
                            enable_partition_id=enable_partition_id)
        self.pr = Prog(self.nc, self.es)
        self.t = {}
        self.pfx = ""
        self.cur_es = self.es

    def scope(self):
        from contextlib import contextmanager
        @contextmanager
        def _s():
            es = ExitStack()
            old = self.cur_es
            self.cur_es = es
            try:
                yield
            finally:
                es.close()
                self.cur_es = old
        return _s()

    def sb(self, name, shape, dtype=F32):
        h = self.cur_es.enter_context(self.nc.sbuf_tensor(self.pfx + name, shape, dtype))
        self.t[name] = h
        return h

    def ps(self, name, shape, dtype=F32):
        h = self.cur_es.enter_context(self.nc.psum_tensor(self.pfx + name, shape, dtype))
        self.t[name] = h
        return h

    def din(self, name, shape, dtype=F32):
        h = self.nc.dram_tensor(self.pfx + name, shape, dtype, kind="ExternalInput")
        self.t[name] = h
        return h

    def dout(self, name, shape, dtype=F32):
        h = self.nc.dram_tensor(self.pfx + name, shape, dtype, kind="ExternalOutput")
        self.t[name] = h
        return h

    def dint(self, name, shape, dtype=F32):
        h = self.nc.dram_tensor(self.pfx + name, shape, dtype, kind="Internal")
        self.t[name] = h
        return h

    def close(self):
        self.pr.finalize()
        self.es.close()
        return self.nc


def cast_weight(cx, src, dst, rows, cols, bufa, bufb, keya, keyb, tag):
    pr = cx.pr
    for k in range((rows + 127) // 128):
        r0, r1 = k * 128, min(rows, (k + 1) * 128)
        pr.op("gpsimd", lambda E, r0=r0, r1=r1: E.dma_start(
            out=dst[r0:r1, :], in_=src[r0:r1, :]),
            writes=[f"{tag}dst{k}"], dma=True)


def cast_weight2(cx, src, dst, rows, cols, tag, bufs=None):
    pr = cx.pr
    for k in range((rows + 127) // 128):
        pr.op("gpsimd", lambda E, k=k: E.dma_start(
            out=dst[k * 128:(k + 1) * 128, :], in_=src[k * 128:(k + 1) * 128, :]),
            writes=[f"{tag}done"], dma=True)


def ln_tile(pr, x_ap, xreads, out_ap, outkey, cols, colskey, sq, sqkey):
    pr.op("vector", lambda E: E.reduce_sum(cols[:, 0:1], x_ap, axis=AXX),
          reads=xreads, writes=[colskey])
    pr.op("scalar", lambda E: E.mul(cols[:, 0:1], cols[:, 0:1], 1.0 / D),
          reads=[colskey], writes=[colskey])
    pr.op("vector", lambda E: E.tensor_scalar(out_ap, x_ap, cols[:, 0:1], None,
                                              ALU.subtract),
          reads=list(xreads) + [colskey], writes=[outkey])
    pr.op("scalar", lambda E: E.activation(sq, out_ap, AF.Square),
          reads=[outkey], writes=[sqkey])
    pr.op("vector", lambda E: E.reduce_sum(cols[:, 1:2], sq, axis=AXX),
          reads=[sqkey], writes=[colskey + "v"])
    pr.op("vector", lambda E: E.tensor_scalar(cols[:, 1:2], cols[:, 1:2], 1.0 / D,
                                              1e-5, ALU.mult, ALU.add),
          reads=[colskey + "v"], writes=[colskey + "v"])
    pr.op("scalar", lambda E: E.activation(cols[:, 1:2], cols[:, 1:2], AF.Sqrt),
          reads=[colskey + "v"], writes=[colskey + "v"])
    pr.op("vector", lambda E: E.reciprocal(cols[:, 1:2], cols[:, 1:2]),
          reads=[colskey + "v"], writes=[colskey + "v"])
    pr.op("vector", lambda E: E.tensor_scalar(out_ap, out_ap, cols[:, 1:2], None,
                                              ALU.mult),
          reads=[outkey, colskey + "v"], writes=[outkey])


def build_l0(cx=None, xin_t=None, xout_t=None, maxphase=9):
    standalone = cx is None
    if standalone:
        cx = Ctx()
    pr = cx.pr

    xin = xin_t if xin_t is not None else cx.din("xin", [T, D])
    wbct = cx.din("wbct", [D, 136])
    vec8 = cx.din("vec8", [8, 3])
    d_bc = cx.din("d_bc", [128, 8])
    wr_in = cx.din("wr_in", [D, D])
    wi_in = cx.din("wi_in", [D, D])
    frc = cx.din("frc", [D, D], BF16)
    fic = cx.din("fic", [D, D], BF16)
    nfic = cx.din("nfic", [D, D], BF16)
    grc = cx.din("grc", [D, D], BF16)
    ngic = cx.din("ngic", [D, D], BF16)
    bias_bcd = cx.din("bias_bcd", [128, D])
    identb = cx.din("identb", [128, 128], BF16)
    identf = cx.din("identf", [128, 128])
    masknegd = cx.din("masknegd", [128, 128])
    ones1fd = cx.din("ones1fd", [1, 128])
    xout = xout_t if xout_t is not None else cx.dout("xout", [T, D])

    wr_bf = cx.dint("wr_bf", [D, D], BF16)
    wi_bf = cx.dint("wi_bf", [D, D], BF16)
    trt_d = cx.dint("trt_d", [D, D], BF16)
    tit_d = cx.dint("tit_d", [D, D], BF16)
    s_d = cx.dint("s_d", [T, D])
    g_d = cx.dint("g_d", [T, D])
    xnb_d = cx.dint("xnb_d", [T, D], BF16)

    xa = cx.sb("xa", [128, D]); xb_ = cx.sb("xb_", [128, D])
    sq = cx.sb("sq", [128, D])
    cols = cx.sb("cols", [128, 4])
    xnf = cx.sb("xnf", [128, D])
    xnb2 = cx.sb("xnb2", [128, 2 * D], BF16)
    xnbt = cx.sb("xnbt", [128, D], BF16)
    xnT = cx.sb("xnT", [128, ND * T], BF16)
    idb = cx.sb("idb", [128, 128], BF16)
    idf = cx.sb("idf", [128, 128])
    mneg = cx.sb("mneg", [128, 128])
    onesf = cx.sb("onesf", [1, 128])
    bias_sb = cx.sb("bias_sb", [128, D])
    wbctb = cx.sb("wbctb", [128, 8 * 136], BF16)
    v8 = cx.sb("v8", [8, 3])
    negA = cx.sb("negA", [8, 1])
    dbc = cx.sb("dbc", [128, 8])
    dtT = cx.sb("dtT", [8, T])
    aT = cx.sb("aT", [8, T])
    caT = cx.sb("caT", [8, T])
    t8a = caT
    t8b = aT
    ones8 = cx.sb("ones8", [8, 128])
    btb = cx.sb("btb", [64, T], BF16)
    ctb = cx.sb("ctb", [64, T], BF16)
    dtcols = cx.sb("dtcols", [128, 8 * NT])
    cacols = cx.sb("cacols", [128, 8 * NT])
    cab = cx.sb("cab", [128, 128])
    lm = cx.sb("lm", [128, 128])
    cbts = cx.sb("cbts", [128, 128])
    intraT = cx.sb("intraT", [128, 128], BF16)
    xdt = cx.sb("xdt", [128, 128], BF16)
    bnorm = cx.sb("bnorm", [128, 64], BF16)
    bhat = cx.sb("bhat", [128, 64], BF16)
    ecol = cx.sb("ecol", [128, 1])
    ecol2 = cx.sb("ecol2", [128, 1])
    lcol = cx.sb("lcol", [64, 1])
    Sst = cx.sb("Sst", [64, 8 * 128])
    Sbf = cx.sb("Sbf", [64, 8 * 128], BF16)
    yoff = cx.sb("yoff", [128, 128])
    ones_t = cx.sb("ones_t", [128, 128])
    bctmp = cx.sb("bctmp", [128, 128])
    yt1 = cx.sb("yt1", [128, 128])
    stile = cx.sb("stile", [128, D])
    wsA = cx.sb("wsA", [128, D], BF16); wsB = cx.sb("wsB", [128, D], BF16)
    fsA = cx.sb("fsA", [128, 512], BF16); fsB = cx.sb("fsB", [128, 512], BF16)
    tevac = cx.sb("tevac", [128, 512], BF16)
    mbf = cx.sb("mbf", [128, ND * D], BF16)
    gtile = cx.sb("gtile", [128, D])
    sgt = cx.sb("sgt", [128, D])

    ps = [cx.ps(f"ps{i}", [128, 512]) for i in range(6)]
    pstb = cx.ps("pstb", [128, 128], BF16)
    pstf = cx.ps("pstf", [128, 16])

    for name, dst, k in [("identb", idb, "idb"), ("identf", idf, "idf"),
                         ("masknegd", mneg, "mneg"), ("ones1fd", onesf, "onesf"),
                         ("vec8", v8, "v8"), ("d_bc", dbc, "dbc"),
                         ("bias_bcd", bias_sb, "biasbc")]:
        pr.op("sync", lambda E, s=cx.t[name], d=dst: E.dma_start(out=d[:, :], in_=s[:, :]),
              writes=[k], dma=True)
    pr.op("gpsimd", lambda E: E.memset(ones8[:, :], 1.0), writes=["ones8"])
    pr.op("gpsimd", lambda E: E.memset(ones_t[:, :], 1.0), writes=["ones_t"])
    pr.op("gpsimd", lambda E: E.memset(Sst[:, :], 0.0),
          writes=["Sst"] + [f"Sst{h}" for h in range(8)])
    pr.op("gpsimd", lambda E: E.memset(Sbf[:, :], 0.0),
          writes=["Sbf"] + [f"Sbf{h}" for h in range(8)])
    pr.op("scalar", lambda E: E.activation(negA[:, :], v8[:, 1:2], AF.Exp),
          reads=["v8"], writes=["negA"])
    pr.op("vector", lambda E: E.tensor_scalar(negA[:, :], negA[:, :], -1.0, None, ALU.mult),
          reads=["negA"], writes=["negA"])
    for k in range(8):
        pr.op("sync", lambda E, k=k: E.dma_start(out=xa[:, 0:136], in_=wbct[k * 128:(k + 1) * 128, :]),
              writes=["xa"], dma=True)
        pr.op("scalar", lambda E, k=k: E.copy(wbctb[:, k * 136:(k + 1) * 136], xa[:, 0:136]),
              reads=["xa"], writes=["wbctb"])
    cast_weight(cx, wr_in, wr_bf, D, D, xa, wsA, "xa", "wsA", "cwr")
    cast_weight(cx, wi_in, wi_bf, D, D, xb_, wsB, "xb_", "wsB", "cwi")

    if maxphase < 1:
        return cx
    for i in range(NT):
        xt, xk = (xa, "xa") if i % 2 == 0 else (xb_, "xb_")
        pr.op("sync", lambda E, i=i, xt=xt: E.dma_start(out=xt[:, :], in_=xin[i * 128:(i + 1) * 128, :]),
              writes=[xk], dma=True)
        ln_tile(pr, xt[:, :], [xk], xnf[:, :], "xnf", cols, "cols", sq[:, :], "sq")
        half = (i % 2) * D
        hk = f"xnb2h{i % 2}"
        pr.op("scalar", lambda E, half=half: E.copy(xnb2[:, half:half + D], xnf[:, :]),
              reads=["xnf"], writes=[hk])
        pr.op("sync", lambda E, i=i, half=half: E.dma_start(
            out=xnb_d[i * 128:(i + 1) * 128, :], in_=xnb2[:, half:half + D]),
            reads=[hk], writes=[f"xnbd{i}"], dma=True)
        for j in range(ND):
            pr.op("tensor", lambda E, half=half, j=j: E.transpose(
                pstb[:, :], xnb2[:, half + j * 128: half + (j + 1) * 128], idb[:, :]),
                reads=[hk, "idb"], writes=["pstb"])
            pr.op("scalar", lambda E, i=i, j=j: E.copy(
                xnT[:, j * T + i * 128: j * T + (i + 1) * 128], pstb[:, :]),
                reads=["pstb"], writes=[f"xnT{j}"])

    if maxphase < 2:
        return cx
    for (off, w, dstt, dkey) in [(0, 8, dtT, "dtT_raw"), (8, 64, btb, "btb"),
                                 (72, 64, ctb, "ctb")]:
        for c in range(8):
            for k in range(8):
                pr.op("tensor", lambda E, off=off, w=w, c=c, k=k: E.matmul(
                    ps[0][0:w, 0:512],
                    wbctb[:, k * 136 + off: k * 136 + off + w],
                    xnT[:, k * T + c * 512: k * T + (c + 1) * 512],
                    start=(k == 0), stop=(k == 7)),
                    reads=["wbctb", f"xnT{k}"],
                    writes=["ps0"] if k == 7 else [], war=["ps0"] if k == 0 else ())
            pr.op("scalar", lambda E, w=w, c=c, dstt=dstt: E.copy(
                dstt[0:w, c * 512:(c + 1) * 512], ps[0][0:w, 0:512]),
                reads=["ps0"], writes=[dkey])
    if maxphase < 1.4:
        return cx
    pr.op("vector", lambda E: E.tensor_scalar(dtT[:, :], dtT[:, :], v8[0:8, 0:1], None, ALU.add),
          reads=["dtT_raw", "v8"], writes=["dtT"])
    pr.op("scalar", lambda E: E.activation(t8a[:, :], dtT[:, :], AF.Abs),
          reads=["dtT"], writes=["t8a"])
    pr.op("scalar", lambda E: E.activation(t8a[:, :], t8a[:, :], AF.Exp, scale=-1.0),
          reads=["t8a"], writes=["t8a"])
    pr.op("scalar", lambda E: E.activation(t8a[:, :], t8a[:, :], AF.Ln, bias=1.0),
          reads=["t8a"], writes=["t8a"])
    pr.op("vector", lambda E: E.tensor_scalar(t8b[:, :], dtT[:, :], 0.0, None, ALU.max),
          reads=["dtT"], writes=["t8b"])
    pr.op("vector", lambda E: E.tensor_add(dtT[:, :], t8a[:, :], t8b[:, :]),
          reads=["t8a", "t8b"], writes=["dtT"])
    if maxphase < 1.6:
        return cx
    pr.op("vector", lambda E: E.tensor_scalar(aT[:, :], dtT[:, :], negA[0:8, 0:1], None, ALU.mult),
          reads=["dtT", "negA"], writes=["aT"])
    for i in range(NT):
        pr.op("vector", lambda E, i=i: E.tensor_tensor_scan(
            caT[:, i * 128:(i + 1) * 128], ones8[:, :], aT[:, i * 128:(i + 1) * 128],
            0.0, ALU.mult, ALU.add),
            reads=["aT", "ones8"], writes=["caT"])
    if maxphase < 1.8:
        return cx
    for i in range(NT):
        pr.op("tensor", lambda E, i=i: E.transpose(
            pstf[:, 0:8], dtT[:, i * 128:(i + 1) * 128], idf[0:8, 0:8]),
            reads=["dtT", "idf"], writes=["pstfA"])
        pr.op("vector", lambda E, i=i: E.tensor_copy(dtcols[:, i * 8:(i + 1) * 8], pstf[:, 0:8]),
              reads=["pstfA"], writes=["dtcols"])
        pr.op("tensor", lambda E, i=i: E.transpose(
            pstf[:, 0:8], caT[:, i * 128:(i + 1) * 128], idf[0:8, 0:8]),
            reads=["caT", "idf"], writes=["pstfA"])
        pr.op("vector", lambda E, i=i: E.tensor_copy(cacols[:, i * 8:(i + 1) * 8], pstf[:, 0:8]),
              reads=["pstfA"], writes=["cacols"])

    if maxphase < 3:
        return cx
    def product_pass(dst_d, dtag, pairs):
        for c in range(2):
            for jh in range(2):
                for k in range(8):
                    rd0 = [pairs[0][2] + f"dst{k}"] if pairs[0][2] else []
                    rd1 = [pairs[1][2] + f"dst{k}"] if pairs[1][2] else []
                    pr.op("sync", lambda E, k=k, w=pairs[0][0]: E.dma_start(
                        out=wsA[:, :], in_=w[k * 128:(k + 1) * 128, :]),
                        reads=rd0, writes=["wsA"], dma=True)
                    pr.op("sync", lambda E, k=k, w=pairs[1][0]: E.dma_start(
                        out=wsB[:, :], in_=w[k * 128:(k + 1) * 128, :]),
                        reads=rd1, writes=["wsB"], dma=True)
                    pr.op("sync", lambda E, k=k, c=c, f=pairs[0][1]: E.dma_start(
                        out=fsA[:, :], in_=f[k * 128:(k + 1) * 128, c * 512:(c + 1) * 512]),
                        writes=["fsA"], dma=True)
                    pr.op("sync", lambda E, k=k, c=c, f=pairs[1][1]: E.dma_start(
                        out=fsB[:, :], in_=f[k * 128:(k + 1) * 128, c * 512:(c + 1) * 512]),
                        writes=["fsB"], dma=True)
                    for jj in range(4):
                        j = jh * 4 + jj
                        last = (k == 7)
                        pr.op("tensor", lambda E, j=j, jj=jj, k=k: E.matmul(
                            ps[jj][:, :], wsA[:, j * 128:(j + 1) * 128], fsA[:, :],
                            start=(k == 0), stop=False), reads=["wsA", "fsA"],
                            war=[f"ps{jj}"] if k == 0 else ())
                        pr.op("tensor", lambda E, j=j, jj=jj, last=last: E.matmul(
                            ps[jj][:, :], wsB[:, j * 128:(j + 1) * 128], fsB[:, :],
                            start=False, stop=last), reads=["wsB", "fsB"],
                            writes=[f"ps{jj}"] if last else [])
                for jj in range(4):
                    j = jh * 4 + jj
                    pr.op("scalar", lambda E, jj=jj: E.copy(tevac[:, :], ps[jj][:, :]),
                          reads=[f"ps{jj}"], writes=["tevac"])
                    pr.op("sync", lambda E, j=j, c=c: E.dma_start(
                        out=dst_d[j * 128:(j + 1) * 128, c * 512:(c + 1) * 512],
                        in_=tevac[:, :]),
                        reads=["tevac"], writes=[f"{dtag}dst{j}_{c}"], dma=True)

    product_pass(trt_d, "trt", [(wr_bf, frc, "cwr"), (wi_bf, nfic, "cwi")])
    product_pass(tit_d, "tit", [(wi_bf, frc, "cwi"), (wr_bf, fic, "cwr")])
    for c in range(2):
        for jh in range(2):
            for k in range(8):
                pr.op("sync", lambda E, k=k: E.dma_start(
                    out=wsA[:, :], in_=trt_d[k * 128:(k + 1) * 128, :]),
                    reads=[f"trtdst{k}_0", f"trtdst{k}_1"], writes=["wsA"], dma=True)
                pr.op("sync", lambda E, k=k: E.dma_start(
                    out=wsB[:, :], in_=tit_d[k * 128:(k + 1) * 128, :]),
                    reads=[f"titdst{k}_0", f"titdst{k}_1"], writes=["wsB"], dma=True)
                pr.op("sync", lambda E, k=k, c=c: E.dma_start(
                    out=fsA[:, :], in_=grc[k * 128:(k + 1) * 128, c * 512:(c + 1) * 512]),
                    writes=["fsA"], dma=True)
                pr.op("sync", lambda E, k=k, c=c: E.dma_start(
                    out=fsB[:, :], in_=ngic[k * 128:(k + 1) * 128, c * 512:(c + 1) * 512]),
                    writes=["fsB"], dma=True)
                for jj in range(4):
                    j = jh * 4 + jj
                    last = (k == 7)
                    pr.op("tensor", lambda E, j=j, jj=jj, k=k: E.matmul(
                        ps[jj][:, :], wsA[:, j * 128:(j + 1) * 128], fsA[:, :],
                        start=(k == 0), stop=False), reads=["wsA", "fsA"],
                        war=[f"ps{jj}"] if k == 0 else ())
                    pr.op("tensor", lambda E, j=j, jj=jj, last=last: E.matmul(
                        ps[jj][:, :], wsB[:, j * 128:(j + 1) * 128], fsB[:, :],
                        start=False, stop=last), reads=["wsB", "fsB"],
                        writes=[f"ps{jj}"] if last else [])
            for jj in range(4):
                j = jh * 4 + jj
                pr.op("scalar", lambda E, jj=jj, j=j, c=c: E.copy(
                    mbf[:, j * D + c * 512: j * D + (c + 1) * 512], ps[jj][:, :]),
                    reads=[f"ps{jj}"], writes=[f"mbf{j}"])

    if maxphase < 4:
        return cx
    for i in range(NT):
        pr.op("sync", lambda E, i=i: E.dma_start(out=xnbt[:, :], in_=xnb_d[i * 128:(i + 1) * 128, :]),
              reads=[f"xnbd{i}"], writes=["xnbt"], dma=True)
        pr.op("tensor", lambda E, i=i: E.transpose(
            pstb[:, 0:64], btb[:, i * 128:(i + 1) * 128], idb[0:64, 0:64]),
            reads=["btb", "idb"], writes=["pstb"])
        pr.op("scalar", lambda E: E.copy(bnorm[:, :], pstb[:, 0:64]),
              reads=["pstb"], writes=["bnorm"])
        pr.op("tensor", lambda E, i=i: E.matmul(
            ps[5][:, 0:128], btb[:, i * 128:(i + 1) * 128], ctb[:, i * 128:(i + 1) * 128],
            start=True, stop=True), reads=["btb", "ctb"], writes=["ps5"])
        pr.op("vector", lambda E: E.tensor_copy(cbts[:, :], ps[5][:, 0:128]),
              reads=["ps5"], writes=["cbts"])
        for h in range(8):
            cac = cacols[:, i * 8 + h: i * 8 + h + 1]
            dtc = dtcols[:, i * 8 + h: i * 8 + h + 1]
            pr.op("vector", lambda E, cac=cac: E.tensor_scalar(
                bctmp[:, :], ones_t[:, :], cac, None, ALU.mult),
                reads=["ones_t", "cacols"], writes=["bctmp"])
            pr.op("tensor", lambda E: E.transpose(ps[0][:, 0:128], bctmp[:, :], idf[:, :]),
                  reads=["bctmp", "idf"], writes=["ps0"])
            pr.op("scalar", lambda E: E.copy(cab[:, :], ps[0][:, 0:128]),
                  reads=["ps0"], writes=["cab"])
            pr.op("vector", lambda E, cac=cac: E.scalar_tensor_tensor(
                lm[:, :], cab[:, :], cac, mneg[:, :], ALU.subtract, ALU.add),
                reads=["cab", "cacols", "mneg"], writes=["lm"])
            pr.op("scalar", lambda E: E.activation(lm[:, :], lm[:, :], AF.Exp),
                  reads=["lm"], writes=["lm"])
            pr.op("vector", lambda E: E.tensor_mul(intraT[:, :], cbts[:, :], lm[:, :]),
                  reads=["cbts", "lm"], writes=["intraT"])
            pr.op("vector", lambda E, h=h, dtc=dtc: E.tensor_scalar(
                xdt[:, :], xnbt[:, h * 128:(h + 1) * 128], dtc, None, ALU.mult),
                reads=["xnbt", "dtcols"], writes=["xdt"])
            pr.op("tensor", lambda E: E.matmul(ps[1][:, 0:128], intraT[:, :], xdt[:, :],
                                               start=True, stop=True),
                  reads=["intraT", "xdt"], writes=["ps1"])
            pr.op("tensor", lambda E, i=i, h=h: E.matmul(
                ps[2][:, 0:128], ctb[:, i * 128:(i + 1) * 128], Sbf[:, h * 128:(h + 1) * 128],
                start=True, stop=True), reads=["ctb", f"Sbf{h}"], writes=["ps2"])
            pr.op("scalar", lambda E, cac=cac: E.activation(ecol[:, :], cac, AF.Exp),
                  reads=["cacols"], writes=["ecol"])
            pr.op("vector", lambda E: E.tensor_scalar(yoff[:, :], ps[2][:, 0:128],
                                                      ecol[:, :], None, ALU.mult),
                  reads=["ps2", "ecol"], writes=["yoff"])
            pr.op("vector", lambda E, h=h: E.scalar_tensor_tensor(
                yt1[:, :], xnbt[:, h * 128:(h + 1) * 128],
                dbc[:, h:h + 1], ps[1][:, 0:128], ALU.mult, ALU.add),
                reads=["xnbt", "dbc", "ps1"], writes=["yt1"])
            pr.op("vector", lambda E, h=h: E.tensor_add(stile[:, h * 128:(h + 1) * 128],
                                                        yt1[:, :], yoff[:, :]),
                  reads=["yt1", "yoff"], writes=["stile"])
            pr.op("vector", lambda E, cac=cac: E.tensor_sub(ecol2[:, :], cab[:, 127:128], cac),
                  reads=["cacols", "cab"], writes=["ecol2"])
            pr.op("scalar", lambda E: E.activation(ecol2[:, :], ecol2[:, :], AF.Exp),
                  reads=["ecol2"], writes=["ecol2"])
            pr.op("vector", lambda E: E.tensor_scalar(bhat[:, :], bnorm[:, :], ecol2[:, :],
                                                      None, ALU.mult),
                  reads=["bnorm", "ecol2"], writes=["bhat"])
            pr.op("tensor", lambda E: E.matmul(ps[3][0:64, 0:128], bhat[:, :], xdt[:, :],
                                               start=True, stop=True),
                  reads=["bhat", "xdt"], writes=["ps3"])
            pr.op("scalar", lambda E: E.activation(lcol[:, :], cab[0:64, 127:128], AF.Exp),
                  reads=["cab"], writes=["lcol"])
            pr.op("vector", lambda E, h=h: E.scalar_tensor_tensor(
                Sst[:, h * 128:(h + 1) * 128], Sst[:, h * 128:(h + 1) * 128],
                lcol[:, :], ps[3][0:64, 0:128], ALU.mult, ALU.add),
                reads=[f"Sst{h}", "lcol", "ps3"], writes=[f"Sst{h}"])
            pr.op("scalar", lambda E, h=h: E.copy(Sbf[:, h * 128:(h + 1) * 128],
                                                  Sst[:, h * 128:(h + 1) * 128]),
                  reads=[f"Sst{h}"], writes=[f"Sbf{h}"])
        pr.op("sync", lambda E, i=i: E.dma_start(out=s_d[i * 128:(i + 1) * 128, :],
                                                 in_=stile[:, :]),
              reads=["stile"], writes=[f"s_d{i}"], dma=True)

    if maxphase < 5:
        return cx
    for i in range(NT):
        for c in range(2):
            for j in range(8):
                pr.op("tensor", lambda E, i=i, j=j, c=c: E.matmul(
                    ps[c][:, :], xnT[:, j * T + i * 128: j * T + (i + 1) * 128],
                    mbf[:, j * D + c * 512: j * D + (c + 1) * 512],
                    start=(j == 0), stop=(j == 7)),
                    reads=[f"xnT{j}", f"mbf{j}"],
                    writes=[f"ps{c}"] if j == 7 else [], war=[f"ps{c}"] if j == 0 else ())
        for c in range(2):
            pr.op("vector", lambda E, c=c: E.tensor_add(
                gtile[:, c * 512:(c + 1) * 512], ps[c][:, :], bias_sb[:, c * 512:(c + 1) * 512]),
                reads=[f"ps{c}", "biasbc"], writes=["gtile"])
        pr.op("scalar", lambda E: E.activation(gtile[:, :], gtile[:, :], AF.Silu),
              reads=["gtile"], writes=["gtile"])
        pr.op("sync", lambda E, i=i: E.dma_start(out=g_d[i * 128:(i + 1) * 128, :],
                                                 in_=gtile[:, :]),
              reads=["gtile"], writes=[f"g_d{i}"], dma=True)

    if maxphase < 6:
        return cx
    for i in range(NT):
        xt, xk = (xa, "xa") if i % 2 == 0 else (xb_, "xb_")
        pr.op("sync", lambda E, i=i, xt=xt: E.dma_start(out=xt[:, :], in_=xin[i * 128:(i + 1) * 128, :]),
              writes=[xk], dma=True)
        pr.op("scalar", lambda E, i=i: E.dma_start(out=sgt[:, :], in_=s_d[i * 128:(i + 1) * 128, :]),
              reads=[f"s_d{i}"], writes=["sgt"], dma=True)
        pr.op("gpsimd", lambda E, i=i: E.dma_start(out=gtile[:, :], in_=g_d[i * 128:(i + 1) * 128, :]),
              reads=[f"g_d{i}"], writes=["gtile"], dma=True)
        pr.op("vector", lambda E: E.tensor_mul(sgt[:, :], sgt[:, :], gtile[:, :]),
              reads=["sgt", "gtile"], writes=["sgt"])
        pr.op("vector", lambda E, xt=xt: E.tensor_add(sgt[:, :], sgt[:, :], xt[:, :]),
              reads=["sgt", xk], writes=["sgt"])
        pr.op("sync", lambda E, i=i: E.dma_start(out=xout[i * 128:(i + 1) * 128, :], in_=sgt[:, :]),
              reads=["sgt"], writes=[f"xout{i}"], dma=True)

    return cx


def _host_consts_l0(inputs):
    idx = np.arange(D)
    ang = (2 * np.pi / D) * np.outer(idx, idx)
    Fr = np.cos(ang).astype(np.float32)
    Fi = (-np.sin(ang)).astype(np.float32)
    Gr = (np.cos(ang) / D).astype(np.float32)
    Gi = (np.sin(ang) / D).astype(np.float32)
    s_i = np.arange(128)[:, None]
    t_i = np.arange(128)[None, :]
    v8 = np.stack([np.asarray(inputs["L0_attn_dt_bias"]),
                   np.asarray(inputs["L0_attn_A_log"]),
                   np.zeros(8)], axis=1).astype(np.float32)
    return {
        "wbct": np.ascontiguousarray(np.asarray(inputs["L0_attn_W_xproj"]).T).astype(np.float32),
        "vec8": v8,
        "d_bc": np.ascontiguousarray(np.broadcast_to(
            np.asarray(inputs["L0_attn_D"])[None, :], (128, 8))).astype(np.float32),
        "wr_in": np.asarray(inputs["L0_ffn_w_freq_real"], np.float32),
        "wi_in": np.asarray(inputs["L0_ffn_w_freq_imag"], np.float32),
        "frc": Fr.astype(BFNP), "fic": Fi.astype(BFNP), "nfic": (-Fi).astype(BFNP),
        "grc": Gr.astype(BFNP), "ngic": (-Gi).astype(BFNP),
        "bias_bcd": np.ascontiguousarray(np.broadcast_to(
            np.asarray(inputs["L0_ffn_bias"])[None, :], (128, D))).astype(np.float32),
        "identb": np.eye(128, dtype=BFNP),
        "identf": np.eye(128, dtype=np.float32),
        "masknegd": np.where(s_i <= t_i, 0.0, -1e30).astype(np.float32),
        "ones1fd": np.ones((1, 128), np.float32),
    }


def build_all():
    cx = Ctx(enable_partition_id=True)
    xin = cx.nc.dram_tensor("xin", [T, D], F32, kind="ExternalInput")
    x1s = cx.nc.dram_tensor("x1s", [T, D], F32, kind="Internal")
    x2s = cx.nc.dram_tensor("x2s", [T, D], F32, kind="Internal")
    x3s = cx.nc.dram_tensor("x3s", [T, D], F32, kind="Internal")
    xout = cx.nc.dram_tensor("xout", [T, D], F32, kind="ExternalOutput")
    chain = [(build_l0, xin, x1s), (build_l1, x1s, x2s),
             (build_l2, x2s, x3s), (build_l3, x3s, xout)]
    for i, (bld, xi, xo) in enumerate(chain):
        cx.pfx = f"y{i}_"
        with cx.scope():
            bld(cx, xin_t=xi, xout_t=xo)
        cx.pr.barrier()
    cx.pfx = ""
    return cx


def _all_consts(inputs):
    com = {}
    h1 = _host_consts_l1(inputs, np.zeros((128, D), np.float32))
    del h1["x16"]
    for i, hc in enumerate([_host_consts_l0(inputs), h1,
                            _host_consts_l2(inputs), _host_consts_l3(inputs)]):
        for k, v in hc.items():
            com[f"y{i}_{k}"] = v
    return com


def kernel(**inputs):
    inputs = {k: np.asarray(v) for k, v in inputs.items()}
    x = np.asarray(inputs["L0_attn_x"], np.float32)
    cx = build_all()
    nc = cx.close()
    com = _all_consts(inputs)
    maps = []
    for b in range(8):
        m = dict(com)
        m["xin"] = np.ascontiguousarray(x[b])
        maps.append(m)
    res = run_bass_kernel_spmd(nc, maps, core_ids=list(range(8)))
    return np.stack([r["xout"] for r in res.results], axis=0)


def kernel_unfused(**inputs):
    inputs = {k: np.asarray(v) for k, v in inputs.items()}
    x = np.asarray(inputs["L0_attn_x"], np.float32)

    def launch(build, com, xcur, extra=None):
        cx = build()
        nc = cx.close()
        maps = []
        for b in range(8):
            m = dict(com) if not callable(com) else dict(com(b))
            m["xin"] = np.ascontiguousarray(xcur[b])
            maps.append(m)
        res = run_bass_kernel_spmd(nc, maps, core_ids=list(range(8)))
        return np.stack([r["xout"] for r in res.results], axis=0)

    x1 = launch(build_l0, _host_consts_l0(inputs), x)
    x16 = x1[:, :16, :]
    x2 = launch(build_l1,
                lambda b: _host_consts_l1(inputs, np.roll(x16, -b, axis=0).reshape(128, D)),
                x1)
    x3 = launch(build_l2, _host_consts_l2(inputs), x2)
    x4 = launch(build_l3, _host_consts_l3(inputs), x3)
    return x4


def build_l3(cx=None, xin_t=None, xout_t=None):
    standalone = cx is None
    if standalone:
        cx = Ctx()
    pr = cx.pr
    NH, HD, C = 16, 64, 128
    NG, GC, GW = 16, 2, 256

    xin = xin_t if xin_t is not None else cx.din("xin", [T, D])
    wqt = cx.din("wqt", [D, D]); wkt = cx.din("wkt", [D, D])
    wvt = cx.din("wvt", [D, D]); wgt = cx.din("wgt", [D, D])
    wot = cx.din("wot", [D, D])
    w1 = cx.din("w1", [D, 4 * D]); w2 = cx.din("w2", [4 * D, D])
    alogit = cx.din("alogit", [NH, HD])
    bqc_d = cx.din("bqc_d", [128, 8]); bgc_d = cx.din("bgc_d", [128, 8])
    bkc_d = cx.din("bkc_d", [128, 8]); bk_d = cx.din("bk_d", [128, D])
    b1c_d = cx.din("b1c_d", [128, 32])
    bv_d = cx.din("bv_d", [128, D]); bo_d = cx.din("bo_d", [128, D])
    b2_d = cx.din("b2_d", [128, D])
    lng_d = cx.din("lng_d", [128, D]); lnb_d = cx.din("lnb_d", [128, D])
    identb3 = cx.din("identb3", [128, 128], BF16)
    identf3 = cx.din("identf3", [128, 128])
    tril01 = cx.din("tril01", [128, 128])
    relrow = cx.din("relrow", [128, 384])
    relkrow = cx.din("relkrow", [64, 128])
    xout = xout_t if xout_t is not None else cx.dout("xout", [T, D])

    wq_bf = cx.dint("wq_bf", [D, D], BF16); wk_bf = cx.dint("wk_bf", [D, D], BF16)
    wv_bf = cx.dint("wv_bf", [D, D], BF16); wg_bf = cx.dint("wg_bf", [D, D], BF16)
    wo_bf = cx.dint("wo_bf", [D, D], BF16)
    w1_bf = cx.dint("w1_bf", [D, 4 * D], BF16)
    w2_bf = cx.dint("w2_bf", [4 * D, D], BF16)
    s_d = cx.dint("s_d", [T, D]); g_d = cx.dint("g_d", [T, D])
    xn_d = cx.dint("xn_d", [T, D])

    xa = cx.sb("xa", [128, D]); xb_ = cx.sb("xb_", [128, D])
    sq = cx.sb("sq", [128, D]); cols = cx.sb("cols", [128, 4])
    xnf = cx.sb("xnf", [128, D])
    sq2 = cx.sb("sq2", [128, D]); cols2 = cx.sb("cols2", [128, 4])
    xnf2 = cx.sb("xnf2", [128, D])
    xnb2 = cx.sb("xnb2", [128, 2 * D], BF16)
    xnT = cx.sb("xnT", [128, ND * T], BF16)
    idb = cx.sb("idb", [128, 128], BF16); idf = cx.sb("idf", [128, 128])
    tril = cx.sb("tril", [128, 128])
    bqc = cx.sb("bqc", [128, 8]); bgc = cx.sb("bgc", [128, 8])
    bkc = cx.sb("bkc", [128, 8]); bkb = cx.sb("bkb", [128, D])
    b1c = cx.sb("b1c", [128, 32])
    bvb = cx.sb("bvb", [128, D]); bob = cx.sb("bob", [128, D])
    b2b = cx.sb("b2b", [128, D])
    lngb = cx.sb("lngb", [128, D]); lnbb = cx.sb("lnbb", [128, D])
    alg = cx.sb("alg", [NH, HD])
    lnal2 = cx.sb("lnal2", [128, NH])
    colp = cx.sb("colp", [128, 8])
    dccp = cx.sb("dccp", [128, 8])
    powA = cx.sb("powA", [128, 8 * 64], BF16)
    powB = cx.sb("powB", [128, 8 * 64], BF16)
    powC = cx.sb("powC", [128, 8 * 64], BF16)
    powD = cx.sb("powD", [128, 8 * 64], BF16)
    powq = cx.sb("powq", [128, 8 * 128], BF16)
    powK = cx.sb("powK", [128, NH * 64], BF16)
    ptmp = cx.sb("ptmp", [128, 128])
    qtb = cx.sb("qtb", [128, 8 * GW], BF16)
    ktb = cx.sb("ktb", [128, 8 * GW], BF16)
    gts = cx.sb("gts", [128, 8 * GW], BF16)
    vtb = cx.sb("vtb", [128, GC * D], BF16)
    knb = cx.sb("knb", [128, GC * D], BF16)
    otb = cx.sb("otb", [128, 8 * GW], BF16)
    qsA = cx.sb("qsA", [128, 64], BF16); qsB = cx.sb("qsB", [128, 64], BF16)
    qsC = cx.sb("qsC", [128, 64], BF16)
    ksA = cx.sb("ksA", [128, 64], BF16); ksB = cx.sb("ksB", [128, 64], BF16)
    ksC = cx.sb("ksC", [128, 64], BF16)
    qh = cx.sb("qh", [128, 128], BF16)
    khn = cx.sb("khn", [128, 64], BF16)
    atb = cx.sb("atb", [128, 128], BF16)
    Sst3 = cx.sb("Sst3", [128, 8 * 64])
    Sb3 = cx.sb("Sb3", [128, 8 * 64], BF16)
    w1cb = cx.sb("w1cb", [128, 8 * 128], BF16)
    w1cbB = cx.sb("w1cbB", [128, 8 * 128], BF16)
    w2rb = cx.sb("w2rb", [128, D], BF16)
    wob_s = cx.sb("wob_s", [128, D], BF16)
    htb = cx.sb("htb", [128, 32 * 256], BF16)
    stile = cx.sb("stile", [128, D]); gtile = cx.sb("gtile", [128, D])
    sgt = cx.sb("sgt", [128, D])

    ps = [cx.ps(f"ps{i}", [128, 512]) for i in range(6)]
    pstb = cx.ps("pstb", [128, 128], BF16)
    pstf = cx.ps("pstf", [128, 128])

    for name, dst, k in [("identb3", idb, "idb"), ("identf3", idf, "idf"),
                         ("tril01", tril, "tril"), ("bqc_d", bqc, "bqc"),
                         ("bgc_d", bgc, "bgc"), ("b1c_d", b1c, "b1c"),
                         ("bkc_d", bkc, "bkc"), ("bk_d", bkb, "bkb"),
                         ("bv_d", bvb, "bvb"), ("bo_d", bob, "bob"),
                         ("b2_d", b2b, "b2b"), ("lng_d", lngb, "lngb"),
                         ("lnb_d", lnbb, "lnbb"), ("alogit", alg, "alg")]:
        pr.op("sync", lambda E, s=cx.t[name], d=dst: E.dma_start(out=d[:, :], in_=s[:, :]),
              writes=[k], dma=True)
    for src_, dst_, tag in [(wqt, wq_bf, "cwq"), (wkt, wk_bf, "cwk"),
                            (wvt, wv_bf, "cwv"), (wgt, wg_bf, "cwg"),
                            (wot, wo_bf, "cwo")]:
        cast_weight2(cx, src_, dst_, D, D, tag)
    cast_weight2(cx, w1, w1_bf, D, 4 * D, "cw1")
    cast_weight2(cx, w2, w2_bf, 4 * D, D, "cw2")
    pr.op("gpsimd", lambda E: E.memset(Sst3[:, :], 0.0),
          writes=[f"S3_{p}_{r0}" for p in range(8) for r0 in (0, 64)])
    pr.op("gpsimd", lambda E: E.memset(Sb3[:, :], 0.0),
          writes=[f"Sb3_{p}_{r0}" for p in range(8) for r0 in (0, 64)])
    pr.op("scalar", lambda E: E.activation(alg[:, :], alg[:, :], AF.Sigmoid),
          reads=["alg"], writes=["alg"])
    pr.op("scalar", lambda E: E.activation(alg[:, :], alg[:, :], AF.Ln),
          reads=["alg"], writes=["alg"])
    pr.op("tensor", lambda E: E.transpose(pstf[0:HD, 0:NH], alg[:, :], idf[0:NH, 0:NH]),
          reads=["alg", "idf"], writes=["pstfA"])
    pr.op("vector", lambda E: E.tensor_copy(lnal2[0:64, :], pstf[0:HD, 0:NH]),
          reads=["pstfA"], writes=["lnal2"])
    pr.op("gpsimd", lambda E: E.dma_start(out=lnal2[64:128, :], in_=lnal2[0:64, :]),
          reads=["lnal2"], writes=["lnal2d"], dma=True)
    for p in range(8):
        pr.op("vector", lambda E, p=p: E.tensor_copy(colp[0:64, p:p + 1], lnal2[0:64, 2 * p:2 * p + 1]),
              reads=["lnal2"], writes=["colp"])
        pr.op("vector", lambda E, p=p: E.tensor_copy(colp[64:128, p:p + 1], lnal2[64:128, 2 * p + 1:2 * p + 2]),
              reads=["lnal2d"], writes=["colp"])
    pr.op("vector", lambda E: E.tensor_scalar(dccp[:, :], colp[:, :], float(C), None, ALU.mult),
          reads=["colp"], writes=["dccp"])
    pr.op("scalar", lambda E: E.activation(dccp[:, :], dccp[:, :], AF.Exp),
          reads=["dccp"], writes=["dccp"])
    relrow_sb = cx.sb("relrow_sb", [128, 384])
    relk_sb = cx.sb("relk_sb", [64, 128])
    pr.op("sync", lambda E: E.dma_start(out=relrow_sb[:, :], in_=relrow[:, :]),
          writes=["relrow_sb"], dma=True)
    pr.op("sync", lambda E: E.dma_start(out=relk_sb[:, :], in_=relkrow[:, :]),
          writes=["relk_sb"], dma=True)
    for p in range(8):
        for (dst, off, w, dk) in [(powA, 0, 64, "powA"), (powB, 64, 64, "powB"),
                                  (powC, 128, 64, "powC"), (powD, 192, 64, "powD"),
                                  (powq, 256, 128, "powq")]:
            pr.op("vector", lambda E, p=p, off=off, w=w: E.tensor_scalar(
                ptmp[:, 0:w], relrow_sb[:, off:off + w], colp[:, p:p + 1], None, ALU.mult),
                reads=["relrow_sb", "colp"], writes=["ptmp"])
            pr.op("scalar", lambda E, p=p, dst=dst, w=w, dk=dk: E.activation(
                dst[:, p * w:(p + 1) * w], ptmp[:, 0:w], AF.Exp),
                reads=["ptmp"], writes=[dk])
    for h in range(NH):
        pr.op("vector", lambda E, h=h: E.tensor_scalar(
            ptmp[0:64, 0:128], relk_sb[:, :], lnal2[0:64, h:h + 1], None, ALU.mult),
            reads=["relk_sb", "lnal2"], writes=["ptmp"])
        pr.op("scalar", lambda E: E.activation(atb[0:64, 0:128], ptmp[0:64, 0:128], AF.Exp),
              reads=["ptmp"], writes=["atbK"])
        pr.op("tensor", lambda E: E.transpose(pstb[:, 0:64], atb[0:64, 0:128], idb[0:64, 0:64]),
              reads=["atbK", "idb"], writes=["pstb"])
        pr.op("scalar", lambda E, h=h: E.copy(powK[:, h * 64:(h + 1) * 64], pstb[:, 0:64]),
              reads=["pstb"], writes=["powKx"])
    pr.op("gpsimd", lambda E: E.memset(atb[64:128, 0:64], 0.0), writes=["atbZ"])

    for i in range(NT):
        xt, xk = (xa, "xa") if i % 2 == 0 else (xb_, "xb_")
        pr.op("sync", lambda E, i=i, xt=xt: E.dma_start(out=xt[:, :], in_=xin[i * 128:(i + 1) * 128, :]),
              writes=[xk], dma=True)
        XF, SQ, CO, XFK, SQK, COK = ((xnf, sq, cols, "xnf", "sq", "cols") if i % 2 == 0
                                     else (xnf2, sq2, cols2, "xnf2", "sq2", "cols2"))
        ln_tile(pr, xt[:, :], [xk], XF[:, :], XFK, CO, COK, SQ[:, :], SQK)
        pr.op("vector", lambda E, XF=XF: E.tensor_mul(XF[:, :], XF[:, :], lngb[:, :]),
              reads=[XFK, "lngb"], writes=[XFK])
        pr.op("vector", lambda E, XF=XF: E.tensor_add(XF[:, :], XF[:, :], lnbb[:, :]),
              reads=[XFK, "lnbb"], writes=[XFK])
        half = (i % 2) * D
        hk = f"xnb2h{i % 2}"
        pr.op("scalar", lambda E, half=half, XF=XF: E.copy(xnb2[:, half:half + D], XF[:, :]),
              reads=[XFK], writes=[hk])
        pr.op("gpsimd", lambda E, i=i, XF=XF: E.dma_start(out=xn_d[i * 128:(i + 1) * 128, :], in_=XF[:, :]),
              reads=[XFK], writes=[f"xn_d{i}"], dma=True)
        for j in range(ND):
            pr.op("tensor", lambda E, half=half, j=j: E.transpose(
                pstb[:, :], xnb2[:, half + j * 128: half + (j + 1) * 128], idb[:, :]),
                reads=[hk, "idb"], writes=["pstb"])
            pr.op("scalar", lambda E, i=i, j=j: E.copy(
                xnT[:, j * T + i * 128: j * T + (i + 1) * 128], pstb[:, :]),
                reads=["pstb"], writes=[f"xnT{j}"])

    for g in range(NG):
        t0 = g * GW
        for (wd, dst, dkey, bias_col, act) in [
                (wq_bf, qtb, "qtb", ("bqc", bqc), None),
                (wk_bf, ktb, "ktb", ("bkc", bkc), None),
                (wg_bf, gts, "gts", ("bgc", bgc), AF.Silu)]:
            for j in range(8):
                pp, pk = ("ps0", 0) if j % 2 == 0 else ("ps5", 5)
                for k in range(8):
                    par = (j * 8 + k) % 4
                    wsl = w2rb[:, par * 128:(par + 1) * 128]
                    wkk = f"w2rb{par}"
                    pr.op("sync" if k % 2 == 0 else "scalar",
                          lambda E, wd=wd, k=k, j=j, wsl=wsl: E.dma_start(
                        out=wsl, in_=wd[k * 128:(k + 1) * 128, j * 128:(j + 1) * 128]),
                        reads=[{"qtb": "cwqdone", "ktb": "cwkdone", "gts": "cwgdone"}[dkey]],
                        writes=[wkk], war=["w2rbh0"], dma=True)
                    pr.op("tensor", lambda E, j=j, k=k, t0=t0, wsl=wsl, pk=pk: E.matmul(
                        ps[pk][:, 0:GW], wsl,
                        xnT[:, k * T + t0: k * T + t0 + GW],
                        start=(k == 0), stop=(k == 7)),
                        reads=[wkk, f"xnT{k}"],
                        writes=[pp] if k == 7 else [], war=[pp] if k == 0 else ())
                if act is None:
                    pr.op("vector", lambda E, j=j, dst=dst, bc=bias_col[1], pk=pk: E.tensor_scalar(
                        dst[:, j * GW:(j + 1) * GW], ps[pk][:, 0:GW], bc[:, j:j + 1], None, ALU.add),
                        reads=[pp, bias_col[0]], writes=[dkey])
                else:
                    pr.op("scalar", lambda E, j=j, dst=dst, bc=bias_col[1], act=act, pk=pk: E.activation(
                        dst[:, j * GW:(j + 1) * GW], ps[pk][:, 0:GW], act, bias=bc[:, j:j + 1]),
                        reads=[pp, bias_col[0]], writes=[dkey])
        for (wd, wtag, dst, dkey, bias_bc) in [
                (wv_bf, "cwv", vtb, "vtb", ("bvb", bvb)), (wk_bf, "cwk", knb, "knb", ("bkb", bkb))]:
            for tt in range(GC):
                for c in range(2):
                    for k in range(8):
                        par = k % 2
                        wsl = w2rb[:, par * 512:(par + 1) * 512]
                        wkk = f"w2rbh{par}"
                        pr.op("sync" if k % 2 == 0 else "scalar",
                              lambda E, wd=wd, k=k, c=c, wsl=wsl: E.dma_start(
                            out=wsl, in_=wd[k * 128:(k + 1) * 128, c * 512:(c + 1) * 512]),
                            reads=[wtag + "done"], writes=[wkk],
                            war=(["w2rb0", "w2rb1", "w2rb2", "w2rb3"] if wkk == "w2rbh0" else ()),
                            dma=True)
                        pr.op("tensor", lambda E, k=k, c=c, tt=tt, t0=t0, wsl=wsl: E.matmul(
                            ps[1][:, :], xnT[:, k * T + t0 + tt * 128: k * T + t0 + (tt + 1) * 128],
                            wsl, start=(k == 0), stop=(k == 7)),
                            reads=[wkk, f"xnT{k}"],
                            writes=["ps1"] if k == 7 else [], war=["ps1"] if k == 0 else ())
                    pr.op("vector", lambda E, tt=tt, c=c, dst=dst, bb=bias_bc[1]: E.tensor_add(
                        dst[:, tt * D + c * 512: tt * D + (c + 1) * 512], ps[1][:, :],
                        bb[:, c * 512:(c + 1) * 512]),
                        reads=["ps1", bias_bc[0]], writes=[dkey])
        for cc in range(GC):
            coff = cc * 128
            for p in range(8):
                base = p * GW + coff
                pr.op("vector", lambda E, base=base, p=p: E.tensor_mul(
                    qsA[:, :], qtb[:, base:base + 64], powA[:, p * 64:(p + 1) * 64]),
                    reads=["qtb", "powA"], writes=["qsA"])
                pr.op("vector", lambda E, base=base, p=p: E.tensor_mul(
                    qsB[:, :], qtb[:, base + 64:base + 128], powA[:, p * 64:(p + 1) * 64]),
                    reads=["qtb", "powA"], writes=["qsB"])
                pr.op("vector", lambda E, base=base, p=p: E.tensor_mul(
                    qsC[:, :], qtb[:, base + 64:base + 128], powC[:, p * 64:(p + 1) * 64]),
                    reads=["qtb", "powC"], writes=["qsC"])
                pr.op("vector", lambda E, base=base, p=p: E.tensor_mul(
                    ksA[:, :], ktb[:, base:base + 64], powB[:, p * 64:(p + 1) * 64]),
                    reads=["ktb", "powB"], writes=["ksA"])
                pr.op("vector", lambda E, base=base, p=p: E.tensor_mul(
                    ksB[:, :], ktb[:, base + 64:base + 128], powB[:, p * 64:(p + 1) * 64]),
                    reads=["ktb", "powB"], writes=["ksB"])
                pr.op("vector", lambda E, base=base, p=p: E.tensor_mul(
                    ksC[:, :], ktb[:, base:base + 64], powD[:, p * 64:(p + 1) * 64]),
                    reads=["ktb", "powD"], writes=["ksC"])
                pr.op("vector", lambda E, base=base, p=p: E.tensor_mul(
                    qh[:, :], qtb[:, base:base + 128], powq[:, p * 128:(p + 1) * 128]),
                    reads=["qtb", "powq"], writes=["qh"])
                for par in range(2):
                    h = 2 * p + par
                    r0 = par * 64
                    vsl = vtb[:, cc * D + h * 64: cc * D + (h + 1) * 64]
                    gsl = gts[r0:r0 + 64, p * GW + coff: p * GW + coff + 128]
                    knsl = knb[:, cc * D + h * 64: cc * D + (h + 1) * 64]
                    pr.op("tensor", lambda E, r0=r0: E.matmul(
                        pstf[0:64, 0:64], ksA[r0:r0 + 64, :], qsA[r0:r0 + 64, :],
                        start=True, stop=True, tile_position=(r0, 0)),
                        reads=["ksA", "qsA"], writes=["pstfT"])
                    pr.op("vector", lambda E: E.tensor_mul(atb[0:64, 0:64],
                                                           pstf[0:64, 0:64], tril[0:64, 0:64]),
                          reads=["pstfT", "tril"], writes=["atb00"])
                    pr.op("tensor", lambda E, r0=r0: E.matmul(
                        pstf[0:64, 64:128] if False else pstf[0:64, 0:64],
                        ksC[r0:r0 + 64, :], qsC[r0:r0 + 64, :],
                        start=True, stop=True, tile_position=(r0, 0)),
                        reads=["ksC", "qsC"], writes=["pstfT"], war=["atb00"])
                    pr.op("vector", lambda E: E.tensor_copy(atb[0:64, 64:128], pstf[0:64, 0:64]),
                          reads=["pstfT"], writes=["atb10"])
                    pr.op("tensor", lambda E, r0=r0: E.matmul(
                        pstf[64:128, 0:64], ksB[r0:r0 + 64, :], qsB[r0:r0 + 64, :],
                        start=True, stop=True, tile_position=(r0, 64)),
                        reads=["ksB", "qsB"], writes=["pstfT2"])
                    pr.op("vector", lambda E: E.tensor_mul(atb[64:128, 64:128],
                                                           pstf[64:128, 0:64], tril[64:128, 64:128]),
                          reads=["pstfT2", "tril"], writes=["atb11"])
                    pr.op("tensor", lambda E, vsl=vsl, r0=r0: E.matmul(
                        ps[2][r0:r0 + 64, 0:128], vsl, atb[:, :],
                        start=True, stop=False, tile_position=(0, r0)),
                        reads=["vtb", "atb00", "atb10", "atb11", "atbZ"], war=["ps2"])
                    pr.op("tensor", lambda E, p=p, r0=r0: E.matmul(
                        ps[2][r0:r0 + 64, 0:128], Sb3[r0:r0 + 64, p * 64:(p + 1) * 64],
                        qh[r0:r0 + 64, :], start=False, stop=True, tile_position=(r0, r0)),
                        reads=[f"Sb3_{p}_{r0}", "qh"], writes=["ps2"])
                    pr.op("vector", lambda E, knsl=knsl, h=h: E.tensor_mul(
                        khn[:, :], knsl, powK[:, h * 64:(h + 1) * 64]),
                        reads=["knb", "powKx"], writes=["khn"])
                    pr.op("tensor", lambda E, r0=r0, vsl=vsl: E.matmul(
                        ps[3][r0:r0 + 64, 0:64], khn[:, :], vsl,
                        start=True, stop=True, tile_position=(0, r0)),
                        reads=["khn", "vtb"], writes=["ps3"])
                    pr.op("vector", lambda E, p=p, r0=r0: E.scalar_tensor_tensor(
                        Sst3[r0:r0 + 64, p * 64:(p + 1) * 64],
                        Sst3[r0:r0 + 64, p * 64:(p + 1) * 64],
                        dccp[r0:r0 + 64, p:p + 1], ps[3][r0:r0 + 64, 0:64],
                        ALU.mult, ALU.add),
                        reads=[f"S3_{p}_{r0}", "dccp", "ps3"], writes=[f"S3_{p}_{r0}"])
                    pr.op("scalar", lambda E, p=p, r0=r0: E.copy(
                        Sb3[r0:r0 + 64, p * 64:(p + 1) * 64],
                        Sst3[r0:r0 + 64, p * 64:(p + 1) * 64]),
                        reads=[f"S3_{p}_{r0}"], writes=[f"Sb3_{p}_{r0}"])
                    pr.op("vector", lambda E, gsl=gsl, p=p, r0=r0, coff=coff: E.tensor_mul(
                        otb[r0:r0 + 64, p * GW + coff: p * GW + coff + 128],
                        ps[2][r0:r0 + 64, 0:128], gsl),
                        reads=["ps2", "gts"], writes=["otb"])
        for k in range(8):
            pr.op("sync", lambda E, k=k: E.dma_start(
                out=wob_s[:, :], in_=wo_bf[k * 128:(k + 1) * 128, :]),
                reads=["cwodone"], writes=["wob_s"], dma=True)
            for tt in range(GC):
                for c in range(2):
                    pj = 1 + tt * 2 + c
                    pr.op("tensor", lambda E, k=k, c=c, tt=tt, pj=pj: E.matmul(
                        ps[pj][:, :], otb[:, k * GW + tt * 128: k * GW + (tt + 1) * 128],
                        wob_s[:, c * 512:(c + 1) * 512],
                        start=(k == 0), stop=(k == 7)),
                        reads=["otb", "wob_s"],
                        writes=[f"ps{pj}"] if k == 7 else [], war=[f"ps{pj}"] if k == 0 else ())
        for tt in range(GC):
            i = g * GC + tt
            xt, xk = (xa, "xa") if i % 2 == 0 else (xb_, "xb_")
            pr.op("scalar", lambda E, i=i, xt=xt: E.dma_start(
                out=xt[:, :], in_=xn_d[i * 128:(i + 1) * 128, :]),
                reads=[f"xn_d{i}"], writes=[xk], dma=True)
            for c in range(2):
                pj = 1 + tt * 2 + c
                pr.op("vector", lambda E, c=c, pj=pj: E.tensor_add(
                    stile[:, c * 512:(c + 1) * 512], ps[pj][:, :], bob[:, c * 512:(c + 1) * 512]),
                    reads=[f"ps{pj}", "bob"], writes=["stile"])
            pr.op("vector", lambda E, xt=xt: E.tensor_add(stile[:, :], stile[:, :], xt[:, :]),
                  reads=["stile", xk], writes=["stile"])
            pr.op("sync", lambda E, i=i: E.dma_start(out=s_d[i * 128:(i + 1) * 128, :], in_=stile[:, :]),
                  reads=["stile"], writes=[f"s_d{i}"], dma=True)

    for tc in range(16):
        t0 = tc * 256
        for kk in range(32):
            wb_, wk_ = (w1cb, "w1cb") if kk % 2 == 0 else (w1cbB, "w1cbB")
            pp, pk = ("ps0", 0) if kk % 2 == 0 else ("ps5", 5)
            pr.op("sync", lambda E, kk=kk, wb_=wb_: E.dma_start(
                out=wb_[:, :].rearrange("p (a q) -> p a q", q=128),
                in_=w1_bf[0:D, kk * 128:(kk + 1) * 128].rearrange("(a p) q -> p a q", p=128)),
                reads=["cw1done"], writes=[wk_], dma=True)
            for k in range(8):
                pr.op("tensor", lambda E, kk=kk, k=k, t0=t0, wb_=wb_, pk=pk: E.matmul(
                    ps[pk][:, 0:256], wb_[:, k * 128:(k + 1) * 128],
                    xnT[:, k * T + t0: k * T + t0 + 256],
                    start=(k == 0), stop=(k == 7)),
                    reads=[wk_, f"xnT{k}"],
                    writes=[pp] if k == 7 else [], war=[pp] if k == 0 else ())
            pr.op("scalar", lambda E, kk=kk, pk=pk: E.activation(
                htb[:, kk * 256:(kk + 1) * 256], ps[pk][:, 0:256], AF.Relu,
                bias=b1c[:, kk:kk + 1]),
                reads=[pp, "b1c"], writes=[f"htb{kk}"])
        for tt in range(2):
            for c in range(2):
                for kk in range(32):
                    par = kk % 2
                    wsl = w2rb[:, par * 512:(par + 1) * 512]
                    wkk = f"w2rbh{par}"
                    pr.op("sync" if kk % 2 == 0 else "scalar",
                          lambda E, kk=kk, c=c, wsl=wsl: E.dma_start(
                        out=wsl, in_=w2_bf[kk * 128:(kk + 1) * 128, c * 512:(c + 1) * 512]),
                        reads=["cw2done"], writes=[wkk],
                        war=(["w2rb0", "w2rb1", "w2rb2", "w2rb3"] if wkk == "w2rbh0" else ()),
                        dma=True)
                    pr.op("tensor", lambda E, kk=kk, c=c, tt=tt, wsl=wsl: E.matmul(
                        ps[1][:, :], htb[:, kk * 256 + tt * 128: kk * 256 + (tt + 1) * 128],
                        wsl, start=(kk == 0), stop=(kk == 31)),
                        reads=[wkk, f"htb{kk}"],
                        writes=["ps1"] if kk == 31 else [], war=["ps1"] if kk == 0 else ())
                pr.op("vector", lambda E, c=c: E.tensor_add(
                    gtile[:, c * 512:(c + 1) * 512], ps[1][:, :], b2b[:, c * 512:(c + 1) * 512]),
                    reads=["ps1", "b2b"], writes=["gtile"])
            pr.op("scalar", lambda E: E.activation(gtile[:, :], gtile[:, :], AF.Silu),
                  reads=["gtile"], writes=["gtile"])
            pr.op("sync", lambda E, t0=t0, tt=tt: E.dma_start(
                out=g_d[t0 + tt * 128: t0 + (tt + 1) * 128, :], in_=gtile[:, :]),
                reads=["gtile"], writes=[f"g_d{2 * tc + tt}"], dma=True)

    for i in range(NT):
        xt, xk = (xa, "xa") if i % 2 == 0 else (xb_, "xb_")
        pr.op("sync", lambda E, i=i, xt=xt: E.dma_start(out=xt[:, :], in_=xin[i * 128:(i + 1) * 128, :]),
              writes=[xk], dma=True)
        pr.op("scalar", lambda E, i=i: E.dma_start(out=sgt[:, :], in_=s_d[i * 128:(i + 1) * 128, :]),
              reads=[f"s_d{i}"], writes=["sgt"], dma=True)
        pr.op("gpsimd", lambda E, i=i: E.dma_start(out=gtile[:, :], in_=g_d[i * 128:(i + 1) * 128, :]),
              reads=[f"g_d{i}"], writes=["gtile"], dma=True)
        pr.op("vector", lambda E: E.tensor_mul(sgt[:, :], sgt[:, :], gtile[:, :]),
              reads=["sgt", "gtile"], writes=["sgt"])
        pr.op("vector", lambda E, xt=xt: E.tensor_add(sgt[:, :], sgt[:, :], xt[:, :]),
              reads=["sgt", xk], writes=["sgt"])
        pr.op("sync", lambda E, i=i: E.dma_start(out=xout[i * 128:(i + 1) * 128, :], in_=sgt[:, :]),
              reads=["sgt"], writes=[f"xout{i}"], dma=True)
    return cx


def _host_consts_l3(inputs):
    def bc(v):
        return np.ascontiguousarray(np.broadcast_to(np.asarray(v, np.float32)[None, :], (128, len(np.asarray(v))))).astype(np.float32)
    def colt(v, n):
        return np.ascontiguousarray(np.asarray(v, np.float32).reshape(n, 128).T)
    s_i = np.arange(128)[:, None]; t_i = np.arange(128)[None, :]
    r64 = np.arange(64, dtype=np.float32)
    r128 = np.arange(128, dtype=np.float32)
    relrow = np.concatenate([np.broadcast_to(r64 - 32, (128, 64)),
                             np.broadcast_to(32 - r64, (128, 64)),
                             np.broadcast_to(r64, (128, 64)),
                             np.broadcast_to(64 - r64, (128, 64)),
                             np.broadcast_to(r128 + 1, (128, 128))], axis=1).astype(np.float32)
    return {
        "wqt": np.ascontiguousarray(np.asarray(inputs["L3_attn_WQ"], np.float32).T),
        "wkt": np.ascontiguousarray(np.asarray(inputs["L3_attn_WK"], np.float32).T),
        "wvt": np.ascontiguousarray(np.asarray(inputs["L3_attn_WV"], np.float32).T),
        "wgt": np.ascontiguousarray(np.asarray(inputs["L3_attn_Wg"], np.float32).T),
        "wot": np.ascontiguousarray(np.asarray(inputs["L3_attn_WO"], np.float32).T),
        "w1": np.asarray(inputs["L3_ffn_W1"], np.float32),
        "w2": np.asarray(inputs["L3_ffn_W2"], np.float32),
        "alogit": np.asarray(inputs["L3_attn_alpha_logit"], np.float32),
        "bqc_d": colt(inputs["L3_attn_bQ"], 8),
        "bkc_d": colt(inputs["L3_attn_bK"], 8),
        "bk_d": bc(inputs["L3_attn_bK"]),
        "bgc_d": colt(inputs["L3_attn_bg"], 8),
        "b1c_d": colt(inputs["L3_ffn_b1"], 32),
        "bv_d": bc(inputs["L3_attn_bV"]),
        "bo_d": bc(inputs["L3_attn_bO"]),
        "b2_d": bc(inputs["L3_ffn_b2"]),
        "lng_d": bc(inputs["L3_attn_ln_gamma"]),
        "lnb_d": bc(inputs["L3_attn_ln_beta"]),
        "identb3": np.eye(128, dtype=BFNP),
        "identf3": np.eye(128, dtype=np.float32),
        "tril01": (s_i <= t_i).astype(np.float32),
        "relrow": relrow,
        "relkrow": np.broadcast_to(127.0 - r128, (64, 128)).astype(np.float32).copy(),
    }


def build_l2(cx=None, xin_t=None, xout_t=None):
    standalone = cx is None
    if standalone:
        cx = Ctx()
    pr = cx.pr
    NSt = 16

    xin = xin_t if xin_t is not None else cx.din("xin", [T, D])
    w1w = cx.din("w1w", [D, D]); v1w = cx.din("v1w", [D, D])
    w2w = cx.din("w2w", [D, D]); dtw = cx.din("dtw", [D, D])
    bpw = cx.din("bpw", [D, NSt]); cpw = cx.din("cpw", [D, NSt])
    alog = cx.din("alog", [D, NSt])
    ff1w = cx.din("ff1w", [D, 3 * D]); ff2w = cx.din("ff2w", [3 * D, D])
    qw = cx.din("qw", [D, D])
    convw = cx.din("convw", [128, 24])
    dpc_d = cx.din("dpc_d", [128, 8])
    w1b_d = cx.din("w1b_d", [128, 8]); v1b_d = cx.din("v1b_d", [128, 8])
    dtb_d = cx.din("dtb_d", [128, 8])
    bpb_d = cx.din("bpb_d", [16, 2])
    ff1b_d = cx.din("ff1b_d", [128, 24])
    w2b_d = cx.din("w2b_d", [128, D]); ff2b_d = cx.din("ff2b_d", [128, D])
    qb_d = cx.din("qb_d", [128, D])
    ln1g_d = cx.din("ln1g_d", [128, D]); ln1b_d = cx.din("ln1b_d", [128, D])
    lng_d = cx.din("lng_d", [128, D]); lnb_d = cx.din("lnb_d", [128, D])
    ln2g_d = cx.din("ln2g_d", [128, D]); ln2b_d = cx.din("ln2b_d", [128, D])
    oh_d = cx.din("oh_d", [16, 16 * 128], BF16)
    identb2 = cx.din("identb2", [128, 128], BF16)
    xout = xout_t if xout_t is not None else cx.dout("xout", [T, D])

    w1_bf = cx.dint("w1_bf", [D, D], BF16); v1_bf = cx.dint("v1_bf", [D, D], BF16)
    w2_bf = cx.dint("w2_bf", [D, D], BF16); dt_bf = cx.dint("dt_bf", [D, D], BF16)
    bp_bf = cx.dint("bp_bf", [D, NSt], BF16); cp_bf = cx.dint("cp_bf", [D, NSt], BF16)
    ff1_bf = cx.dint("ff1_bf", [D, 3 * D], BF16)
    ff2_bf = cx.dint("ff2_bf", [3 * D, D], BF16)
    xn_d = cx.dint("xn_d", [T, D])
    xmT_d = cx.dint("xmT_d", [D, T], BF16)
    xcT_d = cx.dint("xcT_d", [D, T], BF16)
    del_d = cx.dint("del_d", [D, T])
    vT_d = cx.dint("vT_d", [D, T], BF16)
    yvT_d = cx.dint("yvT_d", [D, T], BF16)
    x2m_d = cx.dint("x2m_d", [T, D])
    xn2T_d = cx.dint("xn2T_d", [D, T], BF16)
    s_d = cx.dint("s_d", [T, D]); g_d = cx.dint("g_d", [T, D])
    wdT_d = cx.dint("wdT_d", [D, D], BF16)

    xa = cx.sb("xa", [128, D]); xb_ = cx.sb("xb_", [128, D])
    sq = cx.sb("sq", [128, D]); cols = cx.sb("cols", [128, 4])
    xnf = cx.sb("xnf", [128, D]); xmf = cx.sb("xmf", [128, D])
    xnb2 = cx.sb("xnb2", [128, 2 * D], BF16)
    bb0 = cx.sb("bb0", [128, D]); bb1 = cx.sb("bb1", [128, D])
    bb2 = cx.sb("bb2", [128, D]); bb3 = cx.sb("bb3", [128, D])
    ln1g, ln1b, lng, lnb = bb0, bb1, bb2, bb3
    ln2g, ln2b, w2b = bb0, bb1, bb2
    ff2b = bb3
    qbb = bb0
    convs = cx.sb("convs", [128, 24]); dpc = cx.sb("dpc", [128, 8])
    w1bc = cx.sb("w1bc", [128, 8]); v1bc = cx.sb("v1bc", [128, 8])
    dtbc = cx.sb("dtbc", [128, 8]); bpbc = cx.sb("bpbc", [16, 2])
    ff1bc = cx.sb("ff1bc", [128, 24])
    ohb = cx.sb("ohb", [16, 16 * 128], BF16)
    idb = cx.sb("idb", [128, 128], BF16)
    negA = cx.sb("negA", [128, 8 * 16])
    wrb = cx.sb("wrb", [128, D], BF16)
    wcb = cx.sb("wcb", [128, 8 * 128], BF16)
    wcbB = cx.sb("wcbB", [128, 8 * 128], BF16)
    x1t = cx.sb("x1t", [128, T])
    btTb = cx.sb("btTb", [16, T], BF16); ctTb = cx.sb("ctTb", [16, T], BF16)
    delg = cx.sb("delg", [128, T])
    xcg = cx.sb("xcg", [128, T], BF16)
    ubf = cx.sb("ubf", [128, T], BF16)
    dax = cx.sb("dax", [128, T])
    dbx = x1t
    hsc = cx.sb("hsc", [128, T])
    yg = cx.sb("yg", [128, T])
    vtg = cx.sb("vtg", [128, T], BF16)
    htb = cx.sb("htb", [128, 24 * 256], BF16)
    stile = cx.sb("stile", [128, D]); gtile = cx.sb("gtile", [128, D])
    sgt = cx.sb("sgt", [128, D])
    qx = xnf; qu = xmf
    qsc = cx.sb("qsc", [128, 64]); qm2 = sq; qstep = cx.sb("qstep", [128, D])
    qbf = cx.sb("qbf", [128, D], BF16)
    qxt = cx.sb("qxt", [128, 8 * 128], BF16)

    ps = [cx.ps(f"ps{i}", [128, 512]) for i in range(6)]
    pstb = cx.ps("pstb", [128, 128], BF16)

    for name, dst, k in [("ln1g_d", ln1g, "bb0"), ("ln1b_d", ln1b, "bb1"),
                         ("lng_d", lng, "bb2"), ("lnb_d", lnb, "bb3"),
                         ("convw", convs, "convs"),
                         ("dpc_d", dpc, "dpc"), ("w1b_d", w1bc, "w1bc"),
                         ("v1b_d", v1bc, "v1bc"), ("dtb_d", dtbc, "dtbc"),
                         ("bpb_d", bpbc, "bpbc"), ("ff1b_d", ff1bc, "ff1bc"),
                         ("oh_d", ohb, "ohb"), ("identb2", idb, "idb")]:
        pr.op("sync", lambda E, s=cx.t[name], d=dst: E.dma_start(out=d[:, :], in_=s[:, :]),
              writes=[k], dma=True)
    for src_, dst_, tag in [(w1w, w1_bf, "cw1w"), (v1w, v1_bf, "cv1"),
                            (w2w, w2_bf, "cw2w"), (dtw, dt_bf, "cdt"),
                            (bpw, bp_bf, "cbp"), (cpw, cp_bf, "ccp"),
                            (ff1w, ff1_bf, "cff1"), (ff2w, ff2_bf, "cff2")]:
        cast_weight2(cx, src_, dst_, src_.shape[0], src_.shape[1], tag,
                     bufs=(stile, "stile", qbf, "qbf"))
    for g in range(8):
        pr.op("sync", lambda E, g=g: E.dma_start(out=negA[:, g * 16:(g + 1) * 16],
                                                 in_=alog[g * 128:(g + 1) * 128, :]),
              writes=["negA"], dma=True)
    pr.op("scalar", lambda E: E.activation(negA[:, :], negA[:, :], AF.Exp),
          reads=["negA"], writes=["negA"])
    pr.op("vector", lambda E: E.tensor_scalar(negA[:, :], negA[:, :], -1.0, None, ALU.mult),
          reads=["negA"], writes=["negA"])

    for i in range(NT):
        xt, xk = (xa, "xa") if i % 2 == 0 else (xb_, "xb_")
        pr.op("sync", lambda E, i=i, xt=xt: E.dma_start(out=xt[:, :], in_=xin[i * 128:(i + 1) * 128, :]),
              writes=[xk], dma=True)
        ln_tile(pr, xt[:, :], [xk], xnf[:, :], "xnf", cols, "cols", sq[:, :], "sq")
        pr.op("gpsimd", lambda E, i=i: E.dma_start(out=xn_d[i * 128:(i + 1) * 128, :], in_=xnf[:, :]),
              reads=["xnf"], writes=[f"xn_d{i}"], dma=True)
        pr.op("vector", lambda E: E.tensor_mul(xmf[:, :], xnf[:, :], ln1g[:, :]),
              reads=["xnf", "bb0"], writes=["xmf"])
        pr.op("vector", lambda E: E.tensor_add(xmf[:, :], xmf[:, :], ln1b[:, :]),
              reads=["xmf", "bb1"], writes=["xmf"])
        ln_tile(pr, xmf[:, :], ["xmf"], xmf[:, :], "xmf", cols, "cols", sq[:, :], "sq")
        pr.op("vector", lambda E: E.tensor_mul(xmf[:, :], xmf[:, :], lng[:, :]),
              reads=["xmf", "bb2"], writes=["xmf"])
        pr.op("vector", lambda E: E.tensor_add(xmf[:, :], xmf[:, :], lnb[:, :]),
              reads=["xmf", "bb3"], writes=["xmf"])
        half = (i % 2) * D
        hk = f"xm2h{i % 2}"
        pr.op("scalar", lambda E, half=half: E.copy(xnb2[:, half:half + D], xmf[:, :]),
              reads=["xmf"], writes=[hk])
        for j in range(ND):
            pr.op("tensor", lambda E, half=half, j=j: E.transpose(
                pstb[:, :], xnb2[:, half + j * 128: half + (j + 1) * 128], idb[:, :]),
                reads=[hk, "idb"], writes=["pstb"])
            pr.op("vector", lambda E: E.tensor_copy(qbf[:, 0:128], pstb[:, :]),
                  reads=["pstb"], writes=["qbf"])
            pr.op("gpsimd", lambda E, i=i, j=j: E.dma_start(
                out=xmT_d[j * 128:(j + 1) * 128, i * 128:(i + 1) * 128], in_=qbf[:, 0:128]),
                reads=["qbf"], writes=[f"xmT{j}"], dma=True)

    rgA = cx.sb("rgA", [128, 512], BF16); rgB = cx.sb("rgB", [128, 512], BF16)

    def dproj(w_bf, wtag, in_dram, in_tag, out_w, bias_sbcol, fin_cb, wname):
        njt = (out_w + 127) // 128
        for j in range(njt):
            po = min(128, out_w - j * 128)
            pr.op("sync", lambda E, j=j, po=po, w_bf=w_bf: E.dma_start(
                out=wcb[:, 0:8 * po].rearrange("p (a q) -> p a q", q=po),
                in_=w_bf[0:D, j * 128:j * 128 + po].rearrange("(a p) q -> p a q", p=128)),
                reads=[wtag + "done"], writes=["wcb"], dma=True)
            for tc in range(8):
                for k in range(8):
                    rb, rk = (rgA, "rgA") if k % 2 == 0 else (rgB, "rgB")
                    pr.op("sync" if k % 2 == 0 else "scalar",
                          lambda E, k=k, tc=tc, rb=rb, in_dram=in_dram: E.dma_start(
                        out=rb[:, :], in_=in_dram[k * 128:(k + 1) * 128, tc * 512:(tc + 1) * 512]),
                        reads=[in_tag + str(k)], writes=[rk], dma=True)
                    pr.op("tensor", lambda E, k=k, po=po, rb=rb: E.matmul(
                        ps[0][0:po, :], wcb[:, k * po:(k + 1) * po], rb[:, :],
                        start=(k == 0), stop=(k == 7)),
                        reads=["wcb", rk], writes=["ps0"] if k == 7 else [],
                        war=["ps0"] if k == 0 else ())
                if bias_sbcol is None:
                    pr.op("vector", lambda E, tc=tc, po=po: E.tensor_copy(
                        x1t[0:po, tc * 512:(tc + 1) * 512], ps[0][0:po, :]),
                        reads=["ps0"], writes=["x1t"])
                else:
                    bsb, bkey, = bias_sbcol
                    pr.op("vector", lambda E, tc=tc, po=po, bsb=bsb, j=j: E.tensor_scalar(
                        x1t[0:po, tc * 512:(tc + 1) * 512], ps[0][0:po, :],
                        bsb[0:po, j:j + 1], None, ALU.add),
                        reads=["ps0", bkey], writes=["x1t"])
            fin_cb(j)

    def fin_w1(j):
        pr.op("vector", lambda E, j=j: E.tensor_scalar(
            dax[:, :], x1t[:, :], convs[:, 3 * j + 1:3 * j + 2], None, ALU.mult),
            reads=["x1t", "convs"], writes=["dax"])
        pr.op("vector", lambda E, j=j: E.scalar_tensor_tensor(
            dax[:, 1:T], x1t[:, 0:T - 1], convs[:, 3 * j:3 * j + 1], dax[:, 1:T],
            ALU.mult, ALU.add),
            reads=["x1t", "convs", "dax"], writes=["dax"])
        pr.op("vector", lambda E, j=j: E.scalar_tensor_tensor(
            dax[:, 0:T - 1], x1t[:, 1:T], convs[:, 3 * j + 2:3 * j + 3], dax[:, 0:T - 1],
            ALU.mult, ALU.add),
            reads=["x1t", "convs", "dax"], writes=["dax"])
        pr.op("scalar", lambda E: E.activation(dax[:, :], dax[:, :], AF.Silu),
              reads=["dax"], writes=["dax"])
        pr.op("vector", lambda E: E.tensor_copy(xcg[:, :], dax[:, :]),
              reads=["dax"], writes=["xcg"])
        pr.op("gpsimd", lambda E, j=j: E.dma_start(out=xcT_d[j * 128:(j + 1) * 128, :], in_=xcg[:, :]),
              reads=["xcg"], writes=[f"xcT{j}"], dma=True)

    dproj(w1_bf, "cw1w", xmT_d, "xmT", D, (w1bc, "w1bc"), fin_w1, "w1")

    def fin_v(j):
        pr.op("scalar", lambda E: E.activation(x1t[:, :], x1t[:, :], AF.Silu),
              reads=["x1t"], writes=["x1t"])
        pr.op("vector", lambda E: E.tensor_copy(xcg[:, :], x1t[:, :]),
              reads=["x1t"], writes=["xcg"])
        pr.op("gpsimd", lambda E, j=j: E.dma_start(out=vT_d[j * 128:(j + 1) * 128, :], in_=xcg[:, :]),
              reads=["xcg"], writes=[f"vT{j}"], dma=True)

    dproj(v1_bf, "cv1", xmT_d, "xmT", D, (v1bc, "v1bc"), fin_v, "v1")

    def fin_dt(j):
        pr.op("scalar", lambda E: E.activation(sq[:, 0:T] if False else dax[:, :], x1t[:, :], AF.Abs),
              reads=["x1t"], writes=["dax"])
        pr.op("scalar", lambda E: E.activation(dax[:, :], dax[:, :], AF.Exp, scale=-1.0),
              reads=["dax"], writes=["dax"])
        pr.op("scalar", lambda E: E.activation(dax[:, :], dax[:, :], AF.Ln, bias=1.0),
              reads=["dax"], writes=["dax"])
        pr.op("vector", lambda E: E.tensor_scalar(hsc[:, :], x1t[:, :], 0.0, None, ALU.max),
              reads=["x1t"], writes=["hsc"])
        pr.op("vector", lambda E: E.tensor_add(dax[:, :], dax[:, :], hsc[:, :]),
              reads=["dax", "hsc"], writes=["dax"])
        pr.op("gpsimd", lambda E, j=j: E.dma_start(out=del_d[j * 128:(j + 1) * 128, :], in_=dax[:, :]),
              reads=["dax"], writes=[f"del{j}"], dma=True)

    dproj(dt_bf, "cdt", xcT_d, "xcT", D, (dtbc, "dtbc"), fin_dt, "dt")

    for (wb_, wt_, dstt, dkey, bcol) in [(bp_bf, "cbp", btTb, "btTb", 0), (cp_bf, "ccp", ctTb, "ctTb", 1)]:
        pr.op("sync", lambda E, wb_=wb_: E.dma_start(
            out=wcb[:, 0:8 * 16].rearrange("p (a q) -> p a q", q=16),
            in_=wb_[0:D, 0:16].rearrange("(a p) q -> p a q", p=128)),
            reads=[wt_ + "done"], writes=["wcb"], dma=True)
        for tc in range(8):
            for k in range(8):
                rb, rk = (rgA, "rgA") if k % 2 == 0 else (rgB, "rgB")
                pr.op("sync", lambda E, k=k, tc=tc, rb=rb: E.dma_start(
                    out=rb[:, :], in_=xcT_d[k * 128:(k + 1) * 128, tc * 512:(tc + 1) * 512]),
                    reads=[f"xcT{k}"], writes=[rk], dma=True)
                pr.op("tensor", lambda E, k=k, rb=rb: E.matmul(
                    ps[0][0:16, :], wcb[:, k * 16:(k + 1) * 16], rb[:, :],
                    start=(k == 0), stop=(k == 7)),
                    reads=["wcb", rk], writes=["ps0"] if k == 7 else [],
                    war=["ps0"] if k == 0 else ())
            pr.op("vector", lambda E, tc=tc, dstt=dstt, bcol=bcol: E.tensor_scalar(
                dstt[:, tc * 512:(tc + 1) * 512], ps[0][0:16, :],
                bpbc[:, bcol:bcol + 1], None, ALU.add),
                reads=["ps0", "bpbc"], writes=[dkey])

    for g in range(8):
        pr.op("sync", lambda E, g=g: E.dma_start(out=delg[:, :], in_=del_d[g * 128:(g + 1) * 128, :]),
              reads=[f"del{g}"], writes=["delg"], dma=True)
        pr.op("sync", lambda E, g=g: E.dma_start(out=xcg[:, :], in_=xcT_d[g * 128:(g + 1) * 128, :]),
              reads=[f"xcT{g}"], writes=["xcg"], dma=True)
        pr.op("vector", lambda E: E.tensor_mul(ubf[:, :], delg[:, :], xcg[:, :]),
              reads=["delg", "xcg"], writes=["ubf"])
        pr.op("gpsimd", lambda E: E.memset(yg[:, :], 0.0), writes=["yg"])
        for n in range(NSt):
            pr.op("scalar", lambda E, g=g, n=n: E.activation(
                dax[:, :], delg[:, :], AF.Exp, scale=negA[:, g * 16 + n: g * 16 + n + 1]),
                reads=["delg", "negA"], writes=["dax"])
            for c in range(8):
                pb, pbi = ("ps1", 1) if c % 2 == 0 else ("ps4", 4)
                pr.op("tensor", lambda E, n=n, c=c, pbi=pbi: E.matmul(
                    ps[pbi][:, :], ohb[:, n * 128:(n + 1) * 128],
                    btTb[:, c * 512:(c + 1) * 512], start=True, stop=True),
                    reads=["ohb", "btTb"], writes=[pb])
                pr.op("vector", lambda E, c=c, pbi=pbi: E.tensor_mul(
                    dbx[:, c * 512:(c + 1) * 512], ps[pbi][:, :], ubf[:, c * 512:(c + 1) * 512]),
                    reads=[pb, "ubf"], writes=["dbx"])
            pr.op("vector", lambda E: E.tensor_tensor_scan(
                hsc[:, :], dax[:, :], dbx[:, :], 0.0, ALU.mult, ALU.add),
                reads=["dax", "dbx"], writes=["hsc"])
            for c in range(8):
                pc_, pci = ("ps2", 2) if c % 2 == 0 else ("ps5", 5)
                qsl2 = qstep[:, 0:512] if c % 2 == 0 else qstep[:, 512:1024]
                qk2 = "qstep" if c % 2 == 0 else "qstepB"
                pr.op("tensor", lambda E, n=n, c=c, pci=pci: E.matmul(
                    ps[pci][:, :], ohb[:, n * 128:(n + 1) * 128],
                    ctTb[:, c * 512:(c + 1) * 512], start=True, stop=True),
                    reads=["ohb", "ctTb"], writes=[pc_])
                pr.op("vector", lambda E, c=c, pci=pci, qsl2=qsl2: E.tensor_mul(
                    qsl2, ps[pci][:, :], hsc[:, c * 512:(c + 1) * 512]),
                    reads=[pc_, "hsc"], writes=[qk2])
                pr.op("vector", lambda E, c=c, qsl2=qsl2: E.tensor_add(
                    yg[:, c * 512:(c + 1) * 512], yg[:, c * 512:(c + 1) * 512], qsl2),
                    reads=["yg", qk2], writes=["yg"])
        pr.op("vector", lambda E, g=g: E.scalar_tensor_tensor(
            yg[:, :], xcg[:, :], dpc[:, g:g + 1], yg[:, :], ALU.mult, ALU.add),
            reads=["xcg", "dpc", "yg"], writes=["yg"])
        pr.op("sync", lambda E, g=g: E.dma_start(out=vtg[:, :], in_=vT_d[g * 128:(g + 1) * 128, :]),
              reads=[f"vT{g}"], writes=["vtg"], dma=True)
        pr.op("vector", lambda E: E.tensor_mul(ubf[:, :], yg[:, :], vtg[:, :]),
              reads=["yg", "vtg"], writes=["ubf"])
        pr.op("gpsimd", lambda E, g=g: E.dma_start(out=yvT_d[g * 128:(g + 1) * 128, :], in_=ubf[:, :]),
              reads=["ubf"], writes=[f"yvT{g}"], dma=True)

    for name, dst, k in [("ln2g_d", ln2g, "bb0"), ("ln2b_d", ln2b, "bb1"),
                         ("w2b_d", w2b, "bb2")]:
        pr.op("sync", lambda E, s=cx.t[name], d=dst: E.dma_start(out=d[:, :], in_=s[:, :]),
              writes=[k], dma=True)
    for i in range(NT):
        for c in range(2):
            for k in range(8):
                rb, rk = (rgA, "rgA") if k % 2 == 0 else (rgB, "rgB")
                pr.op("sync", lambda E, i=i, k=k, rb=rb: E.dma_start(
                    out=rb[:, 0:128], in_=yvT_d[k * 128:(k + 1) * 128, i * 128:(i + 1) * 128]),
                    reads=[f"yvT{k}"], writes=[rk], dma=True)
                par = k % 2
                wsl = wrb[:, par * 512:(par + 1) * 512]
                wkk = f"wrbh{par}"
                pr.op("scalar", lambda E, k=k, c=c, wsl=wsl: E.dma_start(
                    out=wsl, in_=w2_bf[k * 128:(k + 1) * 128, c * 512:(c + 1) * 512]),
                    reads=["cw2wdone"], writes=[wkk], dma=True)
                pr.op("tensor", lambda E, k=k, c=c, rb=rb, wsl=wsl: E.matmul(
                    ps[3][:, :], rb[:, 0:128], wsl,
                    start=(k == 0), stop=(k == 7)),
                    reads=[rk, wkk], writes=["ps3"] if k == 7 else [],
                    war=["ps3"] if k == 0 else ())
            pr.op("vector", lambda E, c=c: E.tensor_add(
                stile[:, c * 512:(c + 1) * 512], ps[3][:, :], w2b[:, c * 512:(c + 1) * 512]),
                reads=["ps3", "bb2"], writes=["stile"])
        xt, xk = (xa, "xa") if i % 2 == 0 else (xb_, "xb_")
        pr.op("sync", lambda E, i=i, xt=xt: E.dma_start(out=xt[:, :], in_=xn_d[i * 128:(i + 1) * 128, :]),
              reads=[f"xn_d{i}"], writes=[xk], dma=True)
        pr.op("vector", lambda E, xt=xt: E.tensor_add(stile[:, :], stile[:, :], xt[:, :]),
              reads=["stile", xk], writes=["stile"])
        pr.op("gpsimd", lambda E, i=i: E.dma_start(out=x2m_d[i * 128:(i + 1) * 128, :], in_=stile[:, :]),
              reads=["stile"], writes=[f"x2m{i}"], dma=True)
        ln_tile(pr, stile[:, :], ["stile"], xnf[:, :], "xnf", cols, "cols", sq[:, :], "sq")
        pr.op("vector", lambda E: E.tensor_mul(xnf[:, :], xnf[:, :], ln2g[:, :]),
              reads=["xnf", "bb0"], writes=["xnf"])
        pr.op("vector", lambda E: E.tensor_add(xnf[:, :], xnf[:, :], ln2b[:, :]),
              reads=["xnf", "bb1"], writes=["xnf"])
        half = (i % 2) * D
        hk = f"x2n2h{i % 2}"
        pr.op("scalar", lambda E, half=half: E.copy(xnb2[:, half:half + D], xnf[:, :]),
              reads=["xnf"], writes=[hk])
        for j in range(ND):
            pr.op("tensor", lambda E, half=half, j=j: E.transpose(
                pstb[:, :], xnb2[:, half + j * 128: half + (j + 1) * 128], idb[:, :]),
                reads=[hk, "idb"], writes=["pstb"])
            pr.op("vector", lambda E: E.tensor_copy(qbf[:, 0:128], pstb[:, :]),
                  reads=["pstb"], writes=["qbf"])
            pr.op("gpsimd", lambda E, i=i, j=j: E.dma_start(
                out=xn2T_d[j * 128:(j + 1) * 128, i * 128:(i + 1) * 128], in_=qbf[:, 0:128]),
                reads=["qbf"], writes=[f"xn2T{j}"], dma=True)

    pr.op("sync", lambda E: E.dma_start(out=ff2b[:, :], in_=ff2b_d[:, :]), writes=["bb3"], dma=True)
    for tc in range(16):
        t0 = tc * 256
        for kk in range(24):
            wb_, wk_ = (wcb, "wcb") if kk % 2 == 0 else (wcbB, "wcbB")
            pp, pk = ("ps0", 0) if kk % 2 == 0 else ("ps4", 4)
            pr.op("sync", lambda E, kk=kk, wb_=wb_: E.dma_start(
                out=wb_[:, 0:8 * 128].rearrange("p (a q) -> p a q", q=128),
                in_=ff1_bf[0:D, kk * 128:(kk + 1) * 128].rearrange("(a p) q -> p a q", p=128)),
                reads=["cff1done"], writes=[wk_], dma=True)
            for k in range(8):
                rb, rk = (rgA, "rgA") if k % 2 == 0 else (rgB, "rgB")
                pr.op("sync", lambda E, k=k, t0=t0, rb=rb: E.dma_start(
                    out=rb[:, 0:256], in_=xn2T_d[k * 128:(k + 1) * 128, t0:t0 + 256]),
                    reads=[f"xn2T{k}"], writes=[rk], dma=True)
                pr.op("tensor", lambda E, kk=kk, k=k, rb=rb, wb_=wb_, pk=pk: E.matmul(
                    ps[pk][:, 0:256], wb_[:, k * 128:(k + 1) * 128], rb[:, 0:256],
                    start=(k == 0), stop=(k == 7)),
                    reads=[wk_, rk], writes=[pp] if k == 7 else [],
                    war=[pp] if k == 0 else ())
            pr.op("scalar", lambda E, kk=kk, pk=pk: E.activation(
                htb[:, kk * 256:(kk + 1) * 256], ps[pk][:, 0:256], AF.Gelu,
                bias=ff1bc[:, kk:kk + 1]),
                reads=[pp, "ff1bc"], writes=[f"htb{kk}"])
        for tt in range(2):
            for c in range(2):
                for kk in range(24):
                    par = kk % 2
                    wsl = wrb[:, par * 512:(par + 1) * 512]
                    wkk = f"wrbh{par}"
                    pr.op("sync" if kk % 2 == 0 else "scalar",
                          lambda E, kk=kk, c=c, wsl=wsl: E.dma_start(
                        out=wsl, in_=ff2_bf[kk * 128:(kk + 1) * 128, c * 512:(c + 1) * 512]),
                        reads=["cff2done"], writes=[wkk], dma=True)
                    pr.op("tensor", lambda E, kk=kk, c=c, tt=tt, wsl=wsl: E.matmul(
                        ps[1][:, :], htb[:, kk * 256 + tt * 128: kk * 256 + (tt + 1) * 128],
                        wsl, start=(kk == 0), stop=(kk == 23)),
                        reads=[wkk, f"htb{kk}"], writes=["ps1"] if kk == 23 else [],
                        war=["ps1"] if kk == 0 else ())
                pr.op("vector", lambda E, c=c: E.tensor_add(
                    stile[:, c * 512:(c + 1) * 512], ps[1][:, :], ff2b[:, c * 512:(c + 1) * 512]),
                    reads=["ps1", "bb3"], writes=["stile"])
            i = 2 * tc + tt
            xt, xk = (xa, "xa") if i % 2 == 0 else (xb_, "xb_")
            pr.op("scalar", lambda E, i=i, xt=xt: E.dma_start(
                out=xt[:, :], in_=x2m_d[i * 128:(i + 1) * 128, :]),
                reads=[f"x2m{i}"], writes=[xk], dma=True)
            pr.op("vector", lambda E, xt=xt: E.tensor_add(stile[:, :], stile[:, :], xt[:, :]),
                  reads=["stile", xk], writes=["stile"])
            pr.op("sync", lambda E, i=i: E.dma_start(out=s_d[i * 128:(i + 1) * 128, :], in_=stile[:, :]),
                  reads=["stile"], writes=[f"s_d{i}"], dma=True)

    FP6MAX = 7.5
    def emit_quant(in_ap, in_keys, rows, out_bf_ap, outkey):
        pr.op("scalar", lambda E: E.activation(qu[0:rows, :], in_ap, AF.Abs),
              reads=in_keys, writes=["xmf"])
        pr.op("vector", lambda E: E.reduce_max(qsc[0:rows, 0:32],
                                               qu[0:rows, :].rearrange("p (a b) -> p a b", b=32), axis=AXX),
              reads=["xmf"], writes=["qsc"])
        pr.op("vector", lambda E: E.tensor_scalar(qsc[0:rows, 0:32], qsc[0:rows, 0:32],
                                                  1.0 / FP6MAX, 1e-30, ALU.mult, ALU.max),
              reads=["qsc"], writes=["qsc"])
        pr.op("vector", lambda E: E.reciprocal(qsc[0:rows, 32:64], qsc[0:rows, 0:32]),
              reads=["qsc"], writes=["qsc"])
        qse = cx.t["dax"]
        rse = cx.t["hsc"]
        for b in range(32):
            pr.op("vector", lambda E, b=b: E.tensor_copy(
                qse[0:rows, 0:D].rearrange("p (a b2) -> p a b2", b2=32)[:, :, b:b + 1],
                qsc[0:rows, 0:32].rearrange("p (a o) -> p a o", o=1)),
                reads=["qsc"], writes=["dax"])
            pr.op("vector", lambda E, b=b: E.tensor_copy(
                rse[0:rows, 0:D].rearrange("p (a b2) -> p a b2", b2=32)[:, :, b:b + 1],
                qsc[0:rows, 32:64].rearrange("p (a o) -> p a o", o=1)),
                reads=["qsc"], writes=["hsc"])
        pr.op("vector", lambda E: E.tensor_tensor(qx[0:rows, :], in_ap, rse[0:rows, 0:D],
                                                  ALU.mult),
              reads=in_keys + ["hsc"], writes=["xnf"])
        pr.op("scalar", lambda E: E.activation(qu[0:rows, :], qx[0:rows, :], AF.Abs),
              reads=["xnf"], writes=["xmf"])
        pr.op("vector", lambda E: E.tensor_scalar(qm2[0:rows, :], qu[0:rows, :],
                                                  1.9375, None, ALU.is_gt),
              reads=["xmf"], writes=["sq"])
        pr.op("vector", lambda E: E.tensor_scalar(qstep[0:rows, :], qm2[0:rows, :],
                                                  0.125, 0.125, ALU.mult, ALU.add),
              reads=["sq"], writes=["qstep"])
        pr.op("vector", lambda E: E.tensor_scalar(qu[0:rows, :], qm2[0:rows, :],
                                                  -4.0, 8.0, ALU.mult, ALU.add),
              reads=["sq"], writes=["xmf"])
        pr.op("vector", lambda E: E.tensor_scalar(qm2[0:rows, :], qm2[0:rows, :], 0.0,
                                                  None, ALU.mult),
              reads=["sq"], writes=["sq"]) if False else None
        pr.op("scalar", lambda E: E.activation(qm2[0:rows, :], qx[0:rows, :], AF.Abs),
              reads=["xnf"], writes=["sq"])
        pr.op("vector", lambda E: E.tensor_scalar(qm2[0:rows, :], qm2[0:rows, :],
                                                  3.875, None, ALU.is_gt),
              reads=["sq"], writes=["sq"])
        pr.op("vector", lambda E: E.scalar_tensor_tensor(qstep[0:rows, :], qm2[0:rows, :],
                                                         0.25, qstep[0:rows, :], ALU.mult, ALU.add),
              reads=["sq", "qstep"], writes=["qstep"])
        pr.op("vector", lambda E: E.scalar_tensor_tensor(qu[0:rows, :], qm2[0:rows, :],
                                                         -2.0, qu[0:rows, :], ALU.mult, ALU.add),
              reads=["sq", "xmf"], writes=["xmf"])
        pr.op("vector", lambda E: E.tensor_mul(qx[0:rows, :], qx[0:rows, :], qu[0:rows, :]),
              reads=["xnf", "xmf"], writes=["xnf"])
        MAGIC = 12582912.0
        pr.op("vector", lambda E: E.tensor_scalar(qx[0:rows, :], qx[0:rows, :],
                                                  MAGIC, None, ALU.add),
              reads=["xnf"], writes=["xnf"])
        pr.op("vector", lambda E: E.tensor_scalar(qx[0:rows, :], qx[0:rows, :],
                                                  MAGIC, None, ALU.subtract),
              reads=["xnf"], writes=["xnf"])
        pr.op("vector", lambda E: E.tensor_mul(qx[0:rows, :], qx[0:rows, :], qstep[0:rows, :]),
              reads=["xnf", "qstep"], writes=["xnf"])
        pr.op("vector", lambda E: E.tensor_scalar(qx[0:rows, :], qx[0:rows, :],
                                                  FP6MAX, -FP6MAX, ALU.min, ALU.max),
              reads=["xnf"], writes=["xnf"])
        pr.op("vector", lambda E: E.tensor_tensor(out_bf_ap, qx[0:rows, :],
                                                  qse[0:rows, 0:D], ALU.mult),
              reads=["xnf", "dax"], writes=[outkey])

    pr.op("sync", lambda E: E.dma_start(out=qbb[:, :], in_=qb_d[:, :]), writes=["bb0"], dma=True)
    for k in range(8):
        pr.op("sync", lambda E, k=k: E.dma_start(out=xa[:, :], in_=qw[k * 128:(k + 1) * 128, :]),
              writes=["xa"], dma=True)
        emit_quant(xa[:, :], ["xa"], 128, qbf[:, :], "qbf")
        for j in range(ND):
            pr.op("tensor", lambda E, j=j: E.transpose(
                pstb[:, :], qbf[:, j * 128:(j + 1) * 128], idb[:, :]),
                reads=["qbf", "idb"], writes=["pstb"])
            pr.op("vector", lambda E: E.tensor_copy(qxt[:, 0:128], pstb[:, :]),
                  reads=["pstb"], writes=["qxt"])
            pr.op("gpsimd", lambda E, j=j, k=k: E.dma_start(
                out=wdT_d[j * 128:(j + 1) * 128, k * 128:(k + 1) * 128], in_=qxt[:, 0:128]),
                reads=["qxt"], writes=[f"wdT{j}"], dma=True)
    for i in range(NT):
        xt, xk = (xa, "xa") if i % 2 == 0 else (xb_, "xb_")
        pr.op("sync", lambda E, i=i, xt=xt: E.dma_start(out=xt[:, :], in_=xn_d[i * 128:(i + 1) * 128, :]),
              reads=[f"xn_d{i}"], writes=[xk], dma=True)
        emit_quant(xt[:, :], [xk], 128, qbf[:, :], "qbf")
        for j in range(ND):
            pr.op("tensor", lambda E, j=j: E.transpose(
                pstb[:, :], qbf[:, j * 128:(j + 1) * 128], idb[:, :]),
                reads=["qbf", "idb"], writes=["pstb"])
            pr.op("vector", lambda E, j=j: E.tensor_copy(qxt[:, j * 128:(j + 1) * 128], pstb[:, :]),
                  reads=["pstb"], writes=["qxt"])
        for c in range(2):
            for j in range(ND):
                par = j % 2
                wsl = wrb[:, par * 512:(par + 1) * 512]
                wkk = f"wrbh{par}"
                pr.op("sync", lambda E, j=j, c=c, wsl=wsl: E.dma_start(
                    out=wsl, in_=wdT_d[j * 128:(j + 1) * 128, c * 512:(c + 1) * 512]),
                    reads=[f"wdT{j}"], writes=[wkk], dma=True)
                pr.op("tensor", lambda E, j=j, c=c, wsl=wsl: E.matmul(
                    ps[4][:, :], qxt[:, j * 128:(j + 1) * 128], wsl,
                    start=(j == 0), stop=(j == 7)),
                    reads=["qxt", wkk], writes=["ps4"] if j == 7 else [],
                    war=["ps4"] if j == 0 else ())
            pr.op("vector", lambda E, c=c: E.tensor_add(
                gtile[:, c * 512:(c + 1) * 512], ps[4][:, :], qbb[:, c * 512:(c + 1) * 512]),
                reads=["ps4", "bb0"], writes=["gtile"])
        pr.op("scalar", lambda E: E.activation(gtile[:, :], gtile[:, :], AF.Silu),
              reads=["gtile"], writes=["gtile"])
        pr.op("sync", lambda E, i=i: E.dma_start(out=g_d[i * 128:(i + 1) * 128, :], in_=gtile[:, :]),
              reads=["gtile"], writes=[f"g_d{i}"], dma=True)

    for i in range(NT):
        xt, xk = (xa, "xa") if i % 2 == 0 else (xb_, "xb_")
        pr.op("sync", lambda E, i=i, xt=xt: E.dma_start(out=xt[:, :], in_=xin[i * 128:(i + 1) * 128, :]),
              writes=[xk], dma=True)
        pr.op("scalar", lambda E, i=i: E.dma_start(out=sgt[:, :], in_=s_d[i * 128:(i + 1) * 128, :]),
              reads=[f"s_d{i}"], writes=["sgt"], dma=True)
        pr.op("gpsimd", lambda E, i=i: E.dma_start(out=gtile[:, :], in_=g_d[i * 128:(i + 1) * 128, :]),
              reads=[f"g_d{i}"], writes=["gtile"], dma=True)
        pr.op("vector", lambda E: E.tensor_mul(sgt[:, :], sgt[:, :], gtile[:, :]),
              reads=["sgt", "gtile"], writes=["sgt"])
        pr.op("vector", lambda E, xt=xt: E.tensor_add(sgt[:, :], sgt[:, :], xt[:, :]),
              reads=["sgt", xk], writes=["sgt"])
        pr.op("sync", lambda E, i=i: E.dma_start(out=xout[i * 128:(i + 1) * 128, :], in_=sgt[:, :]),
              reads=["sgt"], writes=[f"xout{i}"], dma=True)
    return cx


def _host_consts_l2(inputs):
    def bc(v):
        v = np.asarray(v, np.float32)
        return np.ascontiguousarray(np.broadcast_to(v[None, :], (128, v.shape[0])))
    def colt(v, n):
        return np.ascontiguousarray(np.asarray(v, np.float32).reshape(n, 128).T)
    oh = np.zeros((16, 16 * 128), np.float32)
    for n in range(16):
        oh[n, n * 128:(n + 1) * 128] = 1.0
    conv = np.asarray(inputs["L2_attn_conv_w"], np.float32).reshape(D, 3)
    convc = np.zeros((128, 24), np.float32)
    for g in range(8):
        convc[:, 3 * g:3 * (g + 1)] = conv[g * 128:(g + 1) * 128, :]
    return {
        "w1w": np.asarray(inputs["L2_attn_w1_w"], np.float32),
        "v1w": np.asarray(inputs["L2_attn_v1_w"], np.float32),
        "w2w": np.asarray(inputs["L2_attn_w2_w"], np.float32),
        "dtw": np.asarray(inputs["L2_attn_dt_w"], np.float32),
        "bpw": np.asarray(inputs["L2_attn_Bp_w"], np.float32),
        "cpw": np.asarray(inputs["L2_attn_Cp_w"], np.float32),
        "alog": np.asarray(inputs["L2_attn_A_log"], np.float32),
        "ff1w": np.asarray(inputs["L2_attn_ff1_w"], np.float32),
        "ff2w": np.asarray(inputs["L2_attn_ff2_w"], np.float32),
        "qw": np.asarray(inputs["L2_ffn_weight"], np.float32),
        "convw": convc,
        "dpc_d": colt(inputs["L2_attn_D_p"], 8),
        "w1b_d": colt(inputs["L2_attn_w1_b"], 8),
        "v1b_d": colt(inputs["L2_attn_v1_b"], 8),
        "dtb_d": colt(inputs["L2_attn_dt_b"], 8),
        "bpb_d": np.ascontiguousarray(np.stack([np.asarray(inputs["L2_attn_Bp_b"], np.float32),
                                                np.asarray(inputs["L2_attn_Cp_b"], np.float32)], axis=1)),
        "ff1b_d": colt(inputs["L2_attn_ff1_b"], 24),
        "w2b_d": bc(inputs["L2_attn_w2_b"]),
        "ff2b_d": bc(inputs["L2_attn_ff2_b"]),
        "qb_d": bc(inputs["L2_ffn_bias"]),
        "ln1g_d": bc(inputs["L2_attn_ln1_g"]), "ln1b_d": bc(inputs["L2_attn_ln1_b"]),
        "lng_d": bc(inputs["L2_attn_ln_g"]), "lnb_d": bc(inputs["L2_attn_ln_b"]),
        "ln2g_d": bc(inputs["L2_attn_ln2_g"]), "ln2b_d": bc(inputs["L2_attn_ln2_b"]),
        "oh_d": oh.astype(BFNP),
        "identb2": np.eye(128, dtype=BFNP),
    }


def build_l1(cx=None, xin_t=None, xout_t=None):
    standalone = cx is None
    if standalone:
        cx = Ctx()
    pr = cx.pr
    E8, KSEL = 8, 1024

    xin = xin_t if xin_t is not None else cx.din("xin", [T, D])
    x16 = None if xin_t is not None else cx.din("x16", [128, D])
    wqt = cx.din("wqt", [D, D]); wkt = cx.din("wkt", [D, D]); wvt = cx.din("wvt", [D, D])
    wot = cx.din("wot", [D, D])
    wlqt = cx.din("wlqt", [64, 64]); wlkt = cx.din("wlkt", [64, 64]); wlvt = cx.din("wlvt", [64, 64])
    bq_d = cx.din("bq_d", [128, D]); bv_d = cx.din("bv_d", [128, D]); bo_d = cx.din("bo_d", [128, D])
    lna_d = cx.din("lna_d", [128, D])
    lnc_d = cx.din("lnc_d", [128, 64]); lnd_d = cx.din("lnd_d", [128, 64])
    blq_d = cx.din("blq_d", [128, 64]); blk_d = cx.din("blk_d", [128, 64]); blv_d = cx.din("blv_d", [128, 64])
    sca_d = cx.din("sca_d", [1, 4])
    maskbd = cx.din("maskbd", [128, 128])
    identf1 = cx.din("identf1", [128, 128])
    identb1 = cx.din("identb1", [128, 128], BF16)
    gatewt = cx.din("gatewt", [D, E8])
    wg_t = cx.din("wg_t", [E8 * D, 4 * D])
    wu_t = cx.din("wu_t", [E8 * D, 4 * D])
    wd_t = cx.din("wd_t", [E8 * 4 * D, D])
    xout = xout_t if xout_t is not None else cx.dout("xout", [T, D])

    wg_bf = cx.dint("wg_bf", [E8 * D, 4 * D], BF16)
    wu_bf = cx.dint("wu_bf", [E8 * D, 4 * D], BF16)
    wd_bf = cx.dint("wd_bf", [E8 * 4 * D, D], BF16)
    wq_bf = cx.dint("wq_bf", [D, D], BF16); wk_bf = cx.dint("wk_bf", [D, D], BF16)
    wv_bf = cx.dint("wv_bf", [D, D], BF16); wo_bf = cx.dint("wo_bf", [D, D], BF16)
    xnb_d = cx.dint("xnb_d", [T, D], BF16)
    DG = D + 32
    xin_es = [cx.dint(f"xin_e{e}", [KSEL, DG], BF16) for e in range(8)]
    ye_ds = [cx.dint(f"ye_d{e}", [KSEL, D]) for e in range(8)]
    s16_d = cx.dint("s16_d", [128, D])
    g_d = cx.dint("g_d", [T, D])

    xa = cx.sb("xa", [128, D]); xb_ = cx.sb("xb_", [128, D])
    sq = cx.sb("sq", [128, D]); cols = cx.sb("cols", [128, 4])
    xnf = cx.sb("xnf", [128, D])
    xnb2 = cx.sb("xnb2", [128, 2 * D], BF16)
    xnT = cx.sb("xnT", [128, ND * T], BF16)
    idb = cx.sb("idb", [128, 128], BF16)
    bqb = cx.sb("bqb", [128, D]); bvb = cx.sb("bvb", [128, D]); bob = cx.sb("bob", [128, D])
    lnab = cx.sb("lnab", [128, D])
    lncb = cx.sb("lncb", [128, 64]); lndb = cx.sb("lndb", [128, 64])
    blqb = cx.sb("blqb", [128, 64]); blkb = cx.sb("blkb", [128, 64]); blvb = cx.sb("blvb", [128, 64])
    scas = cx.sb("scas", [1, 4])
    mbd = cx.sb("mbd", [128, 128])
    wrb = cx.sb("wrb", [128, D], BF16)
    wcb = cx.sb("wcb", [128, 8 * 128], BF16)
    x16s = cx.sb("x16s", [128, D])
    qn = cx.sb("qn", [128, D], BF16)
    kn = cx.sb("kn", [128, D], BF16)
    vn = cx.sb("vn", [128, D], BF16)
    qcur = cx.sb("qcur", [128, D], BF16)
    kcur = cx.sb("kcur", [128, D], BF16)
    vcur = cx.sb("vcur", [128, D], BF16)
    ai0 = cx.sb("ai0", [128, D], BF16)
    ai1 = cx.sb("ai1", [128, D], BF16)
    ai2 = cx.sb("ai2", [128, D], BF16)
    tq = cx.sb("tq", [64, 128], BF16)
    tq2 = cx.sb("tq2", [64, 128], BF16)
    lf = cx.sb("lf", [128, 64])
    lf2 = cx.sb("lf2", [128, 64])
    lfb = cx.sb("lfb", [128, 64], BF16)
    prb = cx.sb("prb", [128, 128], BF16)
    prf = cx.sb("prf", [128, 128])
    c64 = cx.sb("c64", [128, 4])
    dcol = cx.sb("dcol", [128, 16])
    onesc = cx.sb("onesc", [1, 128])
    scc = cx.sb("scc", [128, 4])
    wl3 = cx.sb("wl3", [64, 3 * 64], BF16)
    aff = cx.sb("aff", [128, E8])
    gcol2 = cx.sb("gcol2", [128, 1])
    affT = cx.sb("affT", [E8, T])
    m8 = cx.sb("m8", [E8, T])
    r8 = cx.sb("r8", [E8, T])
    ones8 = cx.sb("ones8", [E8, 128])
    lo8 = cx.sb("lo8", [E8, 4])
    rankc = cx.sb("rankc", [128, 8 * NT])
    rankI = cx.sb("rankI", [128, 8 * NT], I32)
    rkcol = cx.sb("rkcol", [128, 1], I32)
    gatc = cx.sb("gatc", [128, 8 * E8])
    xinT = cx.sb("xinT", [128, 8 * KSEL], BF16)
    htb = xnT
    yet = cx.sb("yet", [128, 512])
    acc = cx.sb("acc", [128, D])
    gtile = cx.sb("gtile", [128, D])

    ps = [cx.ps(f"ps{i}", [128, 512]) for i in range(6)]
    pstb = cx.ps("pstb", [128, 128], BF16)
    sgt = x16s

    for name, dst, k in [("bq_d", bqb, "bqb"), ("bv_d", bvb, "bvb"), ("bo_d", bob, "bob"),
                         ("lna_d", lnab, "lnab"), ("lnc_d", lncb, "lncb"),
                         ("lnd_d", lndb, "lndb"), ("blq_d", blqb, "blqb"),
                         ("blk_d", blkb, "blkb"), ("blv_d", blvb, "blvb"),
                         ("sca_d", scas, "scas"), ("maskbd", mbd, "mbd"),
                         ("identb1", idb, "idb")]:
        pr.op("sync", lambda E, s=cx.t[name], d=dst: E.dma_start(out=d[:, :], in_=s[:, :]),
              writes=[k], dma=True)
    pr.op("gpsimd", lambda E: E.memset(ones8[:, :], 1.0), writes=["ones8"])
    pr.op("gpsimd", lambda E: E.memset(onesc[:, :], 1.0), writes=["onesc"])
    bndreg = cx.es.enter_context(cx.nc.gpsimd.register("bndreg"))
    pr.q["gpsimd"].append(lambda E: E.reg_mov(bndreg, KSEL - 1))
    for src_, dst_, tag in [(wqt, wq_bf, "cwq"), (wkt, wk_bf, "cwk"),
                            (wvt, wv_bf, "cwv"), (wot, wo_bf, "cwo")]:
        cast_weight2(cx, src_, dst_, D, D, tag, bufs=(x16s, "x16s", qn, "qn"))
    cast_weight2(cx, wg_t, wg_bf, E8 * D, 4 * D, "cwg", bufs=(x16s, "x16s", qn, "qn"))
    cast_weight2(cx, wu_t, wu_bf, E8 * D, 4 * D, "cwu", bufs=(x16s, "x16s", qn, "qn"))
    cast_weight2(cx, wd_t, wd_bf, E8 * 4 * D, D, "cwd", bufs=(x16s, "x16s", qn, "qn"))
    for w_, off in [(wlqt, 0), (wlkt, 64), (wlvt, 128)]:
        pr.op("sync", lambda E, w_=w_: E.dma_start(out=lf[0:64, :], in_=w_[:, :]),
              writes=["lf"], dma=True)
        pr.op("scalar", lambda E, off=off: E.copy(wl3[:, off:off + 64], lf[0:64, 0:64]),
              reads=["lf"], writes=["wl3"])

    for i in range(NT):
        xt, xk = (xa, "xa") if i % 2 == 0 else (xb_, "xb_")
        pr.op("sync", lambda E, i=i, xt=xt: E.dma_start(out=xt[:, :], in_=xin[i * 128:(i + 1) * 128, :]),
              writes=[xk], dma=True)
        ln_tile(pr, xt[:, :], [xk], xnf[:, :], "xnf", cols, "cols", sq[:, :], "sq")
        half = (i % 2) * D
        hk = f"xnb2h{i % 2}"
        pr.op("scalar", lambda E, half=half: E.copy(xnb2[:, half:half + D], xnf[:, :]),
              reads=["xnf"], writes=[hk])
        pr.op("gpsimd", lambda E, i=i, half=half: E.dma_start(
            out=xnb_d[i * 128:(i + 1) * 128, :], in_=xnb2[:, half:half + D]),
            reads=[hk], writes=[f"xnbd{i}"], dma=True)
        for j in range(ND):
            pr.op("tensor", lambda E, half=half, j=j: E.transpose(
                pstb[:, :], xnb2[:, half + j * 128: half + (j + 1) * 128], idb[:, :]),
                reads=[hk, "idb"], writes=["pstb"])
            pr.op("scalar", lambda E, i=i, j=j: E.copy(
                xnT[:, j * T + i * 128: j * T + (i + 1) * 128], pstb[:, :]),
                reads=["pstb"], writes=[f"xnT{j}"])

    affN_d = cx.dint("affN_d", [T, E8])
    idf = prf
    pr.op("sync", lambda E: E.dma_start(out=idf[:, :], in_=cx.t["identf1"][:, :]),
          writes=["prf"], dma=True)
    pr.op("gpsimd", lambda E: E.dma_start(
        out=wcb[:, 0:8 * E8].rearrange("p (a q) -> p a q", q=E8),
        in_=gatewt[0:D, :].rearrange("(a p) q -> p a q", p=128)),
        writes=["wcb"], dma=True)
    for i in range(NT):
        for k in range(8):
            pr.op("tensor", lambda E, i=i, k=k: E.matmul(
                ps[0][:, 0:E8], xnT[:, k * T + i * 128: k * T + (i + 1) * 128],
                wcb[:, k * E8:(k + 1) * E8], start=(k == 0), stop=(k == 7)),
                reads=[f"xnT{k}", "wcb"], writes=["ps0"] if k == 7 else [],
                war=["ps0"] if k == 0 else ())
        pr.op("vector", lambda E: E.reduce_max(cols[:, 0:1], ps[0][:, 0:E8], axis=AXX),
              reads=["ps0"], writes=["cols"])
        pr.op("vector", lambda E: E.tensor_scalar(aff[:, :], ps[0][:, 0:E8], cols[:, 0:1],
                                                  None, ALU.subtract),
              reads=["ps0", "cols"], writes=["aff"])
        pr.op("scalar", lambda E: E.activation(aff[:, :], aff[:, :], AF.Exp),
              reads=["aff"], writes=["aff"])
        pr.op("vector", lambda E: E.reduce_sum(cols[:, 1:2], aff[:, :], axis=AXX),
              reads=["aff"], writes=["colsS"])
        pr.op("vector", lambda E: E.reciprocal(cols[:, 1:2], cols[:, 1:2]),
              reads=["colsS"], writes=["colsS"])
        pr.op("vector", lambda E: E.tensor_scalar(aff[:, :], aff[:, :], cols[:, 1:2],
                                                  None, ALU.mult),
              reads=["aff", "colsS"], writes=["aff"])
        pr.op("gpsimd", lambda E, i=i: E.dma_start(out=affN_d[i * 128:(i + 1) * 128, :], in_=aff[:, :]),
              reads=["aff"], writes=[f"affN{i}"], dma=True)
        pr.op("tensor", lambda E, i=i: E.transpose(ps[1][0:E8, 0:128], aff[:, 0:E8],
                                                   idf[:, :]),
              reads=["aff", "prf"], writes=["ps1"])
        pr.op("vector", lambda E, i=i: E.tensor_copy(affT[:, i * 128:(i + 1) * 128],
                                                     ps[1][0:E8, 0:128]),
              reads=["ps1"], writes=["affT"])
    pr.op("gpsimd", lambda E: E.memset(lo8[:, 0:1], 0.0), writes=["lo8"])
    pr.op("gpsimd", lambda E: E.memset(lo8[:, 1:2], 1.0), writes=["lo8"])
    for it in range(36):
        pr.op("vector", lambda E: E.tensor_add(lo8[:, 2:3], lo8[:, 0:1], lo8[:, 1:2]),
              reads=["lo8"], writes=["lo8"])
        pr.op("vector", lambda E: E.tensor_scalar(lo8[:, 2:3], lo8[:, 2:3], 0.5, None, ALU.mult),
              reads=["lo8"], writes=["lo8"])
        pr.op("vector", lambda E: E.tensor_scalar(m8[:, :], affT[:, :], lo8[:, 2:3], None,
                                                  ALU.is_ge),
              reads=["affT", "lo8"], writes=["m8"])
        pr.op("vector", lambda E: E.reduce_sum(lo8[:, 3:4], m8[:, :], axis=AXX),
              reads=["m8"], writes=["lo8"])
        pr.op("vector", lambda E: E.tensor_scalar(c64[0:E8, 0:1], lo8[:, 3:4], float(KSEL),
                                                  None, ALU.is_ge),
              reads=["lo8"], writes=["c64"])
        pr.op("vector", lambda E: E.tensor_sub(c64[0:E8, 1:2], lo8[:, 2:3], lo8[:, 0:1]),
              reads=["lo8"], writes=["c64"])
        pr.op("vector", lambda E: E.tensor_mul(c64[0:E8, 1:2], c64[0:E8, 1:2], c64[0:E8, 0:1]),
              reads=["c64"], writes=["c64"])
        pr.op("vector", lambda E: E.tensor_add(lo8[:, 0:1], lo8[:, 0:1], c64[0:E8, 1:2]),
              reads=["c64", "lo8"], writes=["lo8"])
        pr.op("vector", lambda E: E.tensor_sub(c64[0:E8, 1:2], lo8[:, 2:3], lo8[:, 1:2]),
              reads=["lo8"], writes=["c64"])
        pr.op("vector", lambda E: E.tensor_mul(c64[0:E8, 2:3], c64[0:E8, 1:2], c64[0:E8, 0:1]),
              reads=["c64"], writes=["c64"])
        pr.op("vector", lambda E: E.tensor_add(lo8[:, 1:2], lo8[:, 1:2], c64[0:E8, 1:2]),
              reads=["c64", "lo8"], writes=["lo8"])
        pr.op("vector", lambda E: E.tensor_sub(lo8[:, 1:2], lo8[:, 1:2], c64[0:E8, 2:3]),
              reads=["c64", "lo8"], writes=["lo8"])
    pr.op("vector", lambda E: E.tensor_scalar(m8[:, :], affT[:, :], lo8[:, 0:1], None, ALU.is_ge),
          reads=["affT", "lo8"], writes=["m8"])
    for i in range(NT):
        init = 0.0 if i == 0 else r8[:, i * 128 - 1:i * 128]
        pr.op("vector", lambda E, i=i, init=init: E.tensor_tensor_scan(
            r8[:, i * 128:(i + 1) * 128], ones8[:, :], m8[:, i * 128:(i + 1) * 128],
            init, ALU.mult, ALU.add),
            reads=["m8", "ones8", "r8"], writes=["r8"])
    pr.op("vector", lambda E: E.tensor_mul(r8[:, :], r8[:, :], m8[:, :]),
          reads=["r8", "m8"], writes=["r8"])
    pr.op("vector", lambda E: E.tensor_sub(r8[:, :], r8[:, :], m8[:, :]),
          reads=["r8", "m8"], writes=["r8"])
    pr.op("vector", lambda E: E.tensor_scalar(m8[:, :], m8[:, :], -1000000.0, 1000000.0,
                                              ALU.mult, ALU.add),
          reads=["m8"], writes=["m8"])
    pr.op("vector", lambda E: E.tensor_add(r8[:, :], r8[:, :], m8[:, :]),
          reads=["r8", "m8"], writes=["r8"])
    for i in range(NT):
        pr.op("tensor", lambda E, i=i: E.transpose(ps[1][0:128, 0:E8],
                                                   r8[:, i * 128:(i + 1) * 128], idf[0:E8, 0:E8]),
              reads=["r8", "prf"], writes=["ps1"])
        pr.op("vector", lambda E, i=i: E.tensor_copy(rankc[:, i * 8:(i + 1) * 8], ps[1][0:128, 0:E8]),
              reads=["ps1"], writes=["rankc"])
    pr.op("scalar", lambda E: E.copy(rankI[:, :], rankc[:, :]),
          reads=["rankc"], writes=["rankI"])
    for i in range(NT):
        pr.op("sync", lambda E, i=i: E.dma_start(out=xnb2[:, 0:D],
                                                 in_=xnb_d[i * 128:(i + 1) * 128, :]),
              reads=[f"xnbd{i}"], writes=["xnb2h0"], dma=True)
        pr.op("sync", lambda E, i=i: E.dma_start(out=aff[:, :],
                                                 in_=affN_d[i * 128:(i + 1) * 128, :]),
              reads=[f"affN{i}"], writes=["aff"], dma=True)
        for e in range(E8):
            pr.op("vector", lambda E: E.memset(xnb2[:, D:D + 32], 1.0),
                  reads=[], writes=["xnb2g"])
            pr.op("vector", lambda E, e=e: E.tensor_scalar(
                xnb2[:, D:D + 32], xnb2[:, D:D + 32], aff[:, e:e + 1], None, ALU.mult),
                reads=["aff", "xnb2g"], writes=["xnb2g"])
            pr.op("vector", lambda E, e=e, i=i: E.tensor_copy(
                rkcol[:, :], rankI[:, i * 8 + e:i * 8 + e + 1]),
                reads=["rankI"], writes=["rkcol"])
            pr.op("gpsimd", lambda E, e=e, i=i: E.indirect_dma_start(
                out=xin_es[e][:, :],
                out_offset=IndirectOffsetOnAxis(rkcol[:, 0:1], axis=0),
                in_=xnb2[:, 0:D + 32], in_offset=None,
                bounds_check=bndreg, oob_is_err=False),
                reads=["rkcol", "xnb2h0", "xnb2g"], writes=[f"xin_e{e}"], dma=True)

    SC4 = 0.35355339059327373
    if x16 is None:
        cc_in = cx.dint("cc_in", [16, D])
        cc_out = cx.dint("cc_out", [128, D])
        pr.op("gpsimd", lambda E: E.dma_start(out=cc_in[:, :], in_=xin[0:16, :]),
              writes=["ccin"], dma=True)
        collsem = pr.sem["coll"]
        pr._wait("gpsimd", pr.reg.get("ccin")[0][0], pr.reg["ccin"][0][1])
        pr.q["gpsimd"].append(lambda E: E.collective_compute(
            "AllGather", ALU.bypass, replica_groups=[[0, 1, 2, 3, 4, 5, 6, 7]],
            ins=[cc_in.ap().opt()], outs=[cc_out.ap().opt()]).then_inc(collsem, 1))
        pr.cnt["coll"] += 1
        pr._wait("sync", "coll", 1)
        pr.op("sync", lambda E: E.dma_start(out=x16s[:, :], in_=cc_out[:, :]),
              writes=["x16s"], dma=True)
    else:
        pr.op("sync", lambda E: E.dma_start(out=x16s[:, :], in_=x16[:, :]),
              writes=["x16s"], dma=True)
    ln_tile(pr, x16s[:, :], ["x16s"], xnf[:, :], "xnf", cols, "cols", sq[:, :], "sq")
    ln_tile(pr, xnf[:, :], ["xnf"], x16s[:, :], "x16s", cols, "cols", sq[:, :], "sq")
    pr.op("vector", lambda E: E.tensor_mul(x16s[:, :], x16s[:, :], lnab[:, :]),
          reads=["x16s", "lnab"], writes=["x16s"])
    pr.op("scalar", lambda E: E.copy(xnb2[:, 0:D], x16s[:, :]),
          reads=["x16s"], writes=["xnb2h0"])
    for j in range(ND):
        pr.op("tensor", lambda E, j=j: E.transpose(pstb[:, :], xnb2[:, j * 128:(j + 1) * 128],
                                                   idb[:, :]),
              reads=["xnb2h0", "idb"], writes=["pstb"])
        pr.op("scalar", lambda E, j=j: E.copy(wcb[:, j * 128:(j + 1) * 128], pstb[:, :]),
              reads=["pstb"], writes=["wcb"])
    for (wt, dst, dkey, mode) in [(wq_bf, qn, "qn", "q"), (wk_bf, kn, "kn", "k"),
                                  (wv_bf, vn, "vn", "v")]:
        for c in range(2):
            for k in range(8):
                pr.op("sync", lambda E, wt=wt, k=k, c=c: E.dma_start(
                    out=wrb[:, c * 512:(c + 1) * 512],
                    in_=wt[k * 128:(k + 1) * 128, c * 512:(c + 1) * 512]),
                    reads=["c" + {"q": "wq", "k": "wk", "v": "wv"}[mode] + "done"],
                    writes=["wrb"], dma=True)
                pr.op("tensor", lambda E, k=k, c=c: E.matmul(
                    ps[0][:, :], wcb[:, k * 128:(k + 1) * 128], wrb[:, c * 512:(c + 1) * 512],
                    start=(k == 0), stop=(k == 7)),
                    reads=["wcb", "wrb"], writes=["ps0"] if k == 7 else [],
                    war=["ps0"] if k == 0 else ())
            if mode == "q":
                pr.op("vector", lambda E, c=c: E.tensor_add(xnf[:, c * 512:(c + 1) * 512],
                                                            ps[0][:, :], bqb[:, c * 512:(c + 1) * 512]),
                      reads=["ps0", "bqb"], writes=["xnf"])
                pr.op("vector", lambda E, c=c, dst=dst: E.tensor_scalar(
                    dst[:, c * 512:(c + 1) * 512], xnf[:, c * 512:(c + 1) * 512], SC4, None, ALU.mult),
                    reads=["xnf"], writes=[dkey])
            elif mode == "k":
                pr.op("vector", lambda E, c=c, dst=dst: E.tensor_scalar(
                    dst[:, c * 512:(c + 1) * 512], ps[0][:, :], SC4, None, ALU.mult),
                    reads=["ps0"], writes=[dkey])
            else:
                pr.op("vector", lambda E, c=c, dst=dst: E.tensor_add(
                    dst[:, c * 512:(c + 1) * 512], ps[0][:, :], bvb[:, c * 512:(c + 1) * 512]),
                    reads=["ps0", "bvb"], writes=[dkey])
    pr.op("vector", lambda E: E.tensor_copy(qcur[:, :], qn[:, :]), reads=["qn"], writes=["qcur"])
    pr.op("vector", lambda E: E.tensor_copy(kcur[:, :], kn[:, :]), reads=["kn"], writes=["kcur"])
    pr.op("vector", lambda E: E.tensor_copy(vcur[:, :], vn[:, :]), reads=["vn"], writes=["vcur"])
    kcur2 = kn; vcur2 = vn
    ones128 = cx.sb("ones128", [128, 1])
    tsc4 = lo8
    pr.op("gpsimd", lambda E: E.memset(ones128[:, :], 1.0), writes=["ones128"])
    ai_bufs = [ai0, ai1, ai2]
    for it in range(3):
        kin, kink = (kcur, "kcur") if it % 2 == 0 else (kcur2, "kn")
        kout, koutk = (kcur2, "kn") if it % 2 == 0 else (kcur, "kcur")
        vin, vink = (vcur, "vcur") if it % 2 == 0 else (vcur2, "vn")
        vout, voutk = (vcur2, "vn") if it % 2 == 0 else (vcur, "vcur")
        aib = ai_bufs[it]
        aik = f"ai{it}"
        pr.op("vector", lambda E, it=it: E.tensor_scalar(tsc4[0:1, 0:1], scas[:, 0:1],
                                                         0.005 * it, None, ALU.add),
              reads=["scas"], writes=["lo8"])
        pr.op("vector", lambda E: E.tensor_scalar(tsc4[0:1, 1:2], tsc4[0:1, 0:1], 1.0, None,
                                                  ALU.not_equal),
              reads=["lo8"], writes=["lo8"])
        pr.op("vector", lambda E: E.tensor_scalar(tsc4[0:1, 2:3], tsc4[0:1, 0:1], 0.0, None,
                                                  ALU.is_gt),
              reads=["lo8"], writes=["lo8"])
        pr.op("vector", lambda E: E.tensor_mul(tsc4[0:1, 1:2], tsc4[0:1, 1:2], tsc4[0:1, 2:3]),
              reads=["lo8"], writes=["lo8"])
        pr.op("scalar", lambda E: E.activation(tsc4[0:1, 2:3], tsc4[0:1, 0:1], AF.Sqrt),
              reads=["lo8"], writes=["lo8"])
        pr.op("vector", lambda E: E.reciprocal(tsc4[0:1, 2:3], tsc4[0:1, 2:3]),
              reads=["lo8"], writes=["lo8"])
        pr.op("vector", lambda E: E.tensor_scalar(tsc4[0:1, 2:3], tsc4[0:1, 2:3], -1.0, None,
                                                  ALU.add),
              reads=["lo8"], writes=["lo8"])
        pr.op("vector", lambda E: E.tensor_mul(tsc4[0:1, 1:2], tsc4[0:1, 1:2], tsc4[0:1, 2:3]),
              reads=["lo8"], writes=["lo8"])
        pr.op("vector", lambda E: E.tensor_scalar(tsc4[0:1, 1:2], tsc4[0:1, 1:2], 1.0, None,
                                                  ALU.add),
              reads=["lo8"], writes=["lo8"])
        pr.op("tensor", lambda E: E.matmul(ps[1][:, 0:1], onesc[:, :], tsc4[0:1, 1:2],
                                           start=True, stop=True),
              reads=["onesc", "lo8"], writes=["ps1"])
        pr.op("vector", lambda E: E.tensor_copy(scc[:, 0:1], ps[1][:, 0:1]),
              reads=["ps1"], writes=["scc"])
        for h in range(16):
            hsl = slice(h * 64, (h + 1) * 64)
            pr.op("tensor", lambda E, hsl=hsl, vin=vin: E.transpose(
                pstb[0:64, 0:128], vin[:, hsl], idb[:, :]),
                reads=[vink, "idb"], writes=["pstb"])
            pr.op("vector", lambda E: E.tensor_copy(tq2[0:64, 0:128], pstb[0:64, 0:128]),
                  reads=["pstb"], writes=["tq2"])
            pr.op("tensor", lambda E: E.matmul(ps[2][0:64, 0:128], wl3[:, 128:192], tq2[0:64, 0:128],
                                               start=True, stop=True),
                  reads=["wl3", "tq2"], writes=["ps2"])
            pr.op("vector", lambda E: E.tensor_copy(tq2[0:64, 0:128], ps[2][0:64, 0:128]),
                  reads=["ps2"], writes=["tq2"])
            pr.op("tensor", lambda E: E.transpose(pstb[:, 0:64], tq2[0:64, 0:128], idb[0:64, 0:64]),
                  reads=["tq2", "idb"], writes=["pstb"])
            pr.op("vector", lambda E, hsl=hsl, vout=vout: E.tensor_add(
                vout[:, hsl], pstb[:, 0:64], blvb[:, :]),
                reads=["pstb", "blvb"], writes=[voutk])
            pr.op("tensor", lambda E, hsl=hsl: E.transpose(
                pstb[0:64, 0:128], qcur[:, hsl], idb[:, :]),
                reads=["qcur", "idb"], writes=["pstb"])
            pr.op("vector", lambda E: E.tensor_copy(tq[:, :], pstb[:, 0:64].transpose([0, 1]) if False else pstb[0:64, 0:128]),
                  reads=["pstb"], writes=["tq"]) if False else None
            pr.op("vector", lambda E: E.tensor_copy(tq[0:64, 0:128], pstb[0:64, 0:128]),
                  reads=["pstb"], writes=["tq"])
            pr.op("tensor", lambda E: E.matmul(ps[2][0:64, 0:128], wl3[:, 0:64], tq[0:64, 0:128],
                                               start=True, stop=True),
                  reads=["wl3", "tq"], writes=["ps2"])
            pr.op("vector", lambda E: E.tensor_copy(lfb[0:64, 0:128] if False else tq2[0:64, 0:128], ps[2][0:64, 0:128]),
                  reads=["ps2"], writes=["tq2"])
            pr.op("tensor", lambda E: E.transpose(pstb[:, 0:64], tq2[0:64, 0:128], idb[0:64, 0:64]),
                  reads=["tq2", "idb"], writes=["pstb"])
            pr.op("vector", lambda E: E.tensor_add(lf[:, :], pstb[:, 0:64], blqb[:, :]),
                  reads=["pstb", "blqb"], writes=["lf"])
            pr.op("vector", lambda E: E.reduce_sum(c64[:, 0:1], lf[:, :], axis=AXX),
                  reads=["lf"], writes=["c64b"])
            pr.op("scalar", lambda E: E.mul(c64[:, 0:1], c64[:, 0:1], 1.0 / 64),
                  reads=["c64b"], writes=["c64b"])
            pr.op("vector", lambda E: E.tensor_scalar(lf[:, :], lf[:, :], c64[:, 0:1], None,
                                                      ALU.subtract),
                  reads=["lf", "c64b"], writes=["lf"])
            pr.op("scalar", lambda E: E.activation(lf2[:, :], lf[:, :], AF.Square),
                  reads=["lf"], writes=["lf2"])
            pr.op("vector", lambda E: E.reduce_sum(c64[:, 1:2], lf2[:, :], axis=AXX),
                  reads=["lf2"], writes=["c64b"])
            pr.op("vector", lambda E: E.tensor_scalar(c64[:, 1:2], c64[:, 1:2], 1.0 / 64, 1e-5,
                                                      ALU.mult, ALU.add),
                  reads=["c64b"], writes=["c64b"])
            pr.op("scalar", lambda E: E.activation(c64[:, 1:2], c64[:, 1:2], AF.Sqrt),
                  reads=["c64b"], writes=["c64b"])
            pr.op("vector", lambda E: E.reciprocal(c64[:, 1:2], c64[:, 1:2]),
                  reads=["c64b"], writes=["c64b"])
            pr.op("vector", lambda E: E.tensor_scalar(lf[:, :], lf[:, :], c64[:, 1:2], None,
                                                      ALU.mult),
                  reads=["lf", "c64b"], writes=["lf"])
            pr.op("vector", lambda E: E.tensor_mul(lf[:, :], lf[:, :], lncb[:, :]),
                  reads=["lf", "lncb"], writes=["lf"])
            pr.op("vector", lambda E: E.tensor_scalar(lf[:, :], lf[:, :], scc[:, 0:1], None,
                                                      ALU.mult),
                  reads=["lf", "scc"], writes=["lf"])
            pr.op("vector", lambda E: E.tensor_copy(lfb[:, :], lf[:, :]),
                  reads=["lf"], writes=["lfb"])
            pr.op("tensor", lambda E: E.transpose(pstb[0:64, 0:128], lfb[:, :], idb[:, :]),
                  reads=["lfb", "idb"], writes=["pstb"])
            pr.op("vector", lambda E: E.tensor_copy(tq[0:64, 0:128], pstb[0:64, 0:128]),
                  reads=["pstb"], writes=["tq"])
            pr.op("tensor", lambda E, hsl=hsl, kin=kin: E.transpose(
                pstb[0:64, 0:128], kin[:, hsl], idb[:, :]),
                reads=[kink, "idb"], writes=["pstb"])
            pr.op("vector", lambda E: E.tensor_copy(tq2[0:64, 0:128], pstb[0:64, 0:128]),
                  reads=["pstb"], writes=["tq2"])
            pr.op("tensor", lambda E: E.matmul(ps[2][0:64, 0:128], wl3[:, 64:128], tq2[0:64, 0:128],
                                               start=True, stop=True),
                  reads=["wl3", "tq2"], writes=["ps2"])
            pr.op("vector", lambda E: E.tensor_copy(tq2[0:64, 0:128], ps[2][0:64, 0:128]),
                  reads=["ps2"], writes=["tq2"])
            pr.op("tensor", lambda E: E.transpose(pstb[:, 0:64], tq2[0:64, 0:128], idb[0:64, 0:64]),
                  reads=["tq2", "idb"], writes=["pstb"])
            pr.op("vector", lambda E: E.tensor_add(lf[:, :], pstb[:, 0:64], blkb[:, :]),
                  reads=["pstb", "blkb"], writes=["lf"])
            pr.op("vector", lambda E, hsl=hsl, kout=kout: E.tensor_copy(kout[:, hsl], lf[:, :]),
                  reads=["lf"], writes=[koutk])
            pr.op("vector", lambda E: E.reduce_sum(c64[:, 0:1], lf[:, :], axis=AXX),
                  reads=["lf"], writes=["c64b"])
            pr.op("scalar", lambda E: E.mul(c64[:, 0:1], c64[:, 0:1], 1.0 / 64),
                  reads=["c64b"], writes=["c64b"])
            pr.op("vector", lambda E: E.tensor_scalar(lf[:, :], lf[:, :], c64[:, 0:1], None,
                                                      ALU.subtract),
                  reads=["lf", "c64b"], writes=["lf"])
            pr.op("scalar", lambda E: E.activation(lf2[:, :], lf[:, :], AF.Square),
                  reads=["lf"], writes=["lf2"])
            pr.op("vector", lambda E: E.reduce_sum(c64[:, 1:2], lf2[:, :], axis=AXX),
                  reads=["lf2"], writes=["c64b"])
            pr.op("vector", lambda E: E.tensor_scalar(c64[:, 1:2], c64[:, 1:2], 1.0 / 64, 1e-5,
                                                      ALU.mult, ALU.add),
                  reads=["c64b"], writes=["c64b"])
            pr.op("scalar", lambda E: E.activation(c64[:, 1:2], c64[:, 1:2], AF.Sqrt),
                  reads=["c64b"], writes=["c64b"])
            pr.op("vector", lambda E: E.reciprocal(c64[:, 1:2], c64[:, 1:2]),
                  reads=["c64b"], writes=["c64b"])
            pr.op("vector", lambda E: E.tensor_scalar(lf[:, :], lf[:, :], c64[:, 1:2], None,
                                                      ALU.mult),
                  reads=["lf", "c64b"], writes=["lf"])
            pr.op("vector", lambda E: E.tensor_mul(lf[:, :], lf[:, :], lndb[:, :]),
                  reads=["lf", "lndb"], writes=["lf"])
            pr.op("vector", lambda E: E.tensor_copy(lfb[:, :], lf[:, :]),
                  reads=["lf"], writes=["lfb"])
            pr.op("tensor", lambda E: E.transpose(pstb[0:64, 0:128], lfb[:, :], idb[:, :]),
                  reads=["lfb", "idb"], writes=["pstb"])
            pr.op("vector", lambda E: E.tensor_copy(tq2[0:64, 0:128], pstb[0:64, 0:128]),
                  reads=["pstb"], writes=["tq2"])
            pr.op("tensor", lambda E: E.matmul(ps[3][:, 0:128], tq[0:64, 0:128], tq2[0:64, 0:128],
                                               start=True, stop=True),
                  reads=["tq", "tq2"], writes=["ps3"])
            pr.op("vector", lambda E: E.tensor_scalar(prf[:, :], ps[3][:, 0:128], 0.125, None,
                                                      ALU.mult),
                  reads=["ps3"], writes=["prf"])
            pr.op("vector", lambda E: E.tensor_add(prf[:, :], prf[:, :], mbd[:, :]),
                  reads=["prf", "mbd"], writes=["prf"])
            pr.op("vector", lambda E: E.reduce_max(c64[:, 2:3], prf[:, :], axis=AXX),
                  reads=["prf"], writes=["c64c"])
            pr.op("vector", lambda E: E.tensor_scalar(prf[:, :], prf[:, :], c64[:, 2:3], None,
                                                      ALU.subtract),
                  reads=["prf", "c64c"], writes=["prf"])
            pr.op("scalar", lambda E: E.activation(prf[:, :], prf[:, :], AF.Exp),
                  reads=["prf"], writes=["prf"])
            pr.op("vector", lambda E: E.reduce_sum(c64[:, 3:4], prf[:, :], axis=AXX),
                  reads=["prf"], writes=["c64c"])
            pr.op("vector", lambda E: E.reciprocal(c64[:, 3:4], c64[:, 3:4]),
                  reads=["c64c"], writes=["c64c"])
            pr.op("vector", lambda E: E.tensor_scalar(prb[:, :], prf[:, :], c64[:, 3:4], None,
                                                      ALU.mult),
                  reads=["prf", "c64c"], writes=["prb"])
            pr.op("tensor", lambda E: E.transpose(pstb[:, :], prb[:, :], idb[:, :]),
                  reads=["prb", "idb"], writes=["pstb"])
            pr.op("vector", lambda E: E.tensor_copy(prb[:, :], pstb[:, :]),
                  reads=["pstb"], writes=["prb"])
            pr.op("tensor", lambda E, hsl=hsl, vout=vout: E.matmul(
                ps[4][:, 0:64], prb[:, :], vout[:, hsl], start=True, stop=True),
                reads=["prb", voutk], writes=["ps4"])
            pr.op("vector", lambda E, hsl=hsl, aib=aib: E.tensor_copy(aib[:, hsl], ps[4][:, 0:64]),
                  reads=["ps4"], writes=[aik])
            if it == 1:
                pr.op("vector", lambda E, hsl=hsl: E.tensor_sub(lf[:, :], ai1[:, hsl], ai0[:, hsl]),
                      reads=["ai1", "ai0"], writes=["lf"])
                pr.op("scalar", lambda E: E.activation(lf[:, :], lf[:, :], AF.Abs),
                      reads=["lf"], writes=["lf"])
                pr.op("vector", lambda E, h=h: E.reduce_sum(dcol[:, h:h + 1], lf[:, :], axis=AXX),
                      reads=["lf"], writes=["dcol"])
        pr.op("vector", lambda E, aib=aib: E.tensor_add(qcur[:, :], qcur[:, :], aib[:, :]),
              reads=["qcur", aik], writes=["qcur"])
        if it == 1:
            pr.op("tensor", lambda E: E.matmul(ps[1][0:1, 0:16], ones128[:, :], dcol[:, :],
                                               start=True, stop=True),
                  reads=["ones128", "dcol"], writes=["ps1"])
            pr.op("vector", lambda E: E.tensor_copy(c64[0:1, 0:1] if False else scc[0:1, 2:3],
                                                    ps[1][0:1, 0:1]),
                  reads=["ps1"], writes=["sccd"]) if False else None
            pr.op("vector", lambda E: E.reduce_sum(scc[0:1, 2:3], ps[1][0:1, 0:16], axis=AXX),
                  reads=["ps1"], writes=["sccd"])
            pr.op("vector", lambda E: E.tensor_scalar(scc[0:1, 2:3], scc[0:1, 2:3],
                                                      1.0 / (8 * 16 * 4096 * 64), None, ALU.mult),
                  reads=["sccd"], writes=["sccd"])
            pr.op("vector", lambda E: E.tensor_mul(scc[0:1, 3:4], scc[0:1, 2:3], scas[:, 2:3]),
                  reads=["sccd", "scas"], writes=["sccd"])
            pr.op("vector", lambda E: E.tensor_add(scc[0:1, 3:4], scc[0:1, 3:4], scas[:, 1:2]),
                  reads=["sccd", "scas"], writes=["sccd"])
            pr.op("vector", lambda E: E.tensor_tensor(scc[0:1, 2:3], scc[0:1, 2:3], scc[0:1, 3:4],
                                                      ALU.is_lt),
                  reads=["sccd"], writes=["sccd"])
            pr.op("tensor", lambda E: E.matmul(ps[1][:, 0:1], onesc[:, :], scc[0:1, 2:3],
                                               start=True, stop=True),
                  reads=["onesc", "sccd"], writes=["ps1"])
            pr.op("vector", lambda E: E.tensor_copy(scc[:, 1:2], ps[1][:, 0:1]),
                  reads=["ps1"], writes=["scc"])
    pr.op("vector", lambda E: E.tensor_sub(qn[:, :], ai1[:, :], ai2[:, :]),
          reads=["ai1", "ai2"], writes=["qn"])
    pr.op("vector", lambda E: E.tensor_scalar(qn[:, :], qn[:, :], scc[:, 1:2], None, ALU.mult),
          reads=["qn", "scc"], writes=["qn"])
    pr.op("vector", lambda E: E.tensor_add(ai2[:, :], ai2[:, :], qn[:, :]),
          reads=["ai2", "qn"], writes=["ai2"])
    for j in range(ND):
        pr.op("tensor", lambda E, j=j: E.transpose(pstb[:, :], ai2[:, j * 128:(j + 1) * 128],
                                                   idb[:, :]),
              reads=["ai2", "idb"], writes=["pstb"])
        pr.op("scalar", lambda E, j=j: E.copy(wcb[:, j * 128:(j + 1) * 128], pstb[:, :]),
              reads=["pstb"], writes=["wcb"])
    for c in range(2):
        for k in range(8):
            pr.op("sync", lambda E, k=k, c=c: E.dma_start(
                out=wrb[:, c * 512:(c + 1) * 512],
                in_=wo_bf[k * 128:(k + 1) * 128, c * 512:(c + 1) * 512]),
                reads=["cwodone"], writes=["wrb"], dma=True)
            pr.op("tensor", lambda E, k=k, c=c: E.matmul(
                ps[0][:, :], wcb[:, k * 128:(k + 1) * 128], wrb[:, c * 512:(c + 1) * 512],
                start=(k == 0), stop=(k == 7)),
                reads=["wcb", "wrb"], writes=["ps0"] if k == 7 else [],
                war=["ps0"] if k == 0 else ())
        pr.op("vector", lambda E, c=c: E.tensor_add(x16s[:, c * 512:(c + 1) * 512],
                                                    ps[0][:, :], bob[:, c * 512:(c + 1) * 512]),
              reads=["ps0", "bob"], writes=["x16s"])
    pr.op("sync", lambda E: E.dma_start(out=s16_d[:, :], in_=x16s[:, :]),
          reads=["x16s"], writes=["s16d"], dma=True)

    for e in range(E8):
        for a in range(8):
            pr.op("sync", lambda E, e=e, a=a: E.dma_start(
                out=xnb2[:, 0:DG], in_=xin_es[e][a * 128:(a + 1) * 128, :]),
                reads=[f"xin_e{e}"], writes=["xnb2h0", "xnb2g"], dma=True)
            pr.op("vector", lambda E, e=e, a=a: E.tensor_copy(
                gatc[:, e * 8 + a: e * 8 + a + 1], xnb2[:, D:D + 1]),
                reads=["xnb2g"], writes=["gatc"])
            for j in range(ND):
                pr.op("tensor", lambda E, j=j: E.transpose(pstb[:, :], xnb2[:, j * 128:(j + 1) * 128],
                                                           idb[:, :]),
                      reads=["xnb2h0", "idb"], writes=["pstb"])
                pr.op("scalar", lambda E, j=j, a=a: E.copy(
                    xinT[:, j * KSEL + a * 128: j * KSEL + (a + 1) * 128], pstb[:, :]),
                    reads=["pstb"], writes=[f"xinT{j}"])

        for rh in range(2):
            r0 = rh * 512
            for kk in range(32):
                gb, gk = (wcb, "wcb") if kk % 2 == 0 else (ai0, "ai0")
                ub, uk = (wrb, "wrb") if kk % 2 == 0 else (ai1, "ai1")
                pg = "ps0" if kk % 2 == 0 else "ps2"
                pu = "ps1" if kk % 2 == 0 else "ps3"
                pgi = 0 if kk % 2 == 0 else 2
                pui = 1 if kk % 2 == 0 else 3
                pr.op("sync", lambda E, e=e, kk=kk, gb=gb: E.dma_start(
                    out=gb[:, 0:8 * 128].rearrange("p (a q) -> p a q", q=128),
                    in_=wg_bf[e * D:(e + 1) * D, kk * 128:(kk + 1) * 128].rearrange(
                        "(a p) q -> p a q", p=128)),
                    reads=["cwgdone"], writes=[gk], dma=True)
                pr.op("scalar", lambda E, e=e, kk=kk, ub=ub: E.dma_start(
                    out=ub[:, 0:D].rearrange("p (a q) -> p a q", q=128),
                    in_=wu_bf[e * D:(e + 1) * D, kk * 128:(kk + 1) * 128].rearrange(
                        "(a p) q -> p a q", p=128)),
                    reads=["cwudone"], writes=[uk], dma=True)
                for k in range(8):
                    pr.op("tensor", lambda E, k=k, r0=r0, gb=gb, pgi=pgi: E.matmul(
                        ps[pgi][:, :], gb[:, k * 128:(k + 1) * 128],
                        xinT[:, k * KSEL + r0: k * KSEL + r0 + 512],
                        start=(k == 0), stop=(k == 7)),
                        reads=[gk, f"xinT{k}"], writes=[pg] if k == 7 else [],
                        war=[pg] if k == 0 else ())
                    pr.op("tensor", lambda E, k=k, r0=r0, ub=ub, pui=pui: E.matmul(
                        ps[pui][:, :], ub[:, k * 128:(k + 1) * 128],
                        xinT[:, k * KSEL + r0: k * KSEL + r0 + 512],
                        start=(k == 0), stop=(k == 7)),
                        reads=[uk, f"xinT{k}"], writes=[pu] if k == 7 else [],
                        war=[pu] if k == 0 else ())
                pr.op("scalar", lambda E, pgi=pgi: E.activation(yet[:, :], ps[pgi][:, :], AF.Silu),
                      reads=[pg], writes=["yet"])
                pr.op("vector", lambda E, kk=kk, pui=pui: E.tensor_mul(
                    htb[:, kk * 512:(kk + 1) * 512], yet[:, :], ps[pui][:, :]),
                    reads=["yet", pu], writes=[f"htb{kk}"], war=[f"xnT{kk // 8}"])
            for half in range(2):
                for kk in range(32):
                    db_, dk = (wrb, "wrb") if kk % 2 == 0 else (ai2, "ai2")
                    pr.op("sync" if kk % 2 == 0 else "scalar",
                          lambda E, e=e, kk=kk, db_=db_: E.dma_start(
                        out=db_[:, 0:D],
                        in_=wd_bf[e * 4 * D + kk * 128: e * 4 * D + (kk + 1) * 128, :]),
                        reads=["cwddone"], writes=[dk], dma=True)
                    for rr in range(2):
                        rs = 2 * half + rr
                        for c in range(2):
                            pj = 2 + rr * 2 + c
                            pr.op("tensor", lambda E, kk=kk, rs=rs, c=c, pj=pj, db_=db_: E.matmul(
                                ps[pj][:, :], htb[:, kk * 512 + rs * 128: kk * 512 + (rs + 1) * 128],
                                db_[:, c * 512:(c + 1) * 512], start=(kk == 0), stop=(kk == 31)),
                                reads=[dk, f"htb{kk}"], writes=[f"ps{pj}"] if kk == 31 else [],
                                war=[f"ps{pj}"] if kk == 0 else ())
                for rr in range(2):
                    rs = 2 * half + rr
                    for c in range(2):
                        pj = 2 + rr * 2 + c
                        pr.op("vector", lambda E, e=e, rh=rh, rs=rs, pj=pj: E.tensor_scalar(
                            yet[:, :], ps[pj][:, :], gatc[:, e * 8 + rh * 4 + rs: e * 8 + rh * 4 + rs + 1],
                            None, ALU.mult),
                            reads=[f"ps{pj}", "gatc"], writes=["yet"])
                        pr.op("sync", lambda E, e=e, rh=rh, rs=rs, c=c: E.dma_start(
                            out=ye_ds[e][rh * 512 + rs * 128: rh * 512 + (rs + 1) * 128,
                                         c * 512:(c + 1) * 512], in_=yet[:, :]),
                            reads=["yet"], writes=[f"ye{e}"], dma=True)

    for i in range(NT):
        pr.op("vector", lambda E: E.memset(acc[:, :], 0.0), writes=["acc"])
        for e in range(E8):
            pr.op("vector", lambda E, e=e, i=i: E.tensor_copy(
                rkcol[:, :], rankI[:, i * 8 + e:i * 8 + e + 1]),
                reads=["rankI"], writes=["rkcol"])
            pr.op("gpsimd", lambda E, e=e, i=i: E.indirect_dma_start(
                out=acc[:, :], out_offset=None,
                in_=ye_ds[e][:, :],
                in_offset=IndirectOffsetOnAxis(rkcol[:, 0:1], axis=0),
                bounds_check=bndreg, oob_is_err=False, compute_op=ALU.add),
                reads=["rkcol", f"ye{e}", "acc"], writes=["acc"], dma=True)
        pr.op("scalar", lambda E: E.activation(gtile[:, :], acc[:, :], AF.Silu),
              reads=["acc"], writes=["gtile"])
        pr.op("sync", lambda E, i=i: E.dma_start(out=g_d[i * 128:(i + 1) * 128, :], in_=gtile[:, :]),
              reads=["gtile"], writes=[f"g_d{i}"], dma=True)

    for i in range(NT):
        xt, xk = (xa, "xa") if i % 2 == 0 else (xb_, "xb_")
        pr.op("sync", lambda E, i=i, xt=xt: E.dma_start(out=xt[:, :], in_=xin[i * 128:(i + 1) * 128, :]),
              writes=[xk], dma=True)
        pr.op("gpsimd", lambda E, i=i: E.dma_start(out=gtile[:, :], in_=g_d[i * 128:(i + 1) * 128, :]),
              reads=[f"g_d{i}"], writes=["gtile"], dma=True)
        pr.op("vector", lambda E: E.tensor_copy(sgt[:, :], bob[:, :]),
              reads=["bob"], writes=["x16s"])
        if i == 0:
            if x16 is None:
                pid16 = cx.nc.partition_id() * 16
                pr.op("gpsimd", lambda E: E.dma_start(
                    out=sgt[0:16, :], in_=s16_d[bass.ds(pid16, 16), :]),
                    reads=["s16d"], writes=["x16s"], dma=True)
            else:
                pr.op("scalar", lambda E: E.dma_start(out=sgt[0:16, :], in_=s16_d[0:16, :]),
                      reads=["s16d"], writes=["x16s"], dma=True)
        pr.op("vector", lambda E: E.tensor_mul(sgt[:, :], sgt[:, :], gtile[:, :]),
              reads=["x16s", "gtile"], writes=["x16s"])
        pr.op("vector", lambda E, xt=xt: E.tensor_add(sgt[:, :], sgt[:, :], xt[:, :]),
              reads=["x16s", xk], writes=["x16s"])
        pr.op("sync", lambda E, i=i: E.dma_start(out=xout[i * 128:(i + 1) * 128, :], in_=sgt[:, :]),
              reads=["x16s"], writes=[f"xout{i}"], dma=True)
    return cx


def _host_consts_l1(inputs, x16_rot):
    def bc(v):
        v = np.asarray(v, np.float32)
        return np.ascontiguousarray(np.broadcast_to(v[None, :], (128, v.shape[0])))
    s_i = np.arange(128)[:, None]; t_i = np.arange(128)[None, :]
    blockdiag = (s_i // 16 == t_i // 16)
    return {
        "x16": np.ascontiguousarray(x16_rot.astype(np.float32)),
        "wqt": np.ascontiguousarray(np.asarray(inputs["L1_attn_Wq"], np.float32).T),
        "wkt": np.ascontiguousarray(np.asarray(inputs["L1_attn_Wk"], np.float32).T),
        "wvt": np.ascontiguousarray(np.asarray(inputs["L1_attn_Wv"], np.float32).T),
        "wot": np.ascontiguousarray(np.asarray(inputs["L1_attn_Wo"], np.float32).T),
        "wlqt": np.ascontiguousarray(np.asarray(inputs["L1_attn_Wlq"], np.float32).T),
        "wlkt": np.ascontiguousarray(np.asarray(inputs["L1_attn_Wlk"], np.float32).T),
        "wlvt": np.ascontiguousarray(np.asarray(inputs["L1_attn_Wlv"], np.float32).T),
        "bq_d": bc(inputs["L1_attn_bq"]), "bv_d": bc(inputs["L1_attn_bv"]),
        "bo_d": bc(inputs["L1_attn_bo"]),
        "lna_d": bc(inputs["L1_attn_lna_w"]),
        "lnc_d": bc(inputs["L1_attn_lnc_w"]), "lnd_d": bc(inputs["L1_attn_lnd_w"]),
        "blq_d": bc(inputs["L1_attn_blq"]), "blk_d": bc(inputs["L1_attn_blk"]),
        "blv_d": bc(inputs["L1_attn_blv"]),
        "sca_d": np.array([[float(inputs["L1_attn_temp"]), float(inputs["L1_attn_threshold"]),
                            float(inputs["L1_attn_factor"]), 0.0]], np.float32),
        "maskbd": np.where(blockdiag, 0.0, -1e30).astype(np.float32),
        "identb1": np.eye(128, dtype=BFNP),
        "identf1": np.eye(128, dtype=np.float32),
        "gatewt": np.ascontiguousarray(np.asarray(inputs["L1_ffn_gate_w"], np.float32).T),
        "wg_t": np.ascontiguousarray(np.asarray(inputs["L1_ffn_w_gate"], np.float32).transpose(0, 2, 1).reshape(8 * D, 4 * D)),
        "wu_t": np.ascontiguousarray(np.asarray(inputs["L1_ffn_w_up"], np.float32).transpose(0, 2, 1).reshape(8 * D, 4 * D)),
        "wd_t": np.ascontiguousarray(np.asarray(inputs["L1_ffn_w_down"], np.float32).transpose(0, 2, 1).reshape(8 * 4 * D, D)),
    }
